# version 01
# baseline (speedup 1.0000x reference)
import numpy as np
from contextlib import ExitStack

import concourse.bass as bass
import concourse.mybir as mybir
from concourse.bass_utils import run_bass_kernel_spmd

F32 = mybir.dt.float32
BF16 = mybir.dt.bfloat16
AF = mybir.ActivationFunctionType
ALU = mybir.AluOpType
AX = mybir.AxisListType

T = 2048
D = 1024
G = 4
NCORES = 8



class Op:
    __slots__ = ("engine", "dma", "emit_fn", "deps", "needs_inc", "count_after",
                 "idx", "tag", "slot", "rotation")


class Prog:

    def __init__(self, nc):
        self.nc = nc
        self.ops = []
        self._lw = {}
        self._rd = {}

    def add(self, engine, emit_fn, reads=(), writes=(), deps=(), dma=False, tag=""):
        reads = [r if isinstance(r, (str, tuple)) else id(r) for r in reads]
        writes = [r if isinstance(r, (str, tuple)) else id(r) for r in writes]
        dd = set()
        for r in reads:
            w = self._lw.get(r)
            if w is not None:
                dd.add(w)
        for r in writes:
            w = self._lw.get(r)
            if w is not None:
                dd.add(w)
            dd.update(self._rd.get(r, ()))
        dd.update(deps)
        op = Op()
        op.engine = engine
        op.dma = dma
        op.emit_fn = emit_fn
        op.idx = len(self.ops)
        op.tag = tag
        op.deps = dd
        op.needs_inc = False
        op.count_after = None
        self.ops.append(op)
        for r in reads:
            self._rd.setdefault(r, set()).add(op)
        for r in writes:
            self._lw[r] = op
            self._rd[r] = set()
        return op

    def pe(self, emit_fn, **kw):
        return self.add("pe", emit_fn, **kw)

    def dve(self, emit_fn, **kw):
        return self.add("dve", emit_fn, **kw)

    def act(self, emit_fn, **kw):
        return self.add("act", emit_fn, **kw)

    def pool(self, emit_fn, **kw):
        return self.add("pool", emit_fn, **kw)

    def i_tt(self, aluop, out, a, b, eng="dve", **kw):
        return self.add(eng, lambda h, o=out, a=a, b=b, op=aluop:
                        h.tensor_tensor(o, a, b, op), **kw)

    def i_ts(self, out, a, s1, s2, op0, op1=ALU.add, eng="dve", **kw):
        if s2 is None:
            return self.add(eng, lambda h, o=out, a=a, s1=s1, op0=op0:
                            h.tensor_scalar(o, a, s1, None, op0), **kw)
        return self.add(eng, lambda h, o=out, a=a, s1=s1, s2=s2, op0=op0, op1=op1:
                        h.tensor_scalar(o, a, s1, s2, op0, op1), **kw)

    def i_stt(self, out, in0, scalar, in1, op0, op1, eng="dve", **kw):
        return self.add(eng, lambda h, o=out, i0=in0, s=scalar, i1=in1, op0=op0,
                        op1=op1: h.scalar_tensor_tensor(o, i0, s, i1, op0, op1), **kw)

    def i_act(self, func, out, in_, scale=1.0, bias=0.0, accum_out=None, **kw):
        return self.add("act", lambda h, f=func, o=out, i=in_, s=scale, b=bias,
                        ao=accum_out: h.activation(out=o, in_=i, func=f, bias=b,
                                                   scale=s, accum_out=ao), **kw)

    def i_copy(self, out, in_, eng="dve", **kw):
        if eng == "act":
            return self.add(eng, lambda h, o=out, i=in_: h.copy(out=o, in_=i), **kw)
        return self.add(eng, lambda h, o=out, i=in_: h.tensor_copy(o, i), **kw)

    def i_memset(self, ap, val, eng="dve", **kw):
        return self.add(eng, lambda h, a=ap, v=val: h.memset(a, v), **kw)

    def i_scan(self, out, d0, d1, init, op0=ALU.mult, op1=ALU.add, **kw):
        return self.add("dve", lambda h, o=out, d0=d0, d1=d1, ini=init, op0=op0,
                        op1=op1: h.tensor_tensor_scan(o, d0, d1, ini, op0, op1), **kw)

    def i_reduce(self, aluop, out, in_, negate=None, **kw):
        return self.add("dve", lambda h, o=out, i=in_, op=aluop, ng=negate:
                        h.tensor_reduce(o, i, AX.X, op, negate=ng), **kw)

    def dma_sp(self, out_ap, in_ap, **kw):
        return self.add("sp", lambda h: h.dma_start(out=out_ap, in_=in_ap),
                        dma=True, **kw)

    def dma_pool(self, out_ap, in_ap, **kw):
        return self.add("pool", lambda h: h.dma_start(out=out_ap, in_=in_ap),
                        dma=True, **kw)

    def dma_act(self, out_ap, in_ap, **kw):
        return self.add("act", lambda h: h.dma_start(out=out_ap, in_=in_ap),
                        dma=True, **kw)

    def all_ops(self):
        return list(self.ops)

    NDMASLOTS = 8

    def _dom(self, op):
        if not op.dma:
            return op.engine
        return f"dma_{op.engine}_{op.slot}"

    def emit(self, strict=True):
        nc = self.nc
        qidx = {}
        for op in self.ops:
            if op.dma:
                q = op.engine
                i = qidx.get(q, 0)
                qidx[q] = i + 1
                op.slot = i % self.NDMASLOTS
                op.rotation = i // self.NDMASLOTS
        for op in self.ops:
            dom_op = self._dom(op)
            for d in op.deps:
                if strict or self._dom(d) != dom_op or d.dma:
                    d.needs_inc = True
        for op in self.ops:
            if op.dma:
                op.needs_inc = True
        counters = {}
        for op in self.ops:
            if op.needs_inc:
                dom = self._dom(op)
                counters[dom] = counters.get(dom, 0) + (16 if op.dma else 1)
                op.count_after = counters[dom]
        doms = sorted(counters)
        es = ExitStack()
        with es:
            sems = {d: es.enter_context(nc.semaphore(f"s_{d}")) for d in doms}
            block = es.enter_context(nc.Block())

            def make_body(ename):
                myops = [o for o in self.ops if o.engine == ename]

                def body(h):
                    waited = {}
                    for op in myops:
                        dom_op = self._dom(op)
                        if op.dma and op.rotation > 0:
                            prev = 16 * op.rotation
                            if waited.get(dom_op, 0) < prev:
                                h.wait_ge(sems[dom_op], prev)
                                waited[dom_op] = prev
                        for dep in sorted(op.deps, key=lambda o: o.idx):
                            ddom = self._dom(dep)
                            if ddom == dom_op and not dep.dma and not strict:
                                continue
                            if dep.count_after is None:
                                continue
                            if waited.get(ddom, 0) >= dep.count_after:
                                continue
                            h.wait_ge(sems[ddom], dep.count_after)
                            waited[ddom] = dep.count_after
                        inst = op.emit_fn(h)
                        if op.needs_inc:
                            assert inst is not None, f"op {op.tag} needs inc but emitted none"
                            inst.then_inc(sems[dom_op], 16 if op.dma else 1)
                return body

            for ename, dec in (("pe", block.tensor), ("dve", block.vector),
                               ("act", block.scalar), ("pool", block.gpsimd),
                               ("sp", block.sync)):
                if any(o.engine == ename for o in self.ops):
                    dec(make_body(ename))




class Ctx:
    def __init__(self, nc, es):
        self.nc = nc
        self.es = es
        self.P = Prog(nc)
        self._const_ones_row = None
        self._uid = 0
        self.debug = False

    def tap(self, name, ap, reads):
        if not self.debug:
            return
        dr = self.nc.dram_tensor("tap_" + name, list(ap.shape), ap.dtype,
                                 kind="ExternalOutput").ap()
        self.P.dma_sp(dr, ap, reads=tuple(reads), tag="tap_" + name)

    def uid(self):
        self._uid += 1
        return self._uid

    def sbuf(self, name, shape, dtype):
        return self.es.enter_context(self.nc.sbuf_tensor(f"{name}_{self.uid()}", shape, dtype))

    def psum(self, name, shape, dtype=F32):
        return self.es.enter_context(self.nc.psum_tensor(f"{name}_{self.uid()}", shape, dtype))

    def tiles(self, name, n, cols, dtype, guard=0):
        return [self.sbuf(f"{name}{i}", [128, guard + cols], dtype) for i in range(n)]

    def ones_row(self):
        if self._const_ones_row is None:
            t = self.sbuf("ones_row", [1, 128], F32)
            self.P.dve(lambda h: h.memset(t[:], 1.0), writes=(t,), tag="ones_row")
            self._const_ones_row = t
        return self._const_ones_row

    def ones_col(self):
        if not hasattr(self, "_ones_col") or self._ones_col is None:
            t = self.sbuf("ones_col", [128, 1], F32)
            self.P.dve(lambda h: h.memset(t[:], 1.0), writes=(t,), tag="ones_col")
            self._ones_col = t
        return self._ones_col


def mm_group(P, ps_ap, ps_res, pairs, reads, tag="", accumulate=False, close=True):
    def emit(h):
        inst = None
        n = len(pairs)
        for i, (l, r) in enumerate(pairs):
            inst = h.matmul(ps_ap, l, r, start=(i == 0 and not accumulate),
                            stop=(i == n - 1 and close))
        return inst
    return P.add("pe", emit, reads=tuple(reads), writes=(ps_res,), tag=tag)


def ln_stats(C, src_tiles, src_c0, F, rowbuf, ps_pool, eps=1e-5, tag="ln",
             sq_scratch=None, n_feat=D):
    P, nc = C.P, C.nc
    nct = len(src_tiles)
    ones_col = C.ones_col()
    ones_row = C.ones_row()
    sxT, sxc = ps_pool["sx"]
    sqT, sqc = ps_pool["sq"]
    nmbT, nmbc = ps_pool["nmb"]
    rsbT, rsbc = ps_pool["rsb"]
    del ps_pool
    sx_ap = sxT[0:1, sxc:sxc + F]
    sq_ap = sqT[0:1, sqc:sqc + F]
    pairs = [(ones_col[:, 0:1], t[:, src_c0:src_c0 + F]) for t in src_tiles]
    mm_group(P, sx_ap, sxT, pairs, reads=tuple(src_tiles) + (ones_col,),
             tag=f"{tag}_sx")
    for i, t in enumerate(src_tiles):
        s = sq_scratch[i % len(sq_scratch)]
        P.i_act(AF.Square, s[:, 0:F], t[:, src_c0:src_c0 + F],
                reads=(t,), writes=(s,), tag=f"{tag}_sq{i}")
        mm_group(P, sq_ap, sqT, [(ones_col[:, 0:1], s[:, 0:F])],
                 reads=(s, ones_col), tag=f"{tag}_sqm{i}", accumulate=(i > 0),
                 close=(i == len(src_tiles) - 1))
    r0 = rowbuf[0:1, 0:F]
    r1 = rowbuf[0:1, F:2 * F]
    r2 = rowbuf[0:1, 2 * F:3 * F]
    inv = 1.0 / n_feat
    P.i_ts(r0, sx_ap, -inv, None, ALU.mult,
           reads=(sxT,), writes=(rowbuf,), tag=f"{tag}_nm")
    P.i_ts(r1, sq_ap, inv, None, ALU.mult,
           reads=(sqT, rowbuf), writes=(rowbuf,), tag=f"{tag}_ex2")
    P.i_stt(r2, r0, 1.0, r0, ALU.mult, ALU.mult,
            reads=(rowbuf,), writes=(rowbuf,), tag=f"{tag}_msq")
    P.i_stt(r2, r2, -1.0, r1, ALU.mult, ALU.add,
            reads=(rowbuf,), writes=(rowbuf,), tag=f"{tag}_var")
    P.i_ts(r2, r2, 1.0, eps, ALU.mult, ALU.add,
           reads=(rowbuf,), writes=(rowbuf,), tag=f"{tag}_vare")
    P.i_act(AF.Ln, r2, r2, reads=(rowbuf,), writes=(rowbuf,), tag=f"{tag}_ln")
    P.i_act(AF.Exp, r2, r2, scale=-0.5,
            reads=(rowbuf,), writes=(rowbuf,), tag=f"{tag}_exp")
    mm_group(P, nmbT[:, nmbc:nmbc + F], nmbT, [(ones_row[0:1, :], r0)],
             reads=(rowbuf, ones_row), tag=f"{tag}_nmb")
    mm_group(P, rsbT[:, rsbc:rsbc + F], rsbT, [(ones_row[0:1, :], r2)],
             reads=(rowbuf, ones_row), tag=f"{tag}_rsb")
    return nmbT[:, nmbc:nmbc + F], rsbT[:, rsbc:rsbc + F], nmbT, rsbT


def make_ln_pool(C, F=512):
    return {
        "sx": (C.psum("ln_sx", [1, F]), 0),
        "sq": (C.psum("ln_sq", [1, F]), 0),
        "nmb": (C.psum("ln_nmb", [128, F]), 0),
        "rsb": (C.psum("ln_rsb", [128, F]), 0),
    }




def build_l1_ffn(C, xres, w_sb, mu_sb, CH=512):
    P = C.P
    nct = 8
    om_sb = C.sbuf("l1f_om", [128, 16], F32)
    P.i_ts(om_sb[:], mu_sb[:], -1.0, 1.0, ALU.mult, ALU.add,
           reads=(mu_sb,), writes=(om_sb,), tag="l1f_om")

    ln = C.tiles("l1f_ln", nct, T, BF16, guard=G)
    for t in ln:
        P.i_memset(t[:, 0:G], 0.0, writes=(t,), tag="l1f_lng")
    lnpool = make_ln_pool(C, CH)
    rowbuf = C.sbuf("l1f_rows", [1, 3 * CH], F32)
    sq_scr = [C.sbuf(f"l1f_sqs{i}", [128, CH], F32) for i in range(2)]
    for c0 in range(0, T, CH):
        nmb, rsb, nmb_res, rsb_res = ln_stats(C, xres, G + c0, CH, rowbuf, lnpool,
                                              tag=f"l1f_ln{c0}", sq_scratch=sq_scr)
        for i in range(nct):
            tsc = sq_scr[i % 2]
            P.i_tt(ALU.add, tsc[:, 0:CH], xres[i][:, G + c0:G + c0 + CH], nmb,
                   reads=(xres[i], nmb_res), writes=(tsc,), tag=f"l1f_lna{c0}_{i}")
            P.i_tt(ALU.mult, ln[i][:, G + c0:G + c0 + CH], tsc[:, 0:CH], rsb,
                   reads=(tsc, rsb_res), writes=(ln[i],), tag=f"l1f_lnb{c0}_{i}")

    xrk = [C.sbuf(f"l1f_xrk{i}", [128, CH], BF16) for i in range(nct)]
    rr = [C.sbuf(f"l1f_r{i}", [128, CH], BF16) for i in range(nct)]
    kk = [C.sbuf(f"l1f_k{i}", [128, CH], BF16) for i in range(nct)]
    outc = [C.sbuf(f"l1f_out{i}", [128, CH], F32) for i in range(nct)]
    mmps = [C.psum(f"l1f_ps{i}", [128, CH]) for i in range(3)]
    for ci, c0 in enumerate(range(0, T, CH)):
        for half, wkey in ((0, "r"), (8, "k")):
            for i in range(nct):
                tsc = sq_scr[i % 2]
                P.i_ts(tsc[:, 0:CH], ln[i][:, G + c0:G + c0 + CH],
                       mu_sb[:, half + i:half + i + 1], None, ALU.mult,
                       reads=(ln[i], mu_sb), writes=(tsc,), tag=f"l1f_xa{wkey}{c0}_{i}")
                P.i_stt(xrk[i][:], ln[i][:, G + c0 - 1:G + c0 - 1 + CH],
                        om_sb[:, half + i:half + i + 1], tsc[:, 0:CH],
                        ALU.mult, ALU.add,
                        reads=(ln[i], om_sb, tsc), writes=(xrk[i],),
                        tag=f"l1f_xb{wkey}{c0}_{i}")
            for ot in range(nct):
                ps = mmps[ot % 3]
                pairs = [(w_sb[wkey][:, ct * 1024 + ot * 128: ct * 1024 + ot * 128 + 128],
                          xrk[ct][:]) for ct in range(nct)]
                mm_group(P, ps[:], ps, pairs, reads=tuple(xrk) + (w_sb[wkey],),
                         tag=f"l1f_{wkey}mm{c0}_{ot}")
                if wkey == "r":
                    P.i_act(AF.Sigmoid, rr[ot][:], ps[:],
                            reads=(ps,), writes=(rr[ot],), tag=f"l1f_rsig{c0}_{ot}")
                else:
                    P.i_ts(kk[ot][:], ps[:], 0.0, None, ALU.max,
                           reads=(ps,), writes=(kk[ot],), tag=f"l1f_krelu{c0}_{ot}")
                    P.i_stt(kk[ot][:], kk[ot][:], 1.0, ps[:], ALU.mult, ALU.mult,
                            reads=(ps, kk[ot]), writes=(kk[ot],), tag=f"l1f_ksq{c0}_{ot}")
        for ot in range(nct):
            ps = mmps[ot % 3]
            pairs = [(w_sb["v"][:, ct * 1024 + ot * 128: ct * 1024 + ot * 128 + 128],
                      kk[ct][:]) for ct in range(nct)]
            mm_group(P, ps[:], ps, pairs, reads=tuple(kk) + (w_sb["v"],),
                     tag=f"l1f_vmm{c0}_{ot}")
            P.i_tt(ALU.mult, outc[ot][:], rr[ot][:], ps[:],
                   reads=(ps, rr[ot]), writes=(outc[ot],), tag=f"l1f_gate{c0}_{ot}")
        nmb, rsb, nmb_res, rsb_res = ln_stats(C, outc, 0, CH, rowbuf, lnpool,
                                              tag=f"l1f_gl{c0}", sq_scratch=sq_scr)
        for i in range(nct):
            tsc = sq_scr[i % 2]
            P.i_tt(ALU.add, tsc[:, 0:CH], outc[i][:], nmb,
                   reads=(outc[i], nmb_res), writes=(tsc,), tag=f"l1f_ga{c0}_{i}")
            P.i_tt(ALU.mult, tsc[:, 0:CH], tsc[:, 0:CH], rsb,
                   reads=(tsc, rsb_res), writes=(tsc,), tag=f"l1f_gb{c0}_{i}")
            P.i_tt(ALU.add, xres[i][:, G + c0:G + c0 + CH],
                   xres[i][:, G + c0:G + c0 + CH], tsc[:, 0:CH],
                   reads=(tsc, xres[i]), writes=(xres[i],), tag=f"l1f_gc{c0}_{i}")


def load_l1_ffn_weights(C, names=("l1f_wrT", "l1f_wkT", "l1f_wvT")):
    nc, P = C.nc, C.P
    w_sb = {}
    for key, nm in zip(("r", "k", "v"), names):
        dr = nc.dram_tensor(nm, [1024, 1024], F32, kind="ExternalInput").ap()
        sb = C.sbuf(nm + "_sb", [128, 8 * 1024], BF16)
        src = dr.rearrange("(ct p) d -> p ct d", p=128)
        dst = sb[:].rearrange("p (ct d) -> p ct d", d=1024)
        P.dma_pool(dst, src, writes=(sb,), tag=f"load_{nm}")
        w_sb[key] = sb
    mu_dr = nc.dram_tensor("l1f_mu", [128, 16], F32, kind="ExternalInput").ap()
    mu_sb = C.sbuf("l1f_mu_sb", [128, 16], F32)
    P.dma_sp(mu_sb[:], mu_dr[:], writes=(mu_sb,), tag="load_l1f_mu")
    return w_sb, mu_sb






L1A_TC = 256
L1A_SUB = 16
L1A_CLAMP = 80.0


def l1a_consts():
    SUB, NEAR = 16, 4
    t1 = np.arange(128)[:, None]
    t2 = np.arange(128)[None, :]
    same_sub = (t1 // SUB) == (t2 // SUB)
    amask = (same_sub & ((t2 - t1) >= (NEAR + 1))).astype(np.float32)
    cm = np.zeros((128, 80), np.float32)
    for d in range(0, 5):
        for h in range(16):
            cm[:, d * 16 + h] = ((np.arange(128) % SUB) >= d).astype(np.float32)
    return amask, cm


def build_l1_attn(C, xres_dram_in, xres_dram_out):
    P, nc = C.P, C.nc
    TC, SUB = L1A_TC, L1A_SUB
    NCH = T // TC
    NBLK = TC // 128
    NSUB = TC // SUB
    nct = 8

    wTs = {}
    for nm in ("r", "k", "v", "g", "o"):
        wTs[nm] = nc.dram_tensor(f"l1a_{nm}WT", [1024, 1024], F32, kind="ExternalInput").ap()
    la_dr = nc.dram_tensor("l1a_laT", [1024, 320], F32, kind="ExternalInput").ap()
    lb_dr = nc.dram_tensor("l1a_lbT", [64, 5120], F32, kind="ExternalInput").ap()
    ll_dr = nc.dram_tensor("l1a_ll", [128, 40], F32, kind="ExternalInput").ap()
    xw_dr = nc.dram_tensor("l1a_xw", [128, 8], F32, kind="ExternalInput").ap()
    u_dr = nc.dram_tensor("l1a_u", [128, 8], F32, kind="ExternalInput").ap()
    st0_dr = nc.dram_tensor("l1a_st0", [8, 128, 64], F32, kind="ExternalInput").ap()
    lnwb_dr = nc.dram_tensor("l1a_lnwb", [2, 1024], F32, kind="ExternalInput").ap()
    amask_dr = nc.dram_tensor("l1a_amask", [128, 128], F32, kind="ExternalInput").ap()
    ident_dr = nc.dram_tensor("l1a_ident", [128, 128], F32, kind="ExternalInput").ap()
    cmask_dr = nc.dram_tensor("l1a_cmask", [128, 80], F32, kind="ExternalInput").ap()

    la_sb = C.sbuf("l1a_la", [128, 8 * 320], BF16)
    P.dma_pool(la_sb[:].rearrange("p (c n) -> p c n", n=320),
               la_dr.rearrange("(c p) n -> p c n", p=128), writes=(la_sb,), tag="la")
    lb_sb = C.sbuf("l1a_lb", [64, 5120], BF16)
    P.dma_pool(lb_sb[:], lb_dr[:], writes=(lb_sb,), tag="lb")
    ll_sb = C.sbuf("l1a_ll", [128, 40], F32)
    P.dma_sp(ll_sb[:], ll_dr[:], writes=(ll_sb,), tag="ll")
    xw_sb = C.sbuf("l1a_xw", [128, 8], F32)
    P.dma_sp(xw_sb[:], xw_dr[:], writes=(xw_sb,), tag="xw")
    u_sb = C.sbuf("l1a_u", [128, 8], F32)
    P.dma_sp(u_sb[:], u_dr[:], writes=(u_sb,), tag="u")
    amask_sb = C.sbuf("l1a_am", [128, 128], F32)
    P.dma_sp(amask_sb[:], amask_dr[:], writes=(amask_sb,), tag="am")
    cmask_sb = C.sbuf("l1a_cm", [128, 80], F32)
    P.dma_sp(cmask_sb[:], cmask_dr[:], writes=(cmask_sb,), tag="cm")
    lnw_rep = C.sbuf("l1a_lnw", [128, 1024], F32)
    P.dma_sp(lnw_rep[:], lnwb_dr[0].unsqueeze(0).partition_broadcast(128),
             writes=(lnw_rep,), tag="lnw")
    lnb_rep = C.sbuf("l1a_lnb", [128, 1024], F32)
    P.dma_sp(lnb_rep[:], lnwb_dr[1].unsqueeze(0).partition_broadcast(128),
             writes=(lnb_rep,), tag="lnb")
    st = [C.sbuf(f"l1a_st{i}", [128, 64], F32) for i in range(8)]
    for hp in range(8):
        P.dma_sp(st[hp][:], st0_dr[hp], writes=(st[hp],), tag=f"st0_{hp}")
    ones_bf = C.sbuf("l1a_ones", [128, TC], BF16)
    P.i_memset(ones_bf[:], 1.0, writes=(ones_bf,), tag="onesbf")
    ident_sb = C.sbuf("l1a_id", [128, 128], BF16)
    P.dma_pool(ident_sb[:], ident_dr[:], writes=(ident_sb,), tag="ident")

    rc = C.tiles("l1a_rc", nct, TC, F32)
    xl = C.tiles("l1a_xl", nct, TC, BF16, guard=1)
    dx = C.tiles("l1a_dx", nct, TC, BF16)
    lerp = C.tiles("l1a_lerp", nct, TC, BF16)
    zz = C.tiles("l1a_z", nct, TC, BF16)
    ww = C.tiles("l1a_ww", nct, TC, BF16)
    r_fm = C.tiles("l1a_r", nct, TC, BF16)
    k_fm = C.tiles("l1a_k", nct, TC, BF16, guard=4)
    rate = C.tiles("l1a_rate", nct, TC, F32)
    Rg = C.tiles("l1a_Rg", nct, TC + 1, F32)
    wf = C.tiles("l1a_wf", nct, TC, BF16, guard=4)
    rq = C.tiles("l1a_rq", nct, TC, F32)
    kA = C.tiles("l1a_kA", nct, TC, F32)
    k2 = C.tiles("l1a_k2", nct, TC, BF16)
    ruk = C.tiles("l1a_ruk", nct, TC, BF16)
    Vt = [C.sbuf(f"l1a_Vt{i}", [128, 1024], BF16) for i in range(NBLK)]
    Gt = [C.sbuf(f"l1a_Gt{i}", [128, 1024], BF16) for i in range(NBLK)]
    k2t = [C.sbuf(f"l1a_k2t{i}", [128, 1024], BF16) for i in range(NBLK)]
    Vsh = [C.sbuf(f"l1a_Vsh{i}", [128, 1024], BF16) for i in range(NBLK)]
    k2s = [C.sbuf(f"l1a_k2s{i}", [16, 1024], BF16) for i in range(2)]
    Vts = [C.sbuf(f"l1a_Vts{i}", [16, 1024], BF16) for i in range(2)]
    ystage = [C.sbuf(f"l1a_yst{i}", [16, 1024], F32) for i in range(2)]
    Vtail = C.sbuf("l1a_vtail", [4, 1024], BF16)
    y_tm = [C.sbuf(f"l1a_ytm{i}", [128, 1024], F32) for i in range(NBLK)]
    ysc = [C.sbuf(f"l1a_ysc{i}", [128, 1024], F32) for i in range(2)]
    o_t = [C.sbuf(f"l1a_ot{i}", [128, 1024], BF16) for i in range(NBLK)]
    o_fm = C.tiles("l1a_ofm", nct, TC, BF16)
    outc = C.tiles("l1a_outc", nct, TC, F32)
    th = [C.sbuf(f"l1a_th{i}", [64, TC], BF16) for i in range(2)]
    scr = [C.sbuf(f"l1a_scr{i}", [128, TC], F32) for i in range(2)]
    scrb = [C.sbuf(f"l1a_scrb{i}", [128, TC], BF16) for i in range(2)]
    Eends = [C.sbuf(f"l1a_Ee{i}", [128, NSUB], F32) for i in range(8)]
    c_sb = [C.sbuf(f"l1a_c{i}", [128, 80], F32) for i in range(2)]
    gsum = C.sbuf("l1a_gsum", [128, 16], F32)
    gsq = C.sbuf("l1a_gsq", [128, 16], F32)
    gstat = C.sbuf("l1a_gstat", [128, 16], F32)
    ysq = C.sbuf("l1a_ysq", [128, 1024], F32)
    wbuf = C.sbuf("l1a_wbuf", [128, 8192], BF16)
    rowbuf = C.sbuf("l1a_rows", [1, 3 * TC], F32)
    zcol = C.sbuf("l1a_zc", [128, 1], F32)
    P.i_memset(zcol[:], 0.0, writes=(zcol,), tag="zcol")
    ztile = C.sbuf("l1a_zt", [128, 1024], BF16)
    P.i_memset(ztile[:], 0.0, writes=(ztile,), tag="ztile")

    ps = [C.psum(f"l1a_ps{i}", [128, 512]) for i in range(5)]
    pst = C.psum("l1a_pst", [128, 1024], BF16)
    ps16 = C.psum("l1a_ps16", [16, 1024])
    lnpool = {"sx": (ps16, 0), "sq": (ps16, 512), "nmb": (ps[3], 0), "rsb": (ps[4], 0)}

    for i in range(nct):
        P.i_memset(xl[i][:, 0:1], 0.0, writes=(xl[i],), tag=f"xlg{i}")
        P.i_memset(k_fm[i][:, 0:4], 0.0, writes=(k_fm[i],), tag=f"kg{i}")
        P.i_memset(wf[i][:, 0:4], 0.0, writes=(wf[i],), tag=f"wfg{i}")
    P.i_memset(Vtail[:], 0.0, writes=(Vtail,), tag="vtail0")

    def load_w(nm, tag):
        src = wTs[nm].rearrange("(ct p) d -> p ct d", p=128)
        dst = wbuf[:].rearrange("p (ct d) -> p ct d", d=1024)
        return P.dma_pool(dst, src, writes=(wbuf,), tag=tag)

    def stats_pool_ln(F):
        return {"sx": ps16, "sq": ps16, "nmb": ps[3], "rsb": ps[4]}

    for ch in range(NCH):
        c0 = ch * TC
        if ch > 0:
            for i in range(nct):
                P.i_copy(xl[i][:, 0:1], xl[i][:, TC:TC + 1], eng="dve",
                         reads=(xl[i],), writes=(xl[i],), tag=f"xcar{ch}_{i}")
        for i in range(nct):
            P.dma_sp(rc[i][:], xres_dram_in[i * 128:(i + 1) * 128, c0:c0 + TC],
                     writes=(rc[i],), tag=f"rcl{ch}_{i}")
        nmb, rsb, nmb_res, rsb_res = ln_stats(C, rc, 0, TC, rowbuf, lnpool,
                                              tag=f"l1aln{ch}", sq_scratch=scr)
        for i in range(nct):
            P.i_tt(ALU.add, scr[i % 2][:, 0:TC], rc[i][:], nmb,
                   reads=(rc[i], nmb_res), writes=(scr[i % 2],), tag=f"lna{ch}_{i}")
            P.i_tt(ALU.mult, xl[i][:, 1:TC + 1], scr[i % 2][:, 0:TC], rsb,
                   reads=(scr[i % 2], rsb_res), writes=(xl[i],), tag=f"lnb{ch}_{i}")
        for i in range(nct):
            P.i_tt(ALU.subtract, dx[i][:], xl[i][:, 0:TC], xl[i][:, 1:TC + 1],
                   reads=(xl[i],), writes=(dx[i],), tag=f"dx{ch}_{i}")
            P.i_stt(lerp[i][:], dx[i][:], xw_sb[:, i:i + 1], xl[i][:, 1:TC + 1],
                    ALU.mult, ALU.add, reads=(dx[i], xw_sb, xl[i]),
                    writes=(lerp[i],), tag=f"lerp{ch}_{i}")

        def lora(pidx, src_tiles, dst_tiles, tagp, rate_out=False):
            pairs = [(la_sb[:, ct * 320 + pidx * 64: ct * 320 + pidx * 64 + 64],
                      src_tiles[ct][:]) for ct in range(nct)]
            mm_group(P, ps[4][0:64, 0:TC], ps[4], pairs,
                     reads=tuple(src_tiles) + (la_sb,), tag=f"{tagp}m1")
            tb = th[pidx % 2]
            P.i_act(AF.Tanh, tb[:, 0:TC], ps[4][0:64, 0:TC],
                    reads=(ps[4],), writes=(tb,), tag=f"{tagp}tanh")
            for ot in range(nct):
                p2 = ps[ot % 2]
                mm_group(P, p2[:, 0:TC], p2,
                         [(lb_sb[0:64, pidx * 1024 + ot * 128: pidx * 1024 + ot * 128 + 128],
                           tb[:, 0:TC])],
                         reads=(tb, lb_sb), tag=f"{tagp}m2_{ot}")
                if rate_out:
                    P.i_act(AF.Exp, dst_tiles[ot][:], p2[:, 0:TC],
                            bias=ll_sb[:, pidx * 8 + ot: pidx * 8 + ot + 1],
                            reads=(p2, ll_sb), writes=(dst_tiles[ot],),
                            tag=f"{tagp}rate{ot}")
                else:
                    P.i_stt(scr[ot % 2][:, 0:TC], p2[:, 0:TC],
                            ll_sb[:, pidx * 8 + ot: pidx * 8 + ot + 1], dx[ot][:],
                            ALU.add, ALU.mult,
                            reads=(p2, ll_sb, dx[ot]), writes=(scr[ot % 2],),
                            tag=f"{tagp}sc{ot}")
                    P.i_tt(ALU.add, dst_tiles[ot][:], xl[ot][:, 1:TC + 1],
                           scr[ot % 2][:, 0:TC],
                           reads=(xl[ot], scr[ot % 2]), writes=(dst_tiles[ot],),
                           tag=f"{tagp}z{ot}")

        lora(0, lerp, zz, f"lr{ch}_")
        ldw = load_w("r", f"wr{ch}")
        for ot in range(nct):
            p2 = ps[ot % 2]
            pairs = [(wbuf[:, ct * 1024 + ot * 128: ct * 1024 + ot * 128 + 128],
                      zz[ct][:]) for ct in range(nct)]
            mm_group(P, p2[:, 0:TC], p2, pairs, reads=tuple(zz) + (wbuf,),
                     tag=f"rmm{ch}_{ot}")
            P.i_copy(r_fm[ot][:], p2[:, 0:TC], eng="dve",
                     reads=(p2,), writes=(r_fm[ot],), tag=f"rev{ch}_{ot}")
        lora(1, lerp, zz, f"lk{ch}_")
        load_w("k", f"wk{ch}")
        for ot in range(nct):
            p2 = ps[ot % 2]
            pairs = [(wbuf[:, ct * 1024 + ot * 128: ct * 1024 + ot * 128 + 128],
                      zz[ct][:]) for ct in range(nct)]
            mm_group(P, p2[:, 0:TC], p2, pairs, reads=tuple(zz) + (wbuf,),
                     tag=f"kmm{ch}_{ot}")
            P.i_copy(k_fm[ot][:, 4:4 + TC], p2[:, 0:TC], eng="dve",
                     reads=(p2,), writes=(k_fm[ot],), tag=f"kev{ch}_{ot}")
        lora(2, lerp, zz, f"lv{ch}_")
        load_w("v", f"wv{ch}")
        for bt in range(NBLK):
            for jh in range(2):
                p2 = ps[(bt * 2 + jh) % 2]
                pairs = [(zz[ct][:, bt * 128:(bt + 1) * 128],
                          wbuf[:, ct * 1024 + jh * 512: ct * 1024 + jh * 512 + 512])
                         for ct in range(nct)]
                mm_group(P, p2[:, 0:512], p2, pairs, reads=tuple(zz) + (wbuf,),
                         tag=f"vmm{ch}_{bt}_{jh}")
                P.i_copy(Vt[bt][:, jh * 512:(jh + 1) * 512], p2[:, 0:512], eng="dve",
                         reads=(p2,), writes=(Vt[bt],), tag=f"vev{ch}_{bt}_{jh}")
        lora(3, lerp, zz, f"lg{ch}_")
        load_w("g", f"wg{ch}")
        for bt in range(NBLK):
            for jh in range(2):
                p2 = ps[(bt * 2 + jh) % 2]
                pairs = [(zz[ct][:, bt * 128:(bt + 1) * 128],
                          wbuf[:, ct * 1024 + jh * 512: ct * 1024 + jh * 512 + 512])
                         for ct in range(nct)]
                mm_group(P, p2[:, 0:512], p2, pairs, reads=tuple(zz) + (wbuf,),
                         tag=f"gmm{ch}_{bt}_{jh}")
                P.i_copy(Gt[bt][:, jh * 512:(jh + 1) * 512], p2[:, 0:512], eng="dve",
                         reads=(p2,), writes=(Gt[bt],), tag=f"gev{ch}_{bt}_{jh}")
        lora(4, lerp, ww, f"ld1{ch}_")
        lora(4, ww, rate, f"ld2{ch}_", rate_out=True)

        for i in range(nct):
            P.i_memset(Rg[i][:, 0:1], 0.0, writes=(Rg[i],), tag=f"rg0{ch}_{i}")
            P.i_scan(Rg[i][:, 1:TC + 1], ones_bf[:, 0:TC], rate[i][:], 0.0,
                     reads=(ones_bf, rate[i]), writes=(Rg[i],), tag=f"rgs{ch}_{i}")
            P.i_act(AF.Exp, wf[i][:, 4:4 + TC], rate[i][:], scale=-1.0,
                    reads=(rate[i],), writes=(wf[i],), tag=f"wf{ch}_{i}")
            Bv = Rg[i][:, 0:TC:SUB].unsqueeze(2).broadcast_to([128, NSUB, SUB])
            Ev = Rg[i][:, SUB:TC + 1:SUB].unsqueeze(2).broadcast_to([128, NSUB, SUB])
            s3 = scr[i % 2][:, 0:TC].rearrange("p (s c) -> p s c", c=SUB)
            P.i_tt(ALU.subtract, s3, Bv,
                   Rg[i][:, 0:TC].rearrange("p (s c) -> p s c", c=SUB),
                   reads=(Rg[i],), writes=(scr[i % 2],), tag=f"rqa{ch}_{i}")
            P.i_act(AF.Exp, scr[i % 2][:, 0:TC], scr[i % 2][:, 0:TC],
                    reads=(scr[i % 2],), writes=(scr[i % 2],), tag=f"rqe{ch}_{i}")
            P.i_tt(ALU.mult, rq[i][:], r_fm[i][:], scr[i % 2][:, 0:TC],
                   reads=(r_fm[i], scr[i % 2]), writes=(rq[i],), tag=f"rqm{ch}_{i}")
            P.i_tt(ALU.subtract, s3, Rg[i][:, 1:TC + 1].rearrange("p (s c) -> p s c", c=SUB),
                   Bv, reads=(Rg[i],), writes=(scr[i % 2],), tag=f"kaa{ch}_{i}")
            P.i_ts(scr[i % 2][:, 0:TC], scr[i % 2][:, 0:TC], L1A_CLAMP, None, ALU.min,
                   reads=(scr[i % 2],), writes=(scr[i % 2],), tag=f"kac{ch}_{i}")
            P.i_act(AF.Exp, scr[i % 2][:, 0:TC], scr[i % 2][:, 0:TC],
                    reads=(scr[i % 2],), writes=(scr[i % 2],), tag=f"kae{ch}_{i}")
            P.i_tt(ALU.mult, kA[i][:], k_fm[i][:, 4:4 + TC], scr[i % 2][:, 0:TC],
                   reads=(k_fm[i], scr[i % 2]), writes=(kA[i],), tag=f"kam{ch}_{i}")
            P.i_tt(ALU.subtract, s3, Rg[i][:, 1:TC + 1].rearrange("p (s c) -> p s c", c=SUB),
                   Ev, reads=(Rg[i],), writes=(scr[i % 2],), tag=f"k2a{ch}_{i}")
            P.i_act(AF.Exp, scr[i % 2][:, 0:TC], scr[i % 2][:, 0:TC],
                    reads=(scr[i % 2],), writes=(scr[i % 2],), tag=f"k2e{ch}_{i}")
            P.i_tt(ALU.mult, k2[i][:], k_fm[i][:, 4:4 + TC], scr[i % 2][:, 0:TC],
                   reads=(k_fm[i], scr[i % 2]), writes=(k2[i],), tag=f"k2m{ch}_{i}")
            P.i_tt(ALU.subtract, Eends[i][:, 0:NSUB], Rg[i][:, 0:TC:SUB],
                   Rg[i][:, SUB:TC + 1:SUB], reads=(Rg[i],), writes=(Eends[i],),
                   tag=f"ee{ch}_{i}")
            P.i_act(AF.Exp, Eends[i][:, 0:NSUB], Eends[i][:, 0:NSUB],
                    reads=(Eends[i],), writes=(Eends[i],), tag=f"eee{ch}_{i}")
            P.i_tt(ALU.mult, scrb[i % 2][:], r_fm[i][:], k_fm[i][:, 4:4 + TC],
                   reads=(r_fm[i], k_fm[i]), writes=(scrb[i % 2],), tag=f"ruk{ch}_{i}")
            P.i_ts(ruk[i][:], scrb[i % 2][:], u_sb[:, i:i + 1], None, ALU.mult,
                   reads=(scrb[i % 2], u_sb), writes=(ruk[i],), tag=f"ruku{ch}_{i}")

        for bt in range(NBLK):
            for it in range(nct):
                sl = pst[:, (it % 4) * 128:(it % 4) * 128 + 128]
                P.add("pe", lambda h, o=sl, i=k2[it][:, bt * 128:(bt + 1) * 128],
                      idn=ident_sb: h.transpose(o, i, idn[:]),
                      reads=(k2[it], ident_sb), writes=(pst,), tag=f"tr{ch}_{bt}_{it}")
                P.i_copy(k2t[bt][:, it * 128:(it + 1) * 128], sl, eng="dve",
                         reads=(pst,), writes=(k2t[bt],), tag=f"trev{ch}_{bt}_{it}")


        for d in range(0, 5):
            for hp in range(nct):
                if d == 0:
                    Pd = ruk[hp]
                    Pres = (ruk[hp],)
                else:
                    sb2 = scrb[0]
                    if d == 1:
                        P.i_tt(ALU.mult, sb2[:, 0:TC], r_fm[hp][:],
                               k_fm[hp][:, 4 - d:4 - d + TC],
                               reads=(r_fm[hp], k_fm[hp]), writes=(sb2,),
                               tag=f"P1_{ch}_{hp}")
                    else:
                        P.i_tt(ALU.mult, sb2[:, 0:TC], wf[hp][:, 3:3 + TC],
                               wf[hp][:, 2:2 + TC],
                               reads=(wf[hp],), writes=(sb2,), tag=f"Ga{d}_{ch}_{hp}")
                        if d == 4:
                            P.i_tt(ALU.mult, sb2[:, 0:TC], sb2[:, 0:TC],
                                   wf[hp][:, 1:1 + TC],
                                   reads=(sb2, wf[hp]), writes=(sb2,),
                                   tag=f"Gb{d}_{ch}_{hp}")
                        if d == 2:
                            P.i_copy(sb2[:, 0:TC], wf[hp][:, 3:3 + TC], eng="dve",
                                     reads=(wf[hp],), writes=(sb2,),
                                     tag=f"Gc{d}_{ch}_{hp}")
                        P.i_tt(ALU.mult, sb2[:, 0:TC], sb2[:, 0:TC], r_fm[hp][:],
                               reads=(sb2, r_fm[hp]), writes=(sb2,),
                               tag=f"Pr{d}_{ch}_{hp}")
                        P.i_tt(ALU.mult, sb2[:, 0:TC], sb2[:, 0:TC],
                               k_fm[hp][:, 4 - d:4 - d + TC],
                               reads=(sb2, k_fm[hp]), writes=(sb2,),
                               tag=f"Pk{d}_{ch}_{hp}")
                    Pd = sb2
                    Pres = (sb2,)
                for e in range(2):
                    hh = hp * 2 + e
                    for bt in range(NBLK):
                        src = Pd[64 * e:64 * e + 64, bt * 128:(bt + 1) * 128] if d == 0 \
                            else Pd[64 * e:64 * e + 64, bt * 128:(bt + 1) * 128]
                        P.add("pe", lambda h, o=ps[4][:, bt * 80 + d * 16 + hh: bt * 80 + d * 16 + hh + 1],
                              l=src, r_=ones_bf[64 * e:64 * e + 64, 0:1]:
                              h.matmul(o, l, r_, start=True, stop=True),
                              reads=Pres + (ones_bf,), writes=(ps[4],),
                              tag=f"cmm{ch}_{bt}_{d}_{hh}")
        for bt in range(NBLK):
            P.i_tt(ALU.mult, c_sb[bt][:], ps[4][:, bt * 80:bt * 80 + 80], cmask_sb[:],
                   reads=(ps[4], cmask_sb), writes=(c_sb[bt],), tag=f"cev{ch}_{bt}")

        for bt in range(NBLK):
            for s in range(8):
                sg = bt * 8 + s
                ks_b = k2s[sg % 2]
                vs_b = Vts[sg % 2]
                ys_b = ystage[sg % 2]
                P.add("sp", lambda h, o=ks_b[0:16, :], i=k2t[bt][s * 16:s * 16 + 16, :]:
                      h.dma_start(out=o, in_=i), dma=True,
                      reads=(k2t[bt],), writes=(ks_b,), tag=f"k2sl{ch}_{sg}")
                P.add("sp", lambda h, o=vs_b[0:16, :], i=Vt[bt][s * 16:s * 16 + 16, :]:
                      h.dma_start(out=o, in_=i), dma=True,
                      reads=(Vt[bt],), writes=(vs_b,), tag=f"vtsl{ch}_{sg}")
                for hh in range(16):
                    hp, e = hh // 2, hh % 2
                    P.add("pe", lambda h, o=ps16[0:16, hh * 64:hh * 64 + 64],
                          l=rq[hp][64 * e:64 * e + 64, sg * 16:sg * 16 + 16],
                          r_=st[hp][64 * e:64 * e + 64, :]:
                          h.matmul(o, l, r_, start=True, stop=True),
                          reads=(rq[hp], st[hp]), writes=(ps16,),
                          tag=f"ys{ch}_{sg}_{hh}")
                P.i_copy(ys_b[0:16, :], ps16[0:16, :], eng="act",
                         reads=(ps16,), writes=(ys_b,), tag=f"y16e{ch}_{sg}")
                P.add("sp", lambda h, o=y_tm[bt][s * 16:s * 16 + 16, :], i=ys_b[0:16, :]:
                      h.dma_start(out=o, in_=i), dma=True,
                      reads=(ys_b,), writes=((id(y_tm[bt]), "r", s),),
                      tag=f"ysw{ch}_{sg}")
                for hh in range(16):
                    hp, e = hh // 2, hh % 2
                    P.add("pe", lambda h, o=ps[2][64 * e:64 * e + 64, (hh // 2) * 64:(hh // 2) * 64 + 64],
                          l=ks_b[0:16, hh * 64:hh * 64 + 64],
                          r_=vs_b[0:16, hh * 64:hh * 64 + 64]:
                          h.matmul(o, l, r_, start=True, stop=True),
                          reads=(ks_b, vs_b), writes=(ps[2],),
                          tag=f"kv{ch}_{sg}_{hh}")
                for hp in range(nct):
                    P.i_stt(st[hp][:], st[hp][:], Eends[hp][:, sg:sg + 1],
                            ps[2][:, hp * 64:(hp + 1) * 64], ALU.mult, ALU.add,
                            reads=(Eends[hp], ps[2], st[hp]), writes=(st[hp],),
                            tag=f"stu{ch}_{sg}_{hp}")
        for bt in range(NBLK):
            csb = c_sb[bt]
            ykeys = tuple((id(y_tm[bt]), "r", s) for s in range(8))
            for hh in range(16):
                hp, e = hh // 2, hh % 2
                pA = ps[e]
                P.add("pe", lambda h, o=pA[:, 0:128],
                      l=kA[hp][64 * e:64 * e + 64, bt * 128:(bt + 1) * 128],
                      r_=rq[hp][64 * e:64 * e + 64, bt * 128:(bt + 1) * 128]:
                      h.matmul(o, l, r_, start=True, stop=True),
                      reads=(kA[hp], rq[hp]), writes=(pA,), tag=f"Amm{ch}_{bt}_{hh}")
                Asb = scrb[hh % 2]
                P.i_tt(ALU.mult, Asb[:, 0:128], pA[:, 0:128], amask_sb[:],
                       reads=(pA, amask_sb), writes=(Asb,), tag=f"Amk{ch}_{bt}_{hh}")
                pY = ps[3 + hh // 8] if False else (ps[3] if hh < 8 else ps[4])
                P.add("pe", lambda h, o=pY[:, (hh % 8) * 64:(hh % 8) * 64 + 64],
                      l=Asb[:, 0:128], r_=Vt[bt][:, hh * 64:hh * 64 + 64]:
                      h.matmul(o, l, r_, start=True, stop=True),
                      reads=(Asb, Vt[bt]), writes=(pY,), tag=f"Ymm{ch}_{bt}_{hh}")
            P.i_tt(ALU.add, y_tm[bt][:, 0:512], y_tm[bt][:, 0:512], ps[3][:, 0:512],
                   reads=(ps[3],) + ykeys, writes=(y_tm[bt],), tag=f"yf0{ch}_{bt}")
            P.i_tt(ALU.add, y_tm[bt][:, 512:1024], y_tm[bt][:, 512:1024], ps[4][:, 0:512],
                   reads=(ps[4],) + ykeys, writes=(y_tm[bt],), tag=f"yf1{ch}_{bt}")
            for d in range(0, 5):
                if d == 0:
                    vsrc = Vt[bt]
                    vres = (Vt[bt],)
                else:
                    if bt == 0:
                        P.add("sp", lambda h, o=Vsh[bt][0:d, :], i=Vtail[4 - d:4, :]:
                              h.dma_start(out=o, in_=i), dma=True,
                              reads=(Vtail,), writes=(Vsh[bt],),
                              tag=f"vshh{ch}_{bt}_{d}")
                    else:
                        P.add("sp", lambda h, o=Vsh[bt][0:d, :],
                              i=Vt[bt - 1][128 - d:128, :]:
                              h.dma_start(out=o, in_=i), dma=True,
                              reads=(Vt[bt - 1],), writes=(Vsh[bt],),
                              tag=f"vshh{ch}_{bt}_{d}")
                    P.add("sp", lambda h, o=Vsh[bt][d:128, :], i=Vt[bt][0:128 - d, :]:
                          h.dma_start(out=o, in_=i), dma=True,
                          reads=(Vt[bt],), writes=(Vsh[bt],),
                          tag=f"vshb{ch}_{bt}_{d}")
                    vsrc = Vsh[bt]
                    vres = (Vsh[bt],)
                cview = csb[:, d * 16:d * 16 + 16].unsqueeze(2).broadcast_to([128, 16, 64])
                P.i_tt(ALU.mult, ysc[d % 2][:].rearrange("p (s j) -> p s j", j=64),
                       vsrc[:].rearrange("p (s j) -> p s j", j=64), cview,
                       reads=vres + (csb,), writes=(ysc[d % 2],), tag=f"yn{ch}_{bt}_{d}")
                P.i_tt(ALU.add, y_tm[bt][:], y_tm[bt][:], ysc[d % 2][:],
                       reads=(ysc[d % 2],) + ykeys, writes=(y_tm[bt],),
                       tag=f"yna{ch}_{bt}_{d}")
        P.add("sp", lambda h, o=Vtail[0:4, :], i=Vt[NBLK - 1][124:128, :]:
              h.dma_start(out=o, in_=i), dma=True,
              reads=(Vt[NBLK - 1],), writes=(Vtail,), tag=f"vtailu{ch}")

        for bt in range(NBLK):
            y3 = y_tm[bt][:].rearrange("p (s j) -> p s j", j=64)
            P.i_reduce(ALU.add, gsum[:, 0:16], y3, reads=(y_tm[bt],), writes=(gsum,),
                       tag=f"gn1{ch}_{bt}")
            P.i_act(AF.Square, ysq[:], y_tm[bt][:], reads=(y_tm[bt],), writes=(ysq,),
                    tag=f"gn2{ch}_{bt}")
            P.i_reduce(ALU.add, gsq[:, 0:16], ysq[:].rearrange("p (s j) -> p s j", j=64),
                       reads=(ysq,), writes=(gsq,), tag=f"gn3{ch}_{bt}")
            P.i_ts(gsum[:], gsum[:], -1.0 / 64, None, ALU.mult,
                   reads=(gsum,), writes=(gsum,), tag=f"gn4{ch}_{bt}")
            P.i_ts(gsq[:], gsq[:], 1.0 / 64, None, ALU.mult,
                   reads=(gsq,), writes=(gsq,), tag=f"gn5{ch}_{bt}")
            P.i_stt(gstat[:], gsum[:], 1.0, gsum[:], ALU.mult, ALU.mult,
                    reads=(gsum,), writes=(gstat,), tag=f"gn6{ch}_{bt}")
            P.i_stt(gstat[:], gstat[:], -1.0, gsq[:], ALU.mult, ALU.add,
                    reads=(gstat, gsq), writes=(gstat,), tag=f"gn7{ch}_{bt}")
            P.i_ts(gstat[:], gstat[:], 1.0, 1e-5, ALU.mult, ALU.add,
                   reads=(gstat,), writes=(gstat,), tag=f"gn8{ch}_{bt}")
            P.i_act(AF.Ln, gstat[:], gstat[:], reads=(gstat,), writes=(gstat,),
                    tag=f"gn9{ch}_{bt}")
            P.i_act(AF.Exp, gstat[:], gstat[:], scale=-0.5, reads=(gstat,),
                    writes=(gstat,), tag=f"gn10{ch}_{bt}")
            nmv = gsum[:].unsqueeze(2).broadcast_to([128, 16, 64])
            rsv = gstat[:].unsqueeze(2).broadcast_to([128, 16, 64])
            P.i_tt(ALU.add, ysq[:].rearrange("p (s j) -> p s j", j=64), y3, nmv,
                   reads=(y_tm[bt], gsum), writes=(ysq,), tag=f"gna{ch}_{bt}")
            P.i_tt(ALU.mult, ysq[:].rearrange("p (s j) -> p s j", j=64),
                   ysq[:].rearrange("p (s j) -> p s j", j=64), rsv,
                   reads=(ysq, gstat), writes=(ysq,), tag=f"gnb{ch}_{bt}")
            P.i_tt(ALU.mult, ysq[:], ysq[:], lnw_rep[:],
                   reads=(ysq, lnw_rep), writes=(ysq,), tag=f"gnc{ch}_{bt}")
            P.i_tt(ALU.add, ysq[:], ysq[:], lnb_rep[:],
                   reads=(ysq, lnb_rep), writes=(ysq,), tag=f"gnd{ch}_{bt}")
            P.i_act(AF.Silu, o_t[bt][:], Gt[bt][:], reads=(Gt[bt],), writes=(o_t[bt],),
                    tag=f"sig{ch}_{bt}")
            P.i_tt(ALU.mult, o_t[bt][:], o_t[bt][:], ysq[:],
                   reads=(o_t[bt], ysq), writes=(o_t[bt],), tag=f"og{ch}_{bt}")
        for bt in range(NBLK):
            for it in range(nct):
                sl = pst[:, (it % 4) * 128:(it % 4) * 128 + 128]
                P.add("pe", lambda h, o=sl, i=o_t[bt][:, it * 128:(it + 1) * 128],
                      idn=ident_sb: h.transpose(o, i, idn[:]),
                      reads=(o_t[bt], ident_sb), writes=(pst,), tag=f"otr{ch}_{bt}_{it}")
                P.i_copy(o_fm[it][:, bt * 128:(bt + 1) * 128], sl, eng="dve",
                         reads=(pst,), writes=(o_fm[it],), tag=f"otrev{ch}_{bt}_{it}")
        load_w("o", f"wo{ch}")
        for ot in range(nct):
            p2 = ps[ot % 2]
            pairs = [(wbuf[:, ct * 1024 + ot * 128: ct * 1024 + ot * 128 + 128],
                      o_fm[ct][:]) for ct in range(nct)]
            mm_group(P, p2[:, 0:TC], p2, pairs, reads=tuple(o_fm) + (wbuf,),
                     tag=f"omm{ch}_{ot}")
            P.i_copy(outc[ot][:], p2[:, 0:TC], eng="dve",
                     reads=(p2,), writes=(outc[ot],), tag=f"oev{ch}_{ot}")
        nmb, rsb, nmb_res, rsb_res = ln_stats(C, outc, 0, TC, rowbuf, lnpool,
                                              tag=f"l1agl{ch}", sq_scratch=scr)
        for i in range(nct):
            P.i_tt(ALU.add, scr[i % 2][:, 0:TC], outc[i][:], nmb,
                   reads=(outc[i], nmb_res), writes=(scr[i % 2],), tag=f"ga{ch}_{i}")
            P.i_tt(ALU.mult, scr[i % 2][:, 0:TC], scr[i % 2][:, 0:TC], rsb,
                   reads=(scr[i % 2], rsb_res), writes=(scr[i % 2],), tag=f"gb{ch}_{i}")
            P.i_tt(ALU.add, rc[i][:], rc[i][:], scr[i % 2][:, 0:TC],
                   reads=(scr[i % 2], rc[i]), writes=(rc[i],), tag=f"gc{ch}_{i}")
            P.dma_sp(xres_dram_out[i * 128:(i + 1) * 128, c0:c0 + TC], rc[i][:],
                     reads=(rc[i],), writes=(f"xout{i}",), tag=f"xst{ch}_{i}")


def l1_attn_host_inputs(inputs):
    d = {}
    for nm, key in (("r", "L1_attn_rW"), ("k", "L1_attn_kW"), ("v", "L1_attn_vW"),
                    ("g", "L1_attn_gW"), ("o", "L1_attn_oW")):
        d[f"l1a_{nm}WT"] = hT(inputs[key])
    la = np.concatenate([hT(inputs[f"L1_attn_{p}_la"]) for p in "rkvgd"], axis=1)
    d["l1a_laT"] = np.ascontiguousarray(la)
    lb = np.concatenate([hT(inputs[f"L1_attn_{p}_lb"]) for p in "rkvgd"], axis=1)
    d["l1a_lbT"] = np.ascontiguousarray(lb)
    ll = np.concatenate([colpack(inputs[f"L1_attn_{p}_ll"]) for p in "rkvgd"], axis=1)
    d["l1a_ll"] = np.ascontiguousarray(ll)
    d["l1a_xw"] = colpack(inputs["L1_attn_xw"])
    d["l1a_u"] = colpack(inputs["L1_attn_u"])
    st0 = np.asarray(inputs["L1_attn_init_state"], np.float32)
    d["l1a_st0"] = np.ascontiguousarray(st0.reshape(8, 128, 64))
    lnw = np.tile(np.asarray(inputs["L1_attn_ln_w"], np.float32), 16)
    lnb = np.tile(np.asarray(inputs["L1_attn_ln_b"], np.float32), 16)
    d["l1a_lnwb"] = np.ascontiguousarray(np.stack([lnw, lnb]))
    am, cm = l1a_consts()
    d["l1a_amask"] = am
    d["l1a_cmask"] = cm
    d["l1a_ident"] = np.eye(128, dtype=np.float32)
    return d


def prog_l1_attn(strict=True):
    nc = bass.Bass("TRN2", target_bir_lowering=False, debug=False)
    with ExitStack() as es:
        C = Ctx(nc, es)
        xin = nc.dram_tensor("xres_in", [D, T], F32, kind="ExternalInput").ap()
        xout = nc.dram_tensor("xres_out", [D, T], F32, kind="ExternalOutput").ap()
        build_l1_attn(C, xin, xout)
        stores = [op for op in C.P.ops if op.tag.startswith("xst")]
        C.P.add("sp", lambda h: None, deps=tuple(stores), tag="final_wait")
        C.P.emit(strict=strict)
    return nc





def build_l0_attn_p1(C):
    P, nc = C.P, C.nc
    nct = 8
    TCH = 512
    NCH = T // TCH
    xin = nc.dram_tensor("xres_in", [D, T], F32, kind="ExternalInput").ap()
    ipw = nc.dram_tensor("l0a_inprojT", [1024, 4096], F32, kind="ExternalInput").ap()
    nwb = nc.dram_tensor("l0a_nwb", [128, 16], F32, kind="ExternalInput").ap()
    cwp = nc.dram_tensor("l0a_convw", [128, 16 * 10], F32, kind="ExternalInput").ap()
    xpjf = nc.dram_tensor("l0a_xprojTf", [2048, 96], F32, kind="ExternalInput").ap()
    xpjr = nc.dram_tensor("l0a_xprojTr", [2048, 96], F32, kind="ExternalInput").ap()
    dpjf = nc.dram_tensor("l0a_dtprojTf", [64, 2048], F32, kind="ExternalInput").ap()
    dpjr = nc.dram_tensor("l0a_dtprojTr", [64, 2048], F32, kind="ExternalInput").ap()
    dtbp = nc.dram_tensor("l0a_dtb", [128, 32], F32, kind="ExternalInput").ap()
    z_sp = nc.dram_tensor("l0a_z", [2048, T], BF16, kind="ExternalOutput").ap()
    xc_sp = {dd: nc.dram_tensor(f"l0a_xc_{dd}", [2048, T], BF16, kind="ExternalOutput").ap()
             for dd in "fr"}
    dt_sp = {dd: nc.dram_tensor(f"l0a_dt_{dd}", [2048, T], F32, kind="ExternalOutput").ap()
             for dd in "fr"}
    bc_sp = {dd: nc.dram_tensor(f"l0a_bc_{dd}", [32, T], F32, kind="ExternalOutput").ap()
             for dd in "fr"}

    w_in = C.sbuf("l0a_wi", [128, 8 * 4096], BF16)
    P.dma_pool(w_in[:].rearrange("p (c n) -> p c n", n=4096),
               ipw.rearrange("(c p) n -> p c n", p=128), writes=(w_in,), tag="wi")
    w_xp = {}
    w_dp = {}
    for dd, xp, dp in (("f", xpjf, dpjf), ("r", xpjr, dpjr)):
        w_xp[dd] = C.sbuf(f"l0a_wxp{dd}", [128, 16 * 96], BF16)
        P.dma_pool(w_xp[dd][:].rearrange("p (c n) -> p c n", n=96),
                   xp.rearrange("(c p) n -> p c n", p=128), writes=(w_xp[dd],),
                   tag=f"wxp{dd}")
        w_dp[dd] = C.sbuf(f"l0a_wdp{dd}", [64, 2048], BF16)
        P.dma_pool(w_dp[dd][:], dp[:], writes=(w_dp[dd],), tag=f"wdp{dd}")
    nwb_sb = C.sbuf("l0a_nwb", [128, 16], F32)
    P.dma_sp(nwb_sb[:], nwb[:], writes=(nwb_sb,), tag="nwb")
    cw_sb = C.sbuf("l0a_cw", [128, 160], F32)
    P.dma_sp(cw_sb[:], cwp[:], writes=(cw_sb,), tag="cw")
    dtb_sb = C.sbuf("l0a_dtb", [128, 32], F32)
    P.dma_sp(dtb_sb[:], dtbp[:], writes=(dtb_sb,), tag="dtb")

    rc = C.tiles("l0a_rc", nct, TCH, F32)
    hh = C.tiles("l0a_h", nct, TCH, BF16)
    xp_bufs = [[C.sbuf(f"l0a_xp{b}_{i}", [128, 3 + TCH + 3], BF16) for i in range(16)]
               for b in range(2)]
    xcb = [C.sbuf(f"l0a_xcb{i}", [128, TCH], BF16) for i in range(16)]
    dtr_sb = C.sbuf("l0a_dtr", [64, TCH], BF16)
    zb_sb = C.sbuf("l0a_zb", [128, TCH], BF16)
    bc_sb = C.sbuf("l0a_bcsb", [32, TCH], F32)
    scr = [C.sbuf(f"l0a_scr{i}", [128, TCH], F32) for i in range(3)]
    rowbuf = C.sbuf("l0a_rows", [1, 3 * TCH], F32)
    ps = [C.psum(f"l0a_ps{i}", [128, 512]) for i in range(4)]
    ps_s = C.psum("l0a_pss", [1, 1024])
    lnpool = {"sx": (ps_s, 0), "sq": (ps_s, 512), "nmb": (ps[2], 0), "rsb": (ps[3], 0)}

    for b in range(2):
        for i in range(16):
            P.i_memset(xp_bufs[b][i][:, 0:3], 0.0, writes=(xp_bufs[b][i],), tag="xpg")
            P.i_memset(xp_bufs[b][i][:, 3 + TCH:], 0.0, writes=(xp_bufs[b][i],), tag="xpg2")

    def conv_dir(ch, b, dd):
        c0 = ch * TCH
        for i in range(16):
            xpt = xp_bufs[b][i]
            wcol = lambda j: cw_sb[:, i * 10 + (0 if dd == "f" else 5) + j:
                                   i * 10 + (0 if dd == "f" else 5) + j + 1]
            s0 = scr[i % 2]
            if dd == "f":
                P.i_ts(s0[:, 0:TCH], xpt[:, 0:TCH], wcol(0), None, ALU.mult,
                       reads=(xpt, cw_sb), writes=(s0,), tag=f"cv0{dd}{ch}_{i}")
                for k in (1, 2, 3):
                    P.i_stt(s0[:, 0:TCH], xpt[:, k:k + TCH], wcol(k), s0[:, 0:TCH],
                            ALU.mult, ALU.add, reads=(xpt, cw_sb, s0), writes=(s0,),
                            tag=f"cv{k}{dd}{ch}_{i}")
            else:
                P.i_ts(s0[:, 0:TCH], xpt[:, 6:6 + TCH], wcol(0), None, ALU.mult,
                       reads=(xpt, cw_sb), writes=(s0,), tag=f"cv0{dd}{ch}_{i}")
                for k in (1, 2, 3):
                    P.i_stt(s0[:, 0:TCH], xpt[:, 6 - k:6 - k + TCH], wcol(k),
                            s0[:, 0:TCH], ALU.mult, ALU.add,
                            reads=(xpt, cw_sb, s0), writes=(s0,),
                            tag=f"cv{k}{dd}{ch}_{i}")
            bcol = cw_sb[:, i * 10 + (4 if dd == "f" else 9):
                         i * 10 + (4 if dd == "f" else 9) + 1]
            P.i_act(AF.Silu, xcb[i][:], s0[:, 0:TCH], bias=bcol,
                    reads=(s0, cw_sb), writes=(xcb[i],), tag=f"cvs{dd}{ch}_{i}")
            P.dma_sp(xc_sp[dd][i * 128:(i + 1) * 128, c0:c0 + TCH], xcb[i][:],
                     reads=(xcb[i],), writes=(f"xcsp{dd}{i}",), tag=f"xcw{dd}{ch}_{i}")
        pairs = [(w_xp[dd][:, ct * 96:(ct + 1) * 96], xcb[ct][:]) for ct in range(16)]
        mm_group(P, ps[0][0:96, 0:TCH], ps[0], pairs,
                 reads=tuple(xcb) + (w_xp[dd],), tag=f"xdbl{dd}{ch}")
        P.i_copy(dtr_sb[:, 0:TCH], ps[0][0:64, 0:TCH], eng="dve",
                 reads=(ps[0],), writes=(dtr_sb,), tag=f"dtr{dd}{ch}")
        P.i_copy(bc_sb[:, 0:TCH], ps[0][64:96, 0:TCH], eng="act",
                 reads=(ps[0],), writes=(bc_sb,), tag=f"bc{dd}{ch}")
        P.dma_sp(bc_sp[dd][:, c0:c0 + TCH], bc_sb[:, 0:TCH],
                 reads=(bc_sb,), writes=(f"bcsp{dd}",), tag=f"bcw{dd}{ch}")
        for ot in range(16):
            p1 = ps[1]
            mm_group(P, p1[:, 0:TCH], p1,
                     [(w_dp[dd][:, ot * 128:(ot + 1) * 128], dtr_sb[:, 0:TCH])],
                     reads=(dtr_sb, w_dp[dd]), tag=f"dtm{dd}{ch}_{ot}")
            dcol = dtb_sb[:, (0 if dd == "f" else 16) + ot:
                          (0 if dd == "f" else 16) + ot + 1]
            s1, s2 = scr[0], scr[1]
            P.i_ts(s1[:, 0:TCH], p1[:, 0:TCH], dcol, None, ALU.add,
                   reads=(p1, dtb_sb), writes=(s1,), tag=f"dta{dd}{ch}_{ot}")
            P.i_act(AF.Abs, s2[:, 0:TCH], s1[:, 0:TCH],
                    reads=(s1,), writes=(s2,), tag=f"dtb{dd}{ch}_{ot}")
            P.i_act(AF.Exp, s2[:, 0:TCH], s2[:, 0:TCH], scale=-1.0,
                    reads=(s2,), writes=(s2,), tag=f"dtc{dd}{ch}_{ot}")
            P.i_ts(s2[:, 0:TCH], s2[:, 0:TCH], 1.0, 1.0, ALU.mult, ALU.add,
                   reads=(s2,), writes=(s2,), tag=f"dtd{dd}{ch}_{ot}")
            P.i_act(AF.Ln, s2[:, 0:TCH], s2[:, 0:TCH],
                    reads=(s2,), writes=(s2,), tag=f"dte{dd}{ch}_{ot}")
            P.i_ts(s1[:, 0:TCH], s1[:, 0:TCH], 0.0, None, ALU.max,
                   reads=(s1,), writes=(s1,), tag=f"dtf{dd}{ch}_{ot}")
            P.i_tt(ALU.add, scr[2][:, 0:TCH], s1[:, 0:TCH], s2[:, 0:TCH],
                   reads=(s1, s2), writes=(scr[2],), tag=f"dtg{dd}{ch}_{ot}")
            P.dma_sp(dt_sp[dd][ot * 128:(ot + 1) * 128, c0:c0 + TCH],
                     scr[2][:, 0:TCH], reads=(scr[2],), writes=(f"dtsp{dd}{ot}",),
                     tag=f"dtw{dd}{ch}_{ot}")

    for ch in range(NCH):
        c0 = ch * TCH
        b = ch % 2
        for i in range(nct):
            P.dma_sp(rc[i][:], xin[i * 128:(i + 1) * 128, c0:c0 + TCH],
                     writes=(rc[i],), tag=f"rcl{ch}_{i}")
        nmb, rsb, nmb_res, rsb_res = ln_stats(C, rc, 0, TCH, rowbuf, lnpool,
                                              tag=f"l0aln{ch}", sq_scratch=scr)
        for i in range(nct):
            s0 = scr[i % 2]
            P.i_tt(ALU.add, s0[:, 0:TCH], rc[i][:], nmb,
                   reads=(rc[i], nmb_res), writes=(s0,), tag=f"lna{ch}_{i}")
            P.i_tt(ALU.mult, s0[:, 0:TCH], s0[:, 0:TCH], rsb,
                   reads=(s0, rsb_res), writes=(s0,), tag=f"lnb{ch}_{i}")
            P.i_ts(hh[i][:], s0[:, 0:TCH], nwb_sb[:, i:i + 1], nwb_sb[:, 8 + i:9 + i],
                   ALU.mult, ALU.add, reads=(s0, nwb_sb), writes=(hh[i],),
                   tag=f"lnc{ch}_{i}")
        for ot in range(32):
            p1 = ps[ot % 2]
            pairs = [(w_in[:, ct * 4096 + ot * 128: ct * 4096 + ot * 128 + 128],
                      hh[ct][:]) for ct in range(nct)]
            mm_group(P, p1[:, 0:TCH], p1, pairs, reads=tuple(hh) + (w_in,),
                     tag=f"ip{ch}_{ot}")
            if ot < 16:
                P.i_copy(xp_bufs[b][ot][:, 3:3 + TCH], p1[:, 0:TCH], eng="dve",
                         reads=(p1,), writes=(xp_bufs[b][ot],), tag=f"ipx{ch}_{ot}")
            else:
                P.i_copy(zb_sb[:, 0:TCH], p1[:, 0:TCH], eng="dve",
                         reads=(p1,), writes=(zb_sb,), tag=f"ipz{ch}_{ot}")
                P.dma_sp(z_sp[(ot - 16) * 128:(ot - 15) * 128, c0:c0 + TCH],
                         zb_sb[:, 0:TCH], reads=(zb_sb,), writes=(f"zsp{ot}",),
                         tag=f"zw{ch}_{ot}")
        if ch > 0:
            for i in range(16):
                P.add("sp", lambda h, o=xp_bufs[b][i][:, 0:3],
                      i_=xp_bufs[1 - b][i][:, TCH:TCH + 3]: h.dma_start(out=o, in_=i_),
                      dma=True, reads=(xp_bufs[1 - b][i],), writes=(xp_bufs[b][i],),
                      tag=f"xgc{ch}_{i}")
            for i in range(16):
                P.add("sp", lambda h, o=xp_bufs[1 - b][i][:, 3 + TCH:6 + TCH],
                      i_=xp_bufs[b][i][:, 3:6]: h.dma_start(out=o, in_=i_),
                      dma=True, reads=(xp_bufs[b][i],), writes=(xp_bufs[1 - b][i],),
                      tag=f"xgr{ch}_{i}")
        conv_dir(ch, b, "f")
        if ch > 0:
            bprev = 1 - b
            _conv_rev(C, ch - 1, bprev, TCH, xp_bufs, xcb, cw_sb, scr, w_xp, dtr_sb,
                      bc_sb, ps, dtb_sb, w_dp, xc_sp, dt_sp, bc_sp, P)
    bl = (NCH - 1) % 2
    for i in range(16):
        P.i_memset(xp_bufs[bl][i][:, 3 + TCH:6 + TCH], 0.0,
                   writes=(xp_bufs[bl][i],), tag=f"xpgz{i}")
    _conv_rev(C, NCH - 1, bl, TCH, xp_bufs, xcb, cw_sb, scr, w_xp,
              dtr_sb, bc_sb, ps, dtb_sb, w_dp, xc_sp, dt_sp, bc_sp, P)
    stores = [op for op in P.ops if op.dma and (op.tag.startswith(("xcw", "dtw", "bcw", "zw")))]
    P.add("sp", lambda h: None, deps=tuple(stores), tag="final_wait")


def _conv_rev(C, ch, b, TCH, xp_bufs, xcb, cw_sb, scr, w_xp, dtr_sb, bc_sb, ps,
              dtb_sb, w_dp, xc_sp, dt_sp, bc_sp, P):
    dd = "r"
    c0 = ch * TCH
    for i in range(16):
        xpt = xp_bufs[b][i]
        wcol = lambda j: cw_sb[:, i * 10 + 5 + j: i * 10 + 5 + j + 1]
        s0 = scr[i % 2]
        P.i_ts(s0[:, 0:TCH], xpt[:, 6:6 + TCH], wcol(0), None, ALU.mult,
               reads=(xpt, cw_sb), writes=(s0,), tag=f"rcv0{ch}_{i}")
        for k in (1, 2, 3):
            P.i_stt(s0[:, 0:TCH], xpt[:, 6 - k:6 - k + TCH], wcol(k), s0[:, 0:TCH],
                    ALU.mult, ALU.add, reads=(xpt, cw_sb, s0), writes=(s0,),
                    tag=f"rcv{k}{ch}_{i}")
        bcol = cw_sb[:, i * 10 + 9: i * 10 + 10]
        P.i_act(AF.Silu, xcb[i][:], s0[:, 0:TCH], bias=bcol,
                reads=(s0, cw_sb), writes=(xcb[i],), tag=f"rcvs{ch}_{i}")
        P.dma_sp(xc_sp[dd][i * 128:(i + 1) * 128, c0:c0 + TCH], xcb[i][:],
                 reads=(xcb[i],), writes=(f"xcsp{dd}{i}",), tag=f"xcw{dd}{ch}_{i}")
    pairs = [(w_xp[dd][:, ct * 96:(ct + 1) * 96], xcb[ct][:]) for ct in range(16)]
    mm_group(P, ps[0][0:96, 0:TCH], ps[0], pairs,
             reads=tuple(xcb) + (w_xp[dd],), tag=f"rxdbl{ch}")
    P.i_copy(dtr_sb[:, 0:TCH], ps[0][0:64, 0:TCH], eng="dve",
             reads=(ps[0],), writes=(dtr_sb,), tag=f"rdtr{ch}")
    P.i_copy(bc_sb[:, 0:TCH], ps[0][64:96, 0:TCH], eng="act",
             reads=(ps[0],), writes=(bc_sb,), tag=f"rbc{ch}")
    P.dma_sp(bc_sp[dd][:, c0:c0 + TCH], bc_sb[:, 0:TCH],
             reads=(bc_sb,), writes=(f"bcsp{dd}",), tag=f"bcw{dd}{ch}")
    for ot in range(16):
        p1 = ps[1]
        mm_group(P, p1[:, 0:TCH], p1,
                 [(w_dp[dd][:, ot * 128:(ot + 1) * 128], dtr_sb[:, 0:TCH])],
                 reads=(dtr_sb, w_dp[dd]), tag=f"rdtm{ch}_{ot}")
        dcol = dtb_sb[:, 16 + ot:16 + ot + 1]
        s1, s2 = scr[0], scr[1]
        P.i_ts(s1[:, 0:TCH], p1[:, 0:TCH], dcol, None, ALU.add,
               reads=(p1, dtb_sb), writes=(s1,), tag=f"rdta{ch}_{ot}")
        P.i_act(AF.Abs, s2[:, 0:TCH], s1[:, 0:TCH],
                reads=(s1,), writes=(s2,), tag=f"rdtb{ch}_{ot}")
        P.i_act(AF.Exp, s2[:, 0:TCH], s2[:, 0:TCH], scale=-1.0,
                reads=(s2,), writes=(s2,), tag=f"rdtc{ch}_{ot}")
        P.i_ts(s2[:, 0:TCH], s2[:, 0:TCH], 1.0, 1.0, ALU.mult, ALU.add,
               reads=(s2,), writes=(s2,), tag=f"rdtd{ch}_{ot}")
        P.i_act(AF.Ln, s2[:, 0:TCH], s2[:, 0:TCH],
                reads=(s2,), writes=(s2,), tag=f"rdte{ch}_{ot}")
        P.i_ts(s1[:, 0:TCH], s1[:, 0:TCH], 0.0, None, ALU.max,
               reads=(s1,), writes=(s1,), tag=f"rdtf{ch}_{ot}")
        P.i_tt(ALU.add, scr[2][:, 0:TCH], s1[:, 0:TCH], s2[:, 0:TCH],
               reads=(s1, s2), writes=(scr[2],), tag=f"rdtg{ch}_{ot}")
        P.dma_sp(dt_sp[dd][ot * 128:(ot + 1) * 128, c0:c0 + TCH],
                 scr[2][:, 0:TCH], reads=(scr[2],), writes=(f"dtsp{dd}{ot}",),
                 tag=f"dtw{dd}{ch}_{ot}")


def build_l0_attn_p2(C):
    P, nc = C.P, C.nc
    nct = 8
    SC = 512
    NSC = T // SC
    xin = nc.dram_tensor("xres_in", [D, T], F32, kind="ExternalInput").ap()
    xout = nc.dram_tensor("xres_out", [D, T], F32, kind="ExternalOutput").ap()
    z_sp = nc.dram_tensor("l0a_z", [2048, T], BF16, kind="ExternalInput").ap()
    xc_sp = {dd: nc.dram_tensor(f"l0a_xc_{dd}", [2048, T], BF16, kind="ExternalInput").ap()
             for dd in "fr"}
    dt_sp = {dd: nc.dram_tensor(f"l0a_dt_{dd}", [2048, T], F32, kind="ExternalInput").ap()
             for dd in "fr"}
    bc_sp = {dd: nc.dram_tensor(f"l0a_bc_{dd}", [32, T], F32, kind="ExternalInput").ap()
             for dd in "fr"}
    alog = nc.dram_tensor("l0a_alog", [128, 32 * 16], F32, kind="ExternalInput").ap()
    dpack = nc.dram_tensor("l0a_d", [128, 32], F32, kind="ExternalInput").ap()
    opw = nc.dram_tensor("l0a_outprojT", [2048, 1024], F32, kind="ExternalInput").ap()

    alog_sb = C.sbuf("l0a_alog", [128, 512], F32)
    P.dma_sp(alog_sb[:], alog[:], writes=(alog_sb,), tag="alog")
    A_sb = C.sbuf("l0a_A", [128, 512], F32)
    P.i_act(AF.Exp, A_sb[:], alog_sb[:], reads=(alog_sb,), writes=(A_sb,), tag="Aexp")
    P.i_ts(A_sb[:], A_sb[:], -1.0, None, ALU.mult, reads=(A_sb,), writes=(A_sb,),
           tag="Aneg")
    d_sb = C.sbuf("l0a_dsb", [128, 32], F32)
    P.dma_sp(d_sb[:], dpack[:], writes=(d_sb,), tag="dp")
    w_op = C.sbuf("l0a_wop", [128, 16 * 1024], BF16)
    P.dma_pool(w_op[:].rearrange("p (c n) -> p c n", n=1024),
               opw.rearrange("(c p) n -> p c n", p=128), writes=(w_op,), tag="wop")

    yg_sum = [C.sbuf(f"l0a_yg{i}", [128, T], BF16) for i in range(16)]
    Brep = [C.sbuf(f"l0a_Br{n}", [128, SC], BF16) for n in range(16)]
    Crep = [C.sbuf(f"l0a_Cr{n}", [128, SC], BF16) for n in range(16)]
    hstate = [C.sbuf(f"l0a_hs{i}", [128, 16], F32) for i in range(16)]
    dt_sl = [C.sbuf(f"l0a_dts{i}", [128, SC], F32) for i in range(2)]
    xc_sl = [C.sbuf(f"l0a_xcs{i}", [128, SC], BF16) for i in range(2)]
    z_sl = [C.sbuf(f"l0a_zs{i}", [128, SC], BF16) for i in range(2)]
    du_sl = [C.sbuf(f"l0a_dus{i}", [128, SC], BF16) for i in range(2)]
    a_sc = [C.sbuf(f"l0a_asc{i}", [128, SC], F32) for i in range(2)]
    b_sc = [C.sbuf(f"l0a_bsc{i}", [128, SC], F32) for i in range(2)]
    h_sc = [C.sbuf(f"l0a_hsc{i}", [128, SC], BF16) for i in range(2)]
    y_sl = [C.sbuf(f"l0a_ysl{i}", [128, SC], F32) for i in range(2)]
    scr = [C.sbuf(f"l0a_p2scr{i}", [128, SC], F32) for i in range(2)]
    rc = C.tiles("l0a_p2rc", nct, SC, F32)
    outc = C.tiles("l0a_p2out", nct, SC, F32)
    rowbuf = C.sbuf("l0a_p2rows", [1, 3 * SC], F32)
    ps = [C.psum(f"l0a_p2ps{i}", [128, 512]) for i in range(4)]
    ps_s = C.psum("l0a_p2pss", [1, 1024])
    lnpool = {"sx": (ps_s, 0), "sq": (ps_s, 512), "nmb": (ps[2], 0), "rsb": (ps[3], 0)}

    for i in range(16):
        P.i_memset(hstate[i][:], 0.0, writes=(hstate[i],), tag=f"hs0_{i}")

    for dd in ("f", "r"):
        doff = 0 if dd == "f" else 256
        chunks = list(range(NSC)) if dd == "f" else list(range(NSC - 1, -1, -1))
        if dd == "r":
            for i in range(16):
                P.i_memset(hstate[i][:], 0.0, writes=(hstate[i],), tag=f"hsr_{i}")
        for ci, ch in enumerate(chunks):
            c0 = ch * SC
            for n in range(16):
                P.dma_pool(Brep[n][:],
                           bc_sp[dd][n:n + 1, c0:c0 + SC].partition_broadcast(128),
                           writes=(Brep[n],), tag=f"br{dd}{ch}_{n}")
                P.dma_pool(Crep[n][:],
                           bc_sp[dd][16 + n:17 + n, c0:c0 + SC].partition_broadcast(128),
                           writes=(Crep[n],), tag=f"cr{dd}{ch}_{n}")
            for sl in range(16):
                B = sl % 2
                P.dma_sp(dt_sl[B][:], dt_sp[dd][sl * 128:(sl + 1) * 128, c0:c0 + SC],
                         writes=(dt_sl[B],), tag=f"dtl{dd}{ch}_{sl}")
                P.dma_sp(xc_sl[B][:], xc_sp[dd][sl * 128:(sl + 1) * 128, c0:c0 + SC],
                         writes=(xc_sl[B],), tag=f"xcl{dd}{ch}_{sl}")
                P.dma_sp(z_sl[B][:], z_sp[sl * 128:(sl + 1) * 128, c0:c0 + SC],
                         writes=(z_sl[B],), tag=f"zl{dd}{ch}_{sl}")
                P.i_tt(ALU.mult, du_sl[B][:], dt_sl[B][:], xc_sl[B][:],
                       reads=(dt_sl[B], xc_sl[B]), writes=(du_sl[B],),
                       tag=f"du{dd}{ch}_{sl}")
                P.i_memset(y_sl[B][:], 0.0, writes=(y_sl[B],), tag=f"y0{dd}{ch}_{sl}")
                for n in range(16):
                    acol = A_sb[:, doff + sl * 16 + n: doff + sl * 16 + n + 1]
                    P.i_act(AF.Exp, a_sc[B][:], dt_sl[B][:], scale=acol,
                            reads=(dt_sl[B], A_sb), writes=(a_sc[B],),
                            tag=f"ae{dd}{ch}_{sl}_{n}")
                    P.i_tt(ALU.mult, b_sc[B][:], du_sl[B][:], Brep[n][:],
                           reads=(du_sl[B], Brep[n]), writes=(b_sc[B],),
                           tag=f"bb{dd}{ch}_{sl}_{n}")
                    hs = hstate[sl][:, n:n + 1]
                    if dd == "f":
                        P.i_scan(h_sc[B][:], a_sc[B][:], b_sc[B][:], hs,
                                 reads=(a_sc[B], b_sc[B], hstate[sl]),
                                 writes=(h_sc[B],), tag=f"sc{dd}{ch}_{sl}_{n}")
                        P.i_copy(hstate[sl][:, n:n + 1], h_sc[B][:, SC - 1:SC],
                                 eng="dve", reads=(h_sc[B],), writes=(hstate[sl],),
                                 tag=f"hc{dd}{ch}_{sl}_{n}")
                    else:
                        P.i_scan(h_sc[B][:, ::-1], a_sc[B][:, ::-1], b_sc[B][:, ::-1],
                                 hs, reads=(a_sc[B], b_sc[B], hstate[sl]),
                                 writes=(h_sc[B],), tag=f"sc{dd}{ch}_{sl}_{n}")
                        P.i_copy(hstate[sl][:, n:n + 1], h_sc[B][:, 0:1],
                                 eng="dve", reads=(h_sc[B],), writes=(hstate[sl],),
                                 tag=f"hc{dd}{ch}_{sl}_{n}")
                    P.i_tt(ALU.mult, h_sc[B][:], h_sc[B][:], Crep[n][:],
                           reads=(h_sc[B], Crep[n]), writes=(h_sc[B],),
                           tag=f"hC{dd}{ch}_{sl}_{n}")
                    P.i_tt(ALU.add, y_sl[B][:], y_sl[B][:], h_sc[B][:],
                           reads=(h_sc[B],), writes=(y_sl[B],),
                           tag=f"ya{dd}{ch}_{sl}_{n}")
                dcol = d_sb[:, (0 if dd == "f" else 16) + sl:
                            (0 if dd == "f" else 16) + sl + 1]
                P.i_stt(y_sl[B][:], xc_sl[B][:], dcol, y_sl[B][:], ALU.mult, ALU.add,
                        reads=(xc_sl[B], d_sb, y_sl[B]), writes=(y_sl[B],),
                        tag=f"yd{dd}{ch}_{sl}")
                P.i_act(AF.Silu, scr[B][:], z_sl[B][:],
                        reads=(z_sl[B],), writes=(scr[B],), tag=f"sz{dd}{ch}_{sl}")
                if dd == "f":
                    P.i_tt(ALU.mult, yg_sum[sl][:, c0:c0 + SC], y_sl[B][:], scr[B][:],
                           reads=(y_sl[B], scr[B]), writes=(yg_sum[sl],),
                           tag=f"yg{dd}{ch}_{sl}")
                else:
                    P.i_tt(ALU.mult, scr[B][:], y_sl[B][:], scr[B][:],
                           reads=(y_sl[B], scr[B]), writes=(scr[B],),
                           tag=f"ygm{dd}{ch}_{sl}")
                    P.i_tt(ALU.add, yg_sum[sl][:, c0:c0 + SC],
                           yg_sum[sl][:, c0:c0 + SC], scr[B][:],
                           reads=(scr[B], yg_sum[sl]), writes=(yg_sum[sl],),
                           tag=f"yg{dd}{ch}_{sl}")

    for ch in range(NSC):
        c0 = ch * SC
        for i in range(nct):
            P.dma_sp(rc[i][:], xin[i * 128:(i + 1) * 128, c0:c0 + SC],
                     writes=(rc[i],), tag=f"p2rcl{ch}_{i}")
        for ot in range(nct):
            p1 = ps[ot % 2]
            pairs = [(w_op[:, ct * 1024 + ot * 128: ct * 1024 + ot * 128 + 128],
                      yg_sum[ct][:, c0:c0 + SC]) for ct in range(16)]
            mm_group(P, p1[:, 0:SC], p1, pairs, reads=tuple(yg_sum) + (w_op,),
                     tag=f"op{ch}_{ot}")
            P.i_copy(outc[ot][:], p1[:, 0:SC], eng="dve",
                     reads=(p1,), writes=(outc[ot],), tag=f"opev{ch}_{ot}")
        nmb, rsb, nmb_res, rsb_res = ln_stats(C, rc, 0, SC, rowbuf, lnpool,
                                              tag=f"l0ailn{ch}", sq_scratch=scr)
        for i in range(nct):
            s0 = scr[i % 2]
            P.i_tt(ALU.add, s0[:, 0:SC], rc[i][:], nmb,
                   reads=(rc[i], nmb_res), writes=(s0,), tag=f"p2ia{ch}_{i}")
            P.i_tt(ALU.mult, s0[:, 0:SC], s0[:, 0:SC], rsb,
                   reads=(s0, rsb_res), writes=(s0,), tag=f"p2ib{ch}_{i}")
            P.i_tt(ALU.add, outc[i][:], outc[i][:], s0[:, 0:SC],
                   reads=(s0, outc[i]), writes=(outc[i],), tag=f"p2ic{ch}_{i}")
        nmb, rsb, nmb_res, rsb_res = ln_stats(C, outc, 0, SC, rowbuf, lnpool,
                                              tag=f"l0agl{ch}", sq_scratch=scr)
        for i in range(nct):
            s0 = scr[i % 2]
            P.i_tt(ALU.add, s0[:, 0:SC], outc[i][:], nmb,
                   reads=(outc[i], nmb_res), writes=(s0,), tag=f"p2ga{ch}_{i}")
            P.i_tt(ALU.mult, s0[:, 0:SC], s0[:, 0:SC], rsb,
                   reads=(s0, rsb_res), writes=(s0,), tag=f"p2gb{ch}_{i}")
            P.i_tt(ALU.add, rc[i][:], rc[i][:], s0[:, 0:SC],
                   reads=(s0, rc[i]), writes=(rc[i],), tag=f"p2gc{ch}_{i}")
            P.dma_sp(xout[i * 128:(i + 1) * 128, c0:c0 + SC], rc[i][:],
                     reads=(rc[i],), writes=(f"p2xo{i}",), tag=f"p2xst{ch}_{i}")
    stores = [op for op in P.ops if op.tag.startswith("p2xst")]
    P.add("sp", lambda h: None, deps=tuple(stores), tag="final_wait")


def l0_attn_host_inputs(inputs):
    d = {}
    d["l0a_inprojT"] = hT(inputs["L0_attn_in_proj_w"])
    nw = colpack(inputs["L0_attn_norm_w"])
    nb = colpack(inputs["L0_attn_norm_b"])
    d["l0a_nwb"] = np.ascontiguousarray(np.concatenate([nw, nb], axis=1))
    cw = np.zeros((128, 160), np.float32)
    cwf = np.asarray(inputs["L0_attn_conv_w_f"], np.float32)[:, 0, :]
    cwr = np.asarray(inputs["L0_attn_conv_w_r"], np.float32)[:, 0, :]
    cbf = np.asarray(inputs["L0_attn_conv_b_f"], np.float32)
    cbr = np.asarray(inputs["L0_attn_conv_b_r"], np.float32)
    for i in range(16):
        sl = slice(i * 128, (i + 1) * 128)
        cw[:, i * 10 + 0:i * 10 + 4] = cwf[sl]
        cw[:, i * 10 + 4] = cbf[sl]
        cw[:, i * 10 + 5:i * 10 + 9] = cwr[sl]
        cw[:, i * 10 + 9] = cbr[sl]
    d["l0a_convw"] = cw
    d["l0a_xprojTf"] = hT(inputs["L0_attn_x_proj_w_f"])
    d["l0a_xprojTr"] = hT(inputs["L0_attn_x_proj_w_r"])
    d["l0a_dtprojTf"] = hT(inputs["L0_attn_dt_proj_w_f"])
    d["l0a_dtprojTr"] = hT(inputs["L0_attn_dt_proj_w_r"])
    dtb = np.concatenate([colpack(inputs["L0_attn_dt_proj_b_f"], 16),
                          colpack(inputs["L0_attn_dt_proj_b_r"], 16)], axis=1)
    d["l0a_dtb"] = np.ascontiguousarray(dtb)
    al = np.zeros((128, 512), np.float32)
    for dd, key, off in (("f", "L0_attn_A_log_f", 0), ("r", "L0_attn_A_log_r", 256)):
        a = np.asarray(inputs[key], np.float32)
        for sl in range(16):
            al[:, off + sl * 16:off + sl * 16 + 16] = a[sl * 128:(sl + 1) * 128, :]
    d["l0a_alog"] = al
    dpk = np.concatenate([colpack(inputs["L0_attn_D_f"], 16),
                          colpack(inputs["L0_attn_D_r"], 16)], axis=1)
    d["l0a_d"] = np.ascontiguousarray(dpk)
    d["l0a_outprojT"] = hT(inputs["L0_attn_out_proj_w"])
    return d


def prog_l0_attn_p1(strict=True):
    nc = bass.Bass("TRN2", target_bir_lowering=False, debug=False)
    with ExitStack() as es:
        C = Ctx(nc, es)
        build_l0_attn_p1(C)
        C.P.emit(strict=strict)
    return nc


def prog_l0_attn_p2(strict=True):
    nc = bass.Bass("TRN2", target_bir_lowering=False, debug=False)
    with ExitStack() as es:
        C = Ctx(nc, es)
        build_l0_attn_p2(C)
        C.P.emit(strict=strict)
    return nc





def l0f_consts():
    DIM_HEAD, WINDOW, n = 64, 256, T
    half = np.arange(0, DIM_HEAD, 2, dtype=np.float64)
    inv_freq = 1.0 / 10000.0 ** (half / DIM_HEAD)
    t = np.arange(n, dtype=np.float64)
    freqs = t[:, None] * inv_freq[None, :]
    theta = np.concatenate([freqs, freqs], 1)
    base_scale = (half + 0.4 * DIM_HEAD) / (1.4 * DIM_HEAD)
    power = (t - n // 2) / (WINDOW // 2)
    scale = base_scale[None, :] ** power[:, None]
    scale = np.concatenate([scale, scale], 1)
    cq = (np.cos(theta) * scale).T
    ck = (np.cos(theta) / scale).T
    sgn = np.concatenate([-np.ones(32), np.ones(32)])[:, None]
    sq = (np.sin(theta) * scale).T * sgn
    sk = (np.sin(theta) / scale).T * sgn
    def dup(a):
        return np.ascontiguousarray(np.concatenate([a, a], 0).astype(np.float32))
    tabs = {"rot_cq": dup(cq), "rot_sq": dup(sq), "rot_ck": dup(ck), "rot_sk": dup(sk)}
    sw = np.zeros((128, 128), np.float32)
    for e in range(128):
        base = (e // 64) * 64
        sw[base + (e - base + 32) % 64, e] = 1.0
    tabs["swapP"] = sw
    sel = np.zeros((16, 8 * 128), np.float32)
    for kt in range(8):
        for el in range(128):
            sel[kt * 2 + el // 64, kt * 128 + el] = 1.0
    tabs["selh"] = sel
    m = np.zeros((128, 3 * 384), np.float32)
    for v in range(3):
        for il in range(128):
            for jj in range(384):
                ok = (il <= jj <= il + 256) and (jj >= 256 - v * 128)
                m[il, v * 384 + jj] = 0.0 if ok else -1e30
    tabs["simmask"] = m
    tabs["identity"] = np.eye(128, dtype=np.float32)
    return tabs


L0F_PHASES = 99


def build_l0f_attn(C):
    P, nc = C.P, C.nc
    nct = 8
    TCH = 512
    NCH = T // TCH
    sin = nc.dram_tensor("sres_in", [D, T], F32, kind="ExternalInput").ap()
    sout = nc.dram_tensor("sres_out", [D, T], F32, kind="ExternalOutput").ap()
    wqkv = nc.dram_tensor("l0f_wqkv", [1024, 3072], F32, kind="ExternalInput").ap()
    wo = nc.dram_tensor("l0f_wo", [1024, 1024], F32, kind="ExternalInput").ap()
    qsks = nc.dram_tensor("l0f_qsks", [128, 16], F32, kind="ExternalInput").ap()
    tabs = {}
    for nm, sh in (("rot_cq", [128, T]), ("rot_sq", [128, T]), ("rot_ck", [128, T]),
                   ("rot_sk", [128, T]), ("swapP", [128, 128]),
                   ("selh", [16, 1024]), ("simmask", [128, 1152]),
                   ("identity", [128, 128])):
        tabs[nm] = nc.dram_tensor("l0f_" + nm, sh, F32, kind="ExternalInput").ap()

    wblk = [C.sbuf(f"l0f_wb{i}", [128, 8 * 1024], BF16) for i in range(2)]

    def load_wblock(buf, dram_cols_src, tag):
        P.dma_pool(buf[:].rearrange("p (c n) -> p c n", n=1024),
                   dram_cols_src.rearrange("(c p) n -> p c n", p=128),
                   writes=(buf,), tag=tag)
    qsks_sb = C.sbuf("l0f_qsks", [128, 16], F32)
    P.dma_sp(qsks_sb[:], qsks[:], writes=(qsks_sb,), tag="qsks")
    qs8 = C.sbuf("l0f_qs8", [128, 16], F32)
    P.i_ts(qs8[:, 0:8], qsks_sb[:, 0:8], 8.0, None, ALU.mult,
           reads=(qsks_sb,), writes=(qs8,), tag="qs8")
    P.i_copy(qs8[:, 8:16], qsks_sb[:, 8:16], eng="dve",
             reads=(qsks_sb,), writes=(qs8,), tag="ks1")
    tb = {}
    swap_sb = C.sbuf("swp", [128, 128], BF16)
    P.dma_pool(swap_sb[:], tabs["swapP"][:], writes=(swap_sb,), tag="swp")
    sel_sb = C.sbuf("selh", [16, 1024], BF16)
    P.dma_pool(sel_sb[:], tabs["selh"][:], writes=(sel_sb,), tag="selh")
    mask_sb = C.sbuf("mask", [128, 1152], BF16)
    P.dma_pool(mask_sb[:], tabs["simmask"][:], writes=(mask_sb,), tag="mask")
    id_sb = C.sbuf("idn", [128, 128], BF16)
    P.dma_pool(id_sb[:], tabs["identity"][:], writes=(id_sb,), tag="idn")
    idf_sb = C.sbuf("idnf", [128, 128], F32)
    P.dma_sp(idf_sb[:], tabs["identity"][:], writes=(idf_sb,), tag="idnf")
    hsel = C.sbuf("hsel", [128, 2], F32)
    P.i_memset(hsel[:, 0:2], 0.0, writes=(hsel,), tag="hs0")
    P.i_memset(hsel[0:64, 0:1], 1.0, writes=(hsel,), tag="hs1")
    P.i_memset(hsel[64:128, 1:2], 1.0, writes=(hsel,), tag="hs2")

    qb = [C.sbuf(f"l0f_q{i}", [128, T], BF16) for i in range(nct)]
    kb = [C.sbuf(f"l0f_k{i}", [128, 256 + T], BF16) for i in range(nct)]
    vzero = C.sbuf("l0f_vz", [128, 1024], BF16)
    vt = [vzero, vzero] + [C.sbuf(f"l0f_v{i}", [128, 1024], BF16) for i in range(16)]
    ofm = [C.sbuf(f"l0f_o{i}", [128, T], BF16) for i in range(nct)]
    scr3 = C.sbuf("l0f_scr3", [128, 1536], F32)
    sc = [scr3[:, i * 512:(i + 1) * 512] for i in range(3)]
    scrF = scr3[:]
    SCR = scr3
    attn_sb = C.sbuf("l0f_attn", [128, 1536], BF16)
    attnT = C.sbuf("l0f_attnT", [128, 2 * 384], BF16)
    z4 = C.sbuf("l0f_z4", [128, 12], F32)
    rnq = C.sbuf("l0f_rnq", [128, 16 * 16], F32)
    rnk = C.sbuf("l0f_rnk", [128, 16 * 16], F32)
    rnhT = C.sbuf("l0f_rnhT", [16, 2 * T], BF16)
    for i in range(nct):
        P.i_memset(kb[i][:, 0:256], 0.0, writes=(kb[i],), tag=f"kg{i}")
    P.i_memset(vzero[:], 0.0, writes=(vzero,), tag="vg")

    ps_rot = [C.psum(f"l0f_psr{i}", [128, 512]) for i in range(2)]
    ps4 = C.psum("l0f_ps4", [128, 2048])
    ps_sq = C.psum("l0f_pssq", [128, 512])
    pst = C.psum("l0f_pst", [128, 1024], BF16)

    shb = [C.sbuf(f"l0f_sh{i}", [128, TCH], BF16) for i in range(nct)]

    def load_sh(ch, tag):
        c0 = ch * TCH
        for i in range(nct):
            if i < 4:
                P.dma_pool(shb[i][:], sin[i * 128:(i + 1) * 128, c0:c0 + TCH],
                           writes=(shb[i],), tag=f"{tag}{ch}_{i}")
            elif c0 > 0:
                P.dma_pool(shb[i][:], sin[i * 128:(i + 1) * 128, c0 - 1:c0 - 1 + TCH],
                           writes=(shb[i],), tag=f"{tag}{ch}_{i}")
            else:
                P.i_memset(shb[i][:, 0:1], 0.0, writes=(shb[i],), tag=f"{tag}z{i}")
                P.dma_pool(shb[i][:, 1:TCH], sin[i * 128:(i + 1) * 128, 0:TCH - 1],
                           writes=(shb[i],), tag=f"{tag}{ch}_{i}")

    load_wblock(wblk[0], wqkv[:, 0:1024], "wq_q")
    load_wblock(wblk[1], wqkv[:, 1024:2048], "wq_k")
    for ch in range(NCH):
        c0 = ch * TCH
        load_sh(ch, "shq")
        for ot in range(16):
            wb = wblk[0] if ot < 8 else wblk[1]
            p1 = ps_rot[ot % 2]
            pairs = [(wb[:, ct * 1024 + (ot % 8) * 128: ct * 1024 + (ot % 8) * 128 + 128],
                      shb[ct][:]) for ct in range(nct)]
            mm_group(P, p1[:, 0:TCH], p1, pairs, reads=tuple(shb) + (wb,),
                     tag=f"qk{ch}_{ot}")
            if ot < 8:
                P.i_copy(qb[ot][:, c0:c0 + TCH], p1[:, 0:TCH], eng="dve",
                         reads=(p1,), writes=(qb[ot],), tag=f"qe{ch}_{ot}")
            else:
                P.i_copy(kb[ot - 8][:, 256 + c0:256 + c0 + TCH], p1[:, 0:TCH],
                         eng="dve", reads=(p1,), writes=(kb[ot - 8],),
                         tag=f"ke{ch}_{ot}")
    load_wblock(wblk[1], wqkv[:, 2048:3072], "wq_v")
    for ch in range(NCH):
        c0 = ch * TCH
        load_sh(ch, "shv")
        for sl in range(4):
            for jh in range(2):
                p1 = ps_rot[jh]
                pairs = [(shb[ct][:, sl * 128:(sl + 1) * 128],
                          wblk[1][:, ct * 1024 + jh * 512: ct * 1024 + jh * 512 + 512])
                         for ct in range(nct)]
                mm_group(P, p1[:, 0:512], p1, pairs, reads=tuple(shb) + (wblk[1],),
                         tag=f"vm{ch}_{sl}_{jh}")
                gt = 2 + ch * 4 + sl
                P.i_copy(vt[gt][:, jh * 512:(jh + 1) * 512], p1[:, 0:512], eng="dve",
                         reads=(p1,), writes=(vt[gt],), tag=f"ve{ch}_{sl}_{jh}")
    for ci, nm in enumerate(("rot_cq", "rot_sq", "rot_ck", "rot_sk")):
        tb[nm] = wblk[0][:, ci * 2048:(ci + 1) * 2048]
        P.dma_pool(tb[nm], tabs[nm][:], writes=(wblk[0],), tag=nm)

    def prep(side, buf, goff, ctab, stab, scol_off, rn_sb, rn_off, tabres):
        for sl in range(16):
            for i in range(nct):
                P.i_act(AF.Square, sc[i % 2][:, 0:128],
                        buf[i][:, goff + sl * 128: goff + (sl + 1) * 128],
                        reads=(buf[i],), writes=(SCR,), tag=f"{side}sq{sl}_{i}")
                mm_group(P, ps_sq[:, i * 2:i * 2 + 2], ps_sq,
                         [(sc[i % 2][:, 0:128], hsel[:, 0:2])],
                         reads=(SCR, hsel), tag=f"{side}sm{sl}_{i}")
            P.i_act(AF.Sqrt, rn_sb[:, sl * 16:(sl + 1) * 16], ps_sq[:, 0:16],
                    reads=(ps_sq,), writes=(rn_sb,), tag=f"{side}rs{sl}")
            P.i_ts(rn_sb[:, sl * 16:(sl + 1) * 16], rn_sb[:, sl * 16:(sl + 1) * 16],
                   1e-12, None, ALU.max, reads=(rn_sb,), writes=(rn_sb,),
                   tag=f"{side}rm{sl}")
            P.add("dve", lambda h, o=rn_sb[:, sl * 16:(sl + 1) * 16],
                  i=rn_sb[:, sl * 16:(sl + 1) * 16]: h.reciprocal(o, i),
                  reads=(rn_sb,), writes=(rn_sb,), tag=f"{side}rr{sl}")
            P.add("pe", lambda h, o=ps_sq[0:16, 0:128],
                  l=rn_sb[:, sl * 16:(sl + 1) * 16], r_=idf_sb:
                  h.matmul(o, l, r_[:], start=True, stop=True),
                  reads=(rn_sb, idf_sb), writes=(ps_sq,), tag=f"{side}rt{sl}")
            P.i_copy(rnhT[0:16, rn_off + sl * 128: rn_off + (sl + 1) * 128],
                     ps_sq[0:16, 0:128], eng="act",
                     reads=(ps_sq,), writes=(rnhT,), tag=f"{side}rte{sl}")
        for i in range(nct):
            P.i_ts(buf[i][:, goff:goff + T], buf[i][:, goff:goff + T],
                   qs8[:, scol_off + i:scol_off + i + 1], None, ALU.mult,
                   reads=(buf[i], qs8), writes=(buf[i],), tag=f"{side}qs{i}")
            for ch in range(NCH):
                c0 = ch * TCH
                P.add("pe", lambda h, o=ps_rot[0][:, 0:TCH], l=swap_sb[:],
                      r_=buf[i][:, goff + c0: goff + c0 + TCH]:
                      h.matmul(o, l, r_, start=True, stop=True),
                      reads=(swap_sb, buf[i]), writes=(ps_rot[0],),
                      tag=f"{side}sw{i}_{ch}")
                P.i_tt(ALU.mult, sc[2], buf[i][:, goff + c0: goff + c0 + TCH],
                       ctab[:, c0:c0 + TCH], reads=(buf[i], tabres), writes=(SCR,),
                       tag=f"{side}r1{i}_{ch}")
                P.i_tt(ALU.mult, sc[0], ps_rot[0][:, 0:TCH],
                       stab[:, c0:c0 + TCH], reads=(ps_rot[0], tabres), writes=(SCR,),
                       tag=f"{side}r2{i}_{ch}")
                P.i_tt(ALU.add, buf[i][:, goff + c0: goff + c0 + TCH],
                       sc[2], sc[0],
                       reads=(SCR,), writes=(buf[i],), tag=f"{side}r3{i}_{ch}")
                P.add("pe", lambda h, o=ps_rot[1][:, 0:TCH],
                      l=sel_sb[0:16, i * 128:(i + 1) * 128],
                      r_=rnhT[0:16, rn_off + c0: rn_off + c0 + TCH]:
                      h.matmul(o, l, r_, start=True, stop=True),
                      reads=(sel_sb, rnhT), writes=(ps_rot[1],),
                      tag=f"{side}rb{i}_{ch}")
                P.i_tt(ALU.mult, buf[i][:, goff + c0: goff + c0 + TCH],
                       buf[i][:, goff + c0: goff + c0 + TCH], ps_rot[1][:, 0:TCH],
                       reads=(buf[i], ps_rot[1]), writes=(buf[i],),
                       tag=f"{side}rn{i}_{ch}")

    if L0F_PHASES >= 2:
        prep("q", qb, 0, tb["rot_cq"], tb["rot_sq"], 0, rnq, 0, wblk[0])
        prep("k", kb, 256, tb["rot_ck"], tb["rot_sk"], 8, rnk, T, wblk[0])

    for ib in range(16 if L0F_PHASES >= 3 else 0):
        i0 = ib * 128
        mv = mask_sb[:, min(ib, 2) * 384: min(ib, 2) * 384 + 384]
        for hg in range(4):
            for hi in range(4):
                hh = hg * 4 + hi
                hp, e = hh // 2, hh % 2
                reg = ps4[:, hi * 512: hi * 512 + 384]
                P.add("pe", lambda h, o=reg,
                      l=qb[hp][64 * e:64 * e + 64, i0:i0 + 128],
                      r_=kb[hp][64 * e:64 * e + 64, i0:i0 + 384]:
                      h.matmul(o, l, r_, start=True, stop=False),
                      reads=(qb[hp], kb[hp]), writes=(ps4,), tag=f"qk{ib}_{hh}")
                P.add("pe", lambda h, o=reg, l=id_sb[:], r_=mv:
                      h.matmul(o, l, r_, start=False, stop=True),
                      reads=(id_sb, mask_sb), writes=(ps4,), tag=f"mk{ib}_{hh}")
            v4 = ps4[:].rearrange("p (g x) -> p g x", x=512)[:, :, 0:384]
            P.i_reduce(ALU.max, z4[:, 4:8].unsqueeze(2), v4,
                       reads=(ps4,), writes=(z4,), tag=f"mx{ib}_{hg}")
            m4b = z4[:, 4:8].unsqueeze(2).broadcast_to([128, 4, 384])
            P.i_tt(ALU.subtract, scrF.rearrange("p (g x) -> p g x", x=384), v4,
                   m4b, reads=(ps4, z4), writes=(SCR,), tag=f"sb{ib}_{hg}")
            for hi in range(4):
                P.i_act(AF.Exp, attn_sb[:, hi * 384:(hi + 1) * 384],
                        scrF[:, hi * 384:(hi + 1) * 384],
                        accum_out=z4[:, hi:hi + 1],
                        reads=(SCR,), writes=(attn_sb, z4), tag=f"ex{ib}_{hg}_{hi}")
            P.add("dve", lambda h, o=z4[:, 8:12], i=z4[:, 0:4]: h.reciprocal(o, i),
                  reads=(z4,), writes=(z4,), tag=f"rc{ib}_{hg}")
            rzb = z4[:, 8:12].unsqueeze(2).broadcast_to([128, 4, 384])
            P.i_tt(ALU.mult, attn_sb[:].rearrange("p (g x) -> p g x", x=384),
                   attn_sb[:].rearrange("p (g x) -> p g x", x=384), rzb,
                   reads=(attn_sb, z4), writes=(attn_sb,), tag=f"nm{ib}_{hg}")
            for hi in range(4):
                hh = hg * 4 + hi
                hp, e = hh // 2, hh % 2
                for kk in range(3):
                    P.add("pe", lambda h, o=pst[:, (kk % 4) * 128:(kk % 4) * 128 + 128],
                          i=attn_sb[:, hi * 384 + kk * 128: hi * 384 + (kk + 1) * 128],
                          idn=id_sb: h.transpose(o, i, idn[:]),
                          reads=(attn_sb, id_sb), writes=(pst,), tag=f"tr{ib}_{hh}_{kk}")
                    P.i_copy(attnT[:, (hi % 2) * 384 + kk * 128:(hi % 2) * 384 + (kk + 1) * 128],
                             pst[:, (kk % 4) * 128:(kk % 4) * 128 + 128], eng="dve",
                             reads=(pst,), writes=(attnT,), tag=f"te{ib}_{hh}_{kk}")
                    P.add("pe", lambda h,
                          o=ps_rot[hg % 2][64 * e:64 * e + 64, (hp % 2) * 128:(hp % 2) * 128 + 128],
                          l=vt[ib + kk][:, hh * 64: hh * 64 + 64],
                          r_=attnT[:, (hi % 2) * 384 + kk * 128:(hi % 2) * 384 + (kk + 1) * 128],
                          st=(kk == 0), sp=(kk == 2):
                          h.matmul(o, l, r_, start=st, stop=sp),
                          reads=(vt[ib + kk], attnT), writes=(ps_rot[hg % 2],),
                          tag=f"pv{ib}_{hh}_{kk}")
                if e == 1:
                    P.i_copy(ofm[hp][:, i0:i0 + 128],
                             ps_rot[hg % 2][:, (hp % 2) * 128:(hp % 2) * 128 + 128],
                             eng="dve", reads=(ps_rot[hg % 2],), writes=(ofm[hp],),
                             tag=f"oe{ib}_{hp}")

    if L0F_PHASES < 4:
        for i in range(nct):
            P.i_memset(ofm[i][:], 0.0, writes=(ofm[i],), tag=f"ofz{i}")
    load_wblock(wblk[1], wo[:, :], "wo")
    for ch in range(NCH):
        c0 = ch * TCH
        for ot in range(nct):
            p1 = ps_rot[ot % 2]
            pairs = [(wblk[1][:, ct * 1024 + ot * 128: ct * 1024 + ot * 128 + 128],
                      ofm[ct][:, c0:c0 + TCH]) for ct in range(nct)]
            mm_group(P, p1[:, 0:TCH], p1, pairs, reads=tuple(ofm) + (wblk[1],),
                     tag=f"wom{ch}_{ot}")
            P.dma_sp(sc[2], sin[ot * 128:(ot + 1) * 128, c0:c0 + TCH],
                     writes=(SCR,), tag=f"wrl{ch}_{ot}")
            P.i_tt(ALU.add, sc[2], sc[2], p1[:, 0:TCH],
                   reads=(p1, SCR), writes=(SCR,), tag=f"wra{ch}_{ot}")
            P.dma_sp(sout[ot * 128:(ot + 1) * 128, c0:c0 + TCH], sc[2],
                     reads=(SCR,), writes=(f"so{ot}",), tag=f"sst{ch}_{ot}")
    stores = [op for op in P.ops if op.tag.startswith("sst")]
    P.add("sp", lambda h: None, deps=tuple(stores), tag="final_wait")


def build_l0f_ffn(C):
    P, nc = C.P, C.nc
    nct = 8
    sin = nc.dram_tensor("sres_in", [D, T], F32, kind="ExternalInput").ap()
    sout = nc.dram_tensor("sres_out", [D, T], F32, kind="ExternalOutput").ap()
    w1 = nc.dram_tensor("l0f_w1", [1024, 5460], F32, kind="ExternalInput").ap()
    w2 = nc.dram_tensor("l0f_w2", [2730, 1024], F32, kind="ExternalInput").ap()
    lngb = nc.dram_tensor("l0f_lngb", [128, 16], F32, kind="ExternalInput").ap()
    b1p = nc.dram_tensor("l0f_b1", [128, 44], F32, kind="ExternalInput").ap()
    b2p = nc.dram_tensor("l0f_b2", [128, 8], F32, kind="ExternalInput").ap()
    KB = [128] * 21 + [42]
    K0 = [128 * i for i in range(22)]
    TH = 1024

    lngb_sb = C.sbuf("l0f_lngb", [128, 16], F32)
    P.dma_sp(lngb_sb[:], lngb[:], writes=(lngb_sb,), tag="lngb")
    b1_sb = C.sbuf("l0f_b1s", [128, 44], F32)
    P.dma_sp(b1_sb[:], b1p[:], writes=(b1_sb,), tag="b1")
    b2_sb = C.sbuf("l0f_b2s", [128, 8], F32)
    P.dma_sp(b2_sb[:], b2p[:], writes=(b2_sb,), tag="b2")
    w2_sb = C.sbuf("l0f_w2s", [128, 22 * 1024], BF16)
    P.dma_pool(w2_sb[0:128, :].rearrange("p (c n) -> p c n", n=1024)[:, 0:21, :],
               w2[0:2688, :].rearrange("(c p) n -> p c n", p=128),
               writes=(w2_sb,), tag="w2a")
    P.dma_pool(w2_sb[0:42, 21 * 1024:22 * 1024], w2[2688:2730, :],
               writes=(w2_sb,), tag="w2b")

    xn = [C.sbuf(f"l0f_xn{i}", [128, TH], BF16) for i in range(nct)]
    ff = C.sbuf("l0f_ff", [128, 22 * TH], BF16)
    w1a = [C.sbuf(f"l0f_w1a{i}", [128, 8 * 128], BF16) for i in range(2)]
    w1g = [C.sbuf(f"l0f_w1g{i}", [128, 8 * 128], BF16) for i in range(2)]
    rc = C.tiles("l0f_frc", nct, 512, F32)
    sc = [C.sbuf(f"l0f_fsc{i}", [128, 512], F32) for i in range(3)]
    scb = C.sbuf("l0f_fscb", [128, 512], BF16)
    rowbuf = C.sbuf("l0f_frows", [1, 3 * 512], F32)
    ps = [C.psum(f"l0f_fps{i}", [128, 512]) for i in range(4)]
    ps_s = C.psum("l0f_fpss", [1, 1024])
    lnpool = {"sx": (ps_s, 0), "sq": (ps_s, 512), "nmb": (ps[2], 0), "rsb": (ps[3], 0)}

    for th in range(2):
        t0 = th * TH
        for cc in range(2):
            c0 = t0 + cc * 512
            for i in range(nct):
                if i < 4:
                    P.dma_sp(rc[i][:], sin[i * 128:(i + 1) * 128, c0:c0 + 512],
                             writes=(rc[i],), tag=f"fsl{th}{cc}_{i}")
                elif c0 > 0:
                    P.dma_sp(rc[i][:], sin[i * 128:(i + 1) * 128, c0 - 1:c0 + 511],
                             writes=(rc[i],), tag=f"fsl{th}{cc}_{i}")
                else:
                    P.i_memset(rc[i][:, 0:1], 0.0, writes=(rc[i],), tag=f"fsz{i}")
                    P.dma_sp(rc[i][:, 1:512], sin[i * 128:(i + 1) * 128, 0:511],
                             writes=(rc[i],), tag=f"fsl{th}{cc}_{i}")
            nmb, rsb, nmb_res, rsb_res = ln_stats(C, rc, 0, 512, rowbuf, lnpool,
                                                  tag=f"fln{th}{cc}", sq_scratch=sc)
            for i in range(nct):
                s0 = sc[i % 2]
                P.i_tt(ALU.add, s0[:, 0:512], rc[i][:], nmb,
                       reads=(rc[i], nmb_res), writes=(s0,), tag=f"fla{th}{cc}_{i}")
                P.i_tt(ALU.mult, s0[:, 0:512], s0[:, 0:512], rsb,
                       reads=(s0, rsb_res), writes=(s0,), tag=f"flb{th}{cc}_{i}")
                P.i_ts(xn[i][:, cc * 512:(cc + 1) * 512], s0[:, 0:512],
                       lngb_sb[:, i:i + 1], lngb_sb[:, 8 + i:9 + i],
                       ALU.mult, ALU.add, reads=(s0, lngb_sb), writes=(xn[i],),
                       tag=f"flc{th}{cc}_{i}")
        for k in range(22):
            kb, k0 = KB[k], K0[k]
            wa, wg = w1a[k % 2], w1g[k % 2]
            P.dma_pool(wa[:].rearrange("p (c n) -> p c n", n=128)[:, :, 0:kb],
                       w1[:, k0:k0 + kb].rearrange("(c p) n -> p c n", p=128),
                       writes=(wa,), tag=f"w1a{th}_{k}")
            P.dma_pool(wg[:].rearrange("p (c n) -> p c n", n=128)[:, :, 0:kb],
                       w1[:, 2730 + k0:2730 + k0 + kb].rearrange("(c p) n -> p c n", p=128),
                       writes=(wg,), tag=f"w1g{th}_{k}")
            for cc in range(2):
                pa, pg = ps[0], ps[1]
                pairs_a = [(wa[:, ct * 128: ct * 128 + kb],
                            xn[ct][:, cc * 512:(cc + 1) * 512]) for ct in range(nct)]
                mm_group(P, pa[0:kb, 0:512], pa, pairs_a, reads=tuple(xn) + (wa,),
                         tag=f"fam{th}{cc}_{k}")
                pairs_g = [(wg[:, ct * 128: ct * 128 + kb],
                            xn[ct][:, cc * 512:(cc + 1) * 512]) for ct in range(nct)]
                mm_group(P, pg[0:kb, 0:512], pg, pairs_g, reads=tuple(xn) + (wg,),
                         tag=f"fgm{th}{cc}_{k}")
                P.i_act(AF.Gelu, scb[0:kb, 0:512], pg[0:kb, 0:512],
                        bias=b1_sb[0:kb, 22 + k:23 + k],
                        reads=(pg, b1_sb), writes=(scb,), tag=f"fge{th}{cc}_{k}")
                P.i_ts(sc[2][0:kb, 0:512], pa[0:kb, 0:512], b1_sb[0:kb, k:k + 1],
                       None, ALU.add, reads=(pa, b1_sb), writes=(sc[2],),
                       tag=f"fab{th}{cc}_{k}")
                P.i_tt(ALU.mult, ff[0:kb, k * TH + cc * 512: k * TH + cc * 512 + 512],
                       sc[2][0:kb, 0:512], scb[0:kb, 0:512],
                       reads=(sc[2], scb), writes=((id(ff), k),), tag=f"ffm{th}{cc}_{k}")
        for cc in range(2):
            c0 = t0 + cc * 512
            for ot in range(nct):
                p1 = ps[ot % 2]
                pairs = [(w2_sb[0:KB[k], k * 1024 + ot * 128: k * 1024 + ot * 128 + 128],
                          ff[0:KB[k], k * TH + cc * 512: k * TH + cc * 512 + 512])
                         for k in range(22)]
                mm_group(P, p1[:, 0:512], p1, pairs,
                         reads=tuple((id(ff), k) for k in range(22)) + (w2_sb,),
                         tag=f"w2m{th}{cc}_{ot}")
                P.dma_sp(sc[2][:, 0:512], sin[ot * 128:(ot + 1) * 128, c0:c0 + 512],
                         writes=(sc[2],), tag=f"frl{th}{cc}_{ot}")
                P.i_ts(sc[0][:, 0:512], p1[:, 0:512], b2_sb[:, ot:ot + 1], None,
                       ALU.add, reads=(p1, b2_sb), writes=(sc[0],),
                       tag=f"fb2{th}{cc}_{ot}")
                P.i_tt(ALU.add, sc[2][:, 0:512], sc[2][:, 0:512], sc[0][:, 0:512],
                       reads=(sc[0], sc[2]), writes=(sc[2],), tag=f"fra{th}{cc}_{ot}")
                P.dma_sp(sout[ot * 128:(ot + 1) * 128, c0:c0 + 512], sc[2][:, 0:512],
                         reads=(sc[2],), writes=(f"fso{ot}",), tag=f"fst{th}{cc}_{ot}")
    stores = [op for op in P.ops if op.tag.startswith("fst")]
    P.add("sp", lambda h: None, deps=tuple(stores), tag="final_wait")


def build_ln_block(C, residual=False):
    P, nc = C.P, C.nc
    nct = 8
    xi = nc.dram_tensor("ln_in", [D, T], F32, kind="ExternalInput").ap()
    xo = nc.dram_tensor("ln_out", [D, T], F32, kind="ExternalOutput").ap()
    if residual:
        xr = nc.dram_tensor("ln_res", [D, T], F32, kind="ExternalInput").ap()
    rc = C.tiles("lnb_rc", nct, 512, F32)
    rr = C.tiles("lnb_rr", nct, 512, F32)
    sc = [C.sbuf(f"lnb_sc{i}", [128, 512], F32) for i in range(2)]
    rowbuf = C.sbuf("lnb_rows", [1, 3 * 512], F32)
    ps = [C.psum(f"lnb_ps{i}", [128, 512]) for i in range(2)]
    ps_s = C.psum("lnb_pss", [1, 1024])
    lnpool = {"sx": (ps_s, 0), "sq": (ps_s, 512), "nmb": (ps[0], 0), "rsb": (ps[1], 0)}
    for ch in range(4):
        c0 = ch * 512
        for i in range(nct):
            P.dma_sp(rc[i][:], xi[i * 128:(i + 1) * 128, c0:c0 + 512],
                     writes=(rc[i],), tag=f"ll{ch}_{i}")
            if residual:
                P.dma_sp(rr[i][:], xr[i * 128:(i + 1) * 128, c0:c0 + 512],
                         writes=(rr[i],), tag=f"lr{ch}_{i}")
        nmb, rsb, nmb_res, rsb_res = ln_stats(C, rc, 0, 512, rowbuf, lnpool,
                                              tag=f"lnb{ch}", sq_scratch=sc)
        for i in range(nct):
            s0 = sc[i % 2]
            P.i_tt(ALU.add, s0[:, 0:512], rc[i][:], nmb,
                   reads=(rc[i], nmb_res), writes=(s0,), tag=f"la{ch}_{i}")
            P.i_tt(ALU.mult, s0[:, 0:512], s0[:, 0:512], rsb,
                   reads=(s0, rsb_res), writes=(s0,), tag=f"lb{ch}_{i}")
            if residual:
                P.i_tt(ALU.add, s0[:, 0:512], s0[:, 0:512], rr[i][:],
                       reads=(s0, rr[i]), writes=(s0,), tag=f"lc{ch}_{i}")
            P.dma_sp(xo[i * 128:(i + 1) * 128, c0:c0 + 512], s0[:, 0:512],
                     reads=(s0,), writes=(f"lo{i}",), tag=f"lst{ch}_{i}")
    stores = [op for op in P.ops if op.tag.startswith("lst")]
    P.add("sp", lambda h: None, deps=tuple(stores), tag="final_wait")


def l0f_host_inputs(inputs, layer):
    d = {}
    d["l0f_wqkv"] = np.ascontiguousarray(np.asarray(inputs["L0_ffn_Wqkv"][layer], np.float32))
    d["l0f_wo"] = np.ascontiguousarray(np.asarray(inputs["L0_ffn_Wo"][layer], np.float32))
    qs = np.tile(np.asarray(inputs["L0_ffn_q_scale"][layer], np.float32), 16)
    ks = np.tile(np.asarray(inputs["L0_ffn_k_scale"][layer], np.float32), 16)
    d["l0f_qsks"] = np.ascontiguousarray(
        np.concatenate([colpack(qs), colpack(ks)], axis=1))
    d["l0f_w1"] = np.ascontiguousarray(np.asarray(inputs["L0_ffn_W1"][layer], np.float32))
    d["l0f_w2"] = np.ascontiguousarray(np.asarray(inputs["L0_ffn_W2"][layer], np.float32))
    d["l0f_lngb"] = np.ascontiguousarray(
        np.concatenate([colpack(np.asarray(inputs["L0_ffn_ln_g"][layer], np.float32)),
                        colpack(np.asarray(inputs["L0_ffn_ln_b"][layer], np.float32))],
                       axis=1))
    b1 = np.asarray(inputs["L0_ffn_b1"][layer], np.float32)
    b1a = np.zeros((128, 22), np.float32)
    b1g = np.zeros((128, 22), np.float32)
    for k in range(22):
        kb = 128 if k < 21 else 42
        b1a[0:kb, k] = b1[128 * k:128 * k + kb]
        b1g[0:kb, k] = b1[2730 + 128 * k:2730 + 128 * k + kb]
    d["l0f_b1"] = np.ascontiguousarray(np.concatenate([b1a, b1g], axis=1))
    d["l0f_b2"] = colpack(np.asarray(inputs["L0_ffn_b2"][layer], np.float32))
    return d


def prog_l0f_attn(strict=True):
    nc = bass.Bass("TRN2", target_bir_lowering=False, debug=False)
    with ExitStack() as es:
        C = Ctx(nc, es)
        build_l0f_attn(C)
        C.P.emit(strict=strict)
    return nc


def prog_l0f_ffn(strict=True):
    nc = bass.Bass("TRN2", target_bir_lowering=False, debug=False)
    with ExitStack() as es:
        C = Ctx(nc, es)
        build_l0f_ffn(C)
        C.P.emit(strict=strict)
    return nc


def prog_ln_block(residual=False, strict=True):
    nc = bass.Bass("TRN2", target_bir_lowering=False, debug=False)
    with ExitStack() as es:
        C = Ctx(nc, es)
        build_ln_block(C, residual=residual)
        C.P.emit(strict=strict)
    return nc


def load_residual(C, dram_name="xres_in"):
    nc, P = C.nc, C.P
    xdr = nc.dram_tensor(dram_name, [D, T], F32, kind="ExternalInput").ap()
    xres = C.tiles("xres", 8, T, F32, guard=G)
    for i, t in enumerate(xres):
        P.dve(lambda h, t=t: h.memset(t[:, 0:G], 0.0), writes=(t,), tag=f"xg{i}")
        P.dma_sp(t[:, G:G + T], xdr[i * 128:(i + 1) * 128, :], writes=(t,),
                 tag=f"xload{i}")
    return xres


def store_residual(C, xres, dram_name="xres_out"):
    nc, P = C.nc, C.P
    ydr = nc.dram_tensor(dram_name, [D, T], F32, kind="ExternalOutput").ap()
    stores = []
    for i, t in enumerate(xres):
        stores.append(P.dma_sp(ydr[i * 128:(i + 1) * 128, :], t[:, G:G + T],
                               reads=(t,), tag=f"xstore{i}"))
    C.P.add("sp", lambda h: None, deps=tuple(stores), tag="final_wait")




def colpack(v, nt=8):
    v = np.ascontiguousarray(v, dtype=np.float32)
    return np.ascontiguousarray(v.reshape(nt, 128).T)


def hT(w):
    return np.ascontiguousarray(np.asarray(w, dtype=np.float32).T)




def prog_l1_ffn():
    nc = bass.Bass("TRN2", target_bir_lowering=False, debug=False)
    with ExitStack() as es:
        C = Ctx(nc, es)
        xres = load_residual(C)
        w_sb, mu_sb = load_l1_ffn_weights(C)
        build_l1_ffn(C, xres, w_sb, mu_sb)
        store_residual(C, xres)
        C.P.emit()
    return nc


def l1_ffn_inmaps(inputs, xres_by_core):
    wr = hT(inputs["L1_ffn_W_r"])
    wk = hT(inputs["L1_ffn_W_k"])
    wv = hT(inputs["L1_ffn_W_v"])
    mu = np.concatenate([colpack(inputs["L1_ffn_mu_r"]), colpack(inputs["L1_ffn_mu_k"])], axis=1)
    return [{
        "xres_in": xres_by_core[b],
        "l1f_wrT": wr, "l1f_wkT": wk, "l1f_wvT": wv, "l1f_mu": mu,
    } for b in range(NCORES)]


_PROG_CACHE = {}


def _get_prog(name, builder):
    if name not in _PROG_CACHE:
        _PROG_CACHE[name] = builder()
    return _PROG_CACHE[name]


def kernel(**inputs):
    x0 = np.asarray(inputs["L0_attn_x"], np.float32)
    xres = [np.ascontiguousarray(x0[b].T) for b in range(NCORES)]
    wd_a = l0_attn_host_inputs(inputs)
    consts = {"l0f_" + k: v for k, v in l0f_consts().items()}
    wd_f = [l0f_host_inputs(inputs, i) for i in range(6)]
    wd_l1a = l1_attn_host_inputs(inputs)
    wr = hT(inputs["L1_ffn_W_r"])
    wk = hT(inputs["L1_ffn_W_k"])
    wv = hT(inputs["L1_ffn_W_v"])
    mu = np.concatenate([colpack(inputs["L1_ffn_mu_r"]),
                         colpack(inputs["L1_ffn_mu_k"])], axis=1)
    cores = list(range(NCORES))

    def launch(prog, per_core):
        res = run_bass_kernel_spmd(prog, per_core, core_ids=cores)
        return res.results

    p = _get_prog("l0a1", prog_l0_attn_p1)
    r1 = launch(p, [dict(wd_a, xres_in=xres[b]) for b in cores])
    p = _get_prog("l0a2", prog_l0_attn_p2)
    spill_keys = ("l0a_z", "l0a_xc_f", "l0a_xc_r", "l0a_dt_f", "l0a_dt_r",
                  "l0a_bc_f", "l0a_bc_r")
    r2 = launch(p, [dict(wd_a, xres_in=xres[b],
                         **{k: r1[b][k] for k in spill_keys}) for b in cores])
    x1 = [np.asarray(r2[b]["xres_out"]) for b in cores]

    p = _get_prog("ln", lambda: prog_ln_block(residual=False))
    r = launch(p, [{"ln_in": x1[b]} for b in cores])
    s = [np.asarray(r[b]["ln_out"]) for b in cores]
    pa = _get_prog("l0fa", prog_l0f_attn)
    pf = _get_prog("l0ff", prog_l0f_ffn)
    attn_keys = ("l0f_wqkv", "l0f_wo", "l0f_qsks")
    ffn_keys = ("l0f_w1", "l0f_w2", "l0f_lngb", "l0f_b1", "l0f_b2")
    for i in range(6):
        wa = {k: wd_f[i][k] for k in attn_keys}
        r = launch(pa, [dict(wa, **consts, sres_in=s[b]) for b in cores])
        s = [np.asarray(r[b]["sres_out"]) for b in cores]
        wf = {k: wd_f[i][k] for k in ffn_keys}
        r = launch(pf, [dict(wf, sres_in=s[b]) for b in cores])
        s = [np.asarray(r[b]["sres_out"]) for b in cores]
    p = _get_prog("glue", lambda: prog_ln_block(residual=True))
    r = launch(p, [{"ln_in": s[b], "ln_res": x1[b]} for b in cores])
    x2 = [np.asarray(r[b]["ln_out"]) for b in cores]

    p = _get_prog("l1a", prog_l1_attn)
    r = launch(p, [dict(wd_l1a, xres_in=x2[b]) for b in cores])
    x3 = [np.asarray(r[b]["xres_out"]) for b in cores]

    p = _get_prog("l1f", prog_l1_ffn)
    r = launch(p, [{"xres_in": x3[b], "l1f_wrT": wr, "l1f_wkT": wk,
                    "l1f_wvT": wv, "l1f_mu": mu} for b in cores])
    x4 = [np.asarray(r[b]["xres_out"]) for b in cores]
    return np.ascontiguousarray(np.stack([x4[b].T for b in cores]))


# version 02
# speedup vs baseline: 1.0034x; 1.0034x over previous
import numpy as np
from contextlib import ExitStack

import concourse.bass as bass
import concourse.mybir as mybir
from concourse.bass_utils import run_bass_kernel_spmd

F32 = mybir.dt.float32
BF16 = mybir.dt.bfloat16
AF = mybir.ActivationFunctionType
ALU = mybir.AluOpType
AX = mybir.AxisListType

T = 2048
D = 1024
G = 4
NCORES = 8



class Op:
    __slots__ = ("engine", "dma", "emit_fn", "deps", "needs_inc", "count_after",
                 "idx", "tag", "slot", "rotation")


class Prog:

    def __init__(self, nc):
        self.nc = nc
        self.ops = []
        self._lw = {}
        self._rd = {}

    def add(self, engine, emit_fn, reads=(), writes=(), deps=(), dma=False, tag=""):
        reads = [r if isinstance(r, (str, tuple)) else id(r) for r in reads]
        writes = [r if isinstance(r, (str, tuple)) else id(r) for r in writes]
        dd = set()
        for r in reads:
            w = self._lw.get(r)
            if w is not None:
                dd.add(w)
        for r in writes:
            w = self._lw.get(r)
            if w is not None:
                dd.add(w)
            dd.update(self._rd.get(r, ()))
        dd.update(deps)
        op = Op()
        op.engine = engine
        op.dma = dma
        op.emit_fn = emit_fn
        op.idx = len(self.ops)
        op.tag = tag
        op.deps = dd
        op.needs_inc = False
        op.count_after = None
        self.ops.append(op)
        for r in reads:
            self._rd.setdefault(r, set()).add(op)
        for r in writes:
            self._lw[r] = op
            self._rd[r] = set()
        return op

    def pe(self, emit_fn, **kw):
        return self.add("pe", emit_fn, **kw)

    def dve(self, emit_fn, **kw):
        return self.add("dve", emit_fn, **kw)

    def act(self, emit_fn, **kw):
        return self.add("act", emit_fn, **kw)

    def pool(self, emit_fn, **kw):
        return self.add("pool", emit_fn, **kw)

    def i_tt(self, aluop, out, a, b, eng="dve", **kw):
        return self.add(eng, lambda h, o=out, a=a, b=b, op=aluop:
                        h.tensor_tensor(o, a, b, op), **kw)

    def i_ts(self, out, a, s1, s2, op0, op1=ALU.add, eng="dve", **kw):
        if s2 is None:
            return self.add(eng, lambda h, o=out, a=a, s1=s1, op0=op0:
                            h.tensor_scalar(o, a, s1, None, op0), **kw)
        return self.add(eng, lambda h, o=out, a=a, s1=s1, s2=s2, op0=op0, op1=op1:
                        h.tensor_scalar(o, a, s1, s2, op0, op1), **kw)

    def i_stt(self, out, in0, scalar, in1, op0, op1, eng="dve", **kw):
        return self.add(eng, lambda h, o=out, i0=in0, s=scalar, i1=in1, op0=op0,
                        op1=op1: h.scalar_tensor_tensor(o, i0, s, i1, op0, op1), **kw)

    def i_act(self, func, out, in_, scale=1.0, bias=0.0, accum_out=None, **kw):
        return self.add("act", lambda h, f=func, o=out, i=in_, s=scale, b=bias,
                        ao=accum_out: h.activation(out=o, in_=i, func=f, bias=b,
                                                   scale=s, accum_out=ao), **kw)

    def i_copy(self, out, in_, eng="dve", **kw):
        if eng == "act":
            return self.add(eng, lambda h, o=out, i=in_: h.copy(out=o, in_=i), **kw)
        return self.add(eng, lambda h, o=out, i=in_: h.tensor_copy(o, i), **kw)

    def i_memset(self, ap, val, eng="dve", **kw):
        return self.add(eng, lambda h, a=ap, v=val: h.memset(a, v), **kw)

    def i_scan(self, out, d0, d1, init, op0=ALU.mult, op1=ALU.add, **kw):
        return self.add("dve", lambda h, o=out, d0=d0, d1=d1, ini=init, op0=op0,
                        op1=op1: h.tensor_tensor_scan(o, d0, d1, ini, op0, op1), **kw)

    def i_reduce(self, aluop, out, in_, negate=None, **kw):
        return self.add("dve", lambda h, o=out, i=in_, op=aluop, ng=negate:
                        h.tensor_reduce(o, i, AX.X, op, negate=ng), **kw)

    def dma_sp(self, out_ap, in_ap, **kw):
        return self.add("sp", lambda h: h.dma_start(out=out_ap, in_=in_ap),
                        dma=True, **kw)

    def dma_pool(self, out_ap, in_ap, **kw):
        return self.add("pool", lambda h: h.dma_start(out=out_ap, in_=in_ap),
                        dma=True, **kw)

    def dma_act(self, out_ap, in_ap, **kw):
        return self.add("act", lambda h: h.dma_start(out=out_ap, in_=in_ap),
                        dma=True, **kw)

    def all_ops(self):
        return list(self.ops)

    NDMASLOTS = 8

    def _dom(self, op):
        if not op.dma:
            return op.engine
        return f"dma_{op.engine}_{op.slot}"

    def emit(self, strict=True):
        nc = self.nc
        qidx = {}
        for op in self.ops:
            if op.dma:
                q = op.engine
                i = qidx.get(q, 0)
                qidx[q] = i + 1
                op.slot = i % self.NDMASLOTS
                op.rotation = i // self.NDMASLOTS
        for op in self.ops:
            dom_op = self._dom(op)
            for d in op.deps:
                if strict or self._dom(d) != dom_op or d.dma:
                    d.needs_inc = True
        for op in self.ops:
            if op.dma:
                op.needs_inc = True
        counters = {}
        for op in self.ops:
            if op.needs_inc:
                dom = self._dom(op)
                counters[dom] = counters.get(dom, 0) + (16 if op.dma else 1)
                op.count_after = counters[dom]
        doms = sorted(counters)
        es = ExitStack()
        with es:
            sems = {d: es.enter_context(nc.semaphore(f"s_{d}")) for d in doms}
            block = es.enter_context(nc.Block())

            def make_body(ename):
                myops = [o for o in self.ops if o.engine == ename]

                def body(h):
                    waited = {}
                    for op in myops:
                        dom_op = self._dom(op)
                        if op.dma and op.rotation > 0:
                            prev = 16 * op.rotation
                            if waited.get(dom_op, 0) < prev:
                                h.wait_ge(sems[dom_op], prev)
                                waited[dom_op] = prev
                        for dep in sorted(op.deps, key=lambda o: o.idx):
                            ddom = self._dom(dep)
                            if ddom == dom_op and not dep.dma and not strict:
                                continue
                            if dep.count_after is None:
                                continue
                            if waited.get(ddom, 0) >= dep.count_after:
                                continue
                            h.wait_ge(sems[ddom], dep.count_after)
                            waited[ddom] = dep.count_after
                        inst = op.emit_fn(h)
                        if op.needs_inc:
                            assert inst is not None, f"op {op.tag} needs inc but emitted none"
                            inst.then_inc(sems[dom_op], 16 if op.dma else 1)
                return body

            for ename, dec in (("pe", block.tensor), ("dve", block.vector),
                               ("act", block.scalar), ("pool", block.gpsimd),
                               ("sp", block.sync)):
                if any(o.engine == ename for o in self.ops):
                    dec(make_body(ename))




class Ctx:
    def __init__(self, nc, es):
        self.nc = nc
        self.es = es
        self.P = Prog(nc)
        self._const_ones_row = None
        self._uid = 0
        self.debug = False

    def tap(self, name, ap, reads):
        if not self.debug:
            return
        dr = self.nc.dram_tensor("tap_" + name, list(ap.shape), ap.dtype,
                                 kind="ExternalOutput").ap()
        self.P.dma_sp(dr, ap, reads=tuple(reads), tag="tap_" + name)

    def uid(self):
        self._uid += 1
        return self._uid

    def sbuf(self, name, shape, dtype):
        return self.es.enter_context(self.nc.sbuf_tensor(f"{name}_{self.uid()}", shape, dtype))

    def psum(self, name, shape, dtype=F32):
        return self.es.enter_context(self.nc.psum_tensor(f"{name}_{self.uid()}", shape, dtype))

    def tiles(self, name, n, cols, dtype, guard=0):
        return [self.sbuf(f"{name}{i}", [128, guard + cols], dtype) for i in range(n)]

    def ones_row(self):
        if self._const_ones_row is None:
            t = self.sbuf("ones_row", [1, 128], F32)
            self.P.dve(lambda h: h.memset(t[:], 1.0), writes=(t,), tag="ones_row")
            self._const_ones_row = t
        return self._const_ones_row

    def ones_col(self):
        if not hasattr(self, "_ones_col") or self._ones_col is None:
            t = self.sbuf("ones_col", [128, 1], F32)
            self.P.dve(lambda h: h.memset(t[:], 1.0), writes=(t,), tag="ones_col")
            self._ones_col = t
        return self._ones_col


def mm_group(P, ps_ap, ps_res, pairs, reads, tag="", accumulate=False, close=True):
    def emit(h):
        inst = None
        n = len(pairs)
        for i, (l, r) in enumerate(pairs):
            inst = h.matmul(ps_ap, l, r, start=(i == 0 and not accumulate),
                            stop=(i == n - 1 and close))
        return inst
    return P.add("pe", emit, reads=tuple(reads), writes=(ps_res,), tag=tag)


def ln_stats(C, src_tiles, src_c0, F, rowbuf, ps_pool, eps=1e-5, tag="ln",
             sq_scratch=None, n_feat=D):
    P, nc = C.P, C.nc
    nct = len(src_tiles)
    ones_col = C.ones_col()
    ones_row = C.ones_row()
    sxT, sxc = ps_pool["sx"]
    sqT, sqc = ps_pool["sq"]
    nmbT, nmbc = ps_pool["nmb"]
    rsbT, rsbc = ps_pool["rsb"]
    del ps_pool
    sx_ap = sxT[0:1, sxc:sxc + F]
    sq_ap = sqT[0:1, sqc:sqc + F]
    pairs = [(ones_col[:, 0:1], t[:, src_c0:src_c0 + F]) for t in src_tiles]
    mm_group(P, sx_ap, sxT, pairs, reads=tuple(src_tiles) + (ones_col,),
             tag=f"{tag}_sx")
    for i, t in enumerate(src_tiles):
        s = sq_scratch[i % len(sq_scratch)]
        P.i_act(AF.Square, s[:, 0:F], t[:, src_c0:src_c0 + F],
                reads=(t,), writes=(s,), tag=f"{tag}_sq{i}")
        mm_group(P, sq_ap, sqT, [(ones_col[:, 0:1], s[:, 0:F])],
                 reads=(s, ones_col), tag=f"{tag}_sqm{i}", accumulate=(i > 0),
                 close=(i == len(src_tiles) - 1))
    r0 = rowbuf[0:1, 0:F]
    r1 = rowbuf[0:1, F:2 * F]
    r2 = rowbuf[0:1, 2 * F:3 * F]
    inv = 1.0 / n_feat
    P.i_ts(r0, sx_ap, -inv, None, ALU.mult,
           reads=(sxT,), writes=(rowbuf,), tag=f"{tag}_nm")
    P.i_ts(r1, sq_ap, inv, None, ALU.mult,
           reads=(sqT, rowbuf), writes=(rowbuf,), tag=f"{tag}_ex2")
    P.i_stt(r2, r0, 1.0, r0, ALU.mult, ALU.mult,
            reads=(rowbuf,), writes=(rowbuf,), tag=f"{tag}_msq")
    P.i_stt(r2, r2, -1.0, r1, ALU.mult, ALU.add,
            reads=(rowbuf,), writes=(rowbuf,), tag=f"{tag}_var")
    P.i_ts(r2, r2, 1.0, eps, ALU.mult, ALU.add,
           reads=(rowbuf,), writes=(rowbuf,), tag=f"{tag}_vare")
    P.i_act(AF.Ln, r2, r2, reads=(rowbuf,), writes=(rowbuf,), tag=f"{tag}_ln")
    P.i_act(AF.Exp, r2, r2, scale=-0.5,
            reads=(rowbuf,), writes=(rowbuf,), tag=f"{tag}_exp")
    mm_group(P, nmbT[:, nmbc:nmbc + F], nmbT, [(ones_row[0:1, :], r0)],
             reads=(rowbuf, ones_row), tag=f"{tag}_nmb")
    mm_group(P, rsbT[:, rsbc:rsbc + F], rsbT, [(ones_row[0:1, :], r2)],
             reads=(rowbuf, ones_row), tag=f"{tag}_rsb")
    return nmbT[:, nmbc:nmbc + F], rsbT[:, rsbc:rsbc + F], nmbT, rsbT


def make_ln_pool(C, F=512):
    return {
        "sx": (C.psum("ln_sx", [1, F]), 0),
        "sq": (C.psum("ln_sq", [1, F]), 0),
        "nmb": (C.psum("ln_nmb", [128, F]), 0),
        "rsb": (C.psum("ln_rsb", [128, F]), 0),
    }




def build_l1_ffn(C, xres, w_sb, mu_sb, CH=512):
    P = C.P
    nct = 8
    om_sb = C.sbuf("l1f_om", [128, 16], F32)
    P.i_ts(om_sb[:], mu_sb[:], -1.0, 1.0, ALU.mult, ALU.add,
           reads=(mu_sb,), writes=(om_sb,), tag="l1f_om")

    ln = C.tiles("l1f_ln", nct, T, BF16, guard=G)
    for t in ln:
        P.i_memset(t[:, 0:G], 0.0, writes=(t,), tag="l1f_lng")
    lnpool = make_ln_pool(C, CH)
    rowbuf = C.sbuf("l1f_rows", [1, 3 * CH], F32)
    sq_scr = [C.sbuf(f"l1f_sqs{i}", [128, CH], F32) for i in range(2)]
    for c0 in range(0, T, CH):
        nmb, rsb, nmb_res, rsb_res = ln_stats(C, xres, G + c0, CH, rowbuf, lnpool,
                                              tag=f"l1f_ln{c0}", sq_scratch=sq_scr)
        for i in range(nct):
            tsc = sq_scr[i % 2]
            P.i_tt(ALU.add, tsc[:, 0:CH], xres[i][:, G + c0:G + c0 + CH], nmb,
                   reads=(xres[i], nmb_res), writes=(tsc,), tag=f"l1f_lna{c0}_{i}")
            P.i_tt(ALU.mult, ln[i][:, G + c0:G + c0 + CH], tsc[:, 0:CH], rsb,
                   reads=(tsc, rsb_res), writes=(ln[i],), tag=f"l1f_lnb{c0}_{i}")

    xrk = [C.sbuf(f"l1f_xrk{i}", [128, CH], BF16) for i in range(nct)]
    rr = [C.sbuf(f"l1f_r{i}", [128, CH], BF16) for i in range(nct)]
    kk = [C.sbuf(f"l1f_k{i}", [128, CH], BF16) for i in range(nct)]
    outc = [C.sbuf(f"l1f_out{i}", [128, CH], F32) for i in range(nct)]
    mmps = [C.psum(f"l1f_ps{i}", [128, CH]) for i in range(3)]
    for ci, c0 in enumerate(range(0, T, CH)):
        for half, wkey in ((0, "r"), (8, "k")):
            for i in range(nct):
                tsc = sq_scr[i % 2]
                P.i_ts(tsc[:, 0:CH], ln[i][:, G + c0:G + c0 + CH],
                       mu_sb[:, half + i:half + i + 1], None, ALU.mult,
                       reads=(ln[i], mu_sb), writes=(tsc,), tag=f"l1f_xa{wkey}{c0}_{i}")
                P.i_stt(xrk[i][:], ln[i][:, G + c0 - 1:G + c0 - 1 + CH],
                        om_sb[:, half + i:half + i + 1], tsc[:, 0:CH],
                        ALU.mult, ALU.add,
                        reads=(ln[i], om_sb, tsc), writes=(xrk[i],),
                        tag=f"l1f_xb{wkey}{c0}_{i}")
            for ot in range(nct):
                ps = mmps[ot % 3]
                pairs = [(w_sb[wkey][:, ct * 1024 + ot * 128: ct * 1024 + ot * 128 + 128],
                          xrk[ct][:]) for ct in range(nct)]
                mm_group(P, ps[:], ps, pairs, reads=tuple(xrk) + (w_sb[wkey],),
                         tag=f"l1f_{wkey}mm{c0}_{ot}")
                if wkey == "r":
                    P.i_act(AF.Sigmoid, rr[ot][:], ps[:],
                            reads=(ps,), writes=(rr[ot],), tag=f"l1f_rsig{c0}_{ot}")
                else:
                    P.i_ts(kk[ot][:], ps[:], 0.0, None, ALU.max,
                           reads=(ps,), writes=(kk[ot],), tag=f"l1f_krelu{c0}_{ot}")
                    P.i_stt(kk[ot][:], kk[ot][:], 1.0, ps[:], ALU.mult, ALU.mult,
                            reads=(ps, kk[ot]), writes=(kk[ot],), tag=f"l1f_ksq{c0}_{ot}")
        for ot in range(nct):
            ps = mmps[ot % 3]
            pairs = [(w_sb["v"][:, ct * 1024 + ot * 128: ct * 1024 + ot * 128 + 128],
                      kk[ct][:]) for ct in range(nct)]
            mm_group(P, ps[:], ps, pairs, reads=tuple(kk) + (w_sb["v"],),
                     tag=f"l1f_vmm{c0}_{ot}")
            P.i_tt(ALU.mult, outc[ot][:], rr[ot][:], ps[:],
                   reads=(ps, rr[ot]), writes=(outc[ot],), tag=f"l1f_gate{c0}_{ot}")
        nmb, rsb, nmb_res, rsb_res = ln_stats(C, outc, 0, CH, rowbuf, lnpool,
                                              tag=f"l1f_gl{c0}", sq_scratch=sq_scr)
        for i in range(nct):
            tsc = sq_scr[i % 2]
            P.i_tt(ALU.add, tsc[:, 0:CH], outc[i][:], nmb,
                   reads=(outc[i], nmb_res), writes=(tsc,), tag=f"l1f_ga{c0}_{i}")
            P.i_tt(ALU.mult, tsc[:, 0:CH], tsc[:, 0:CH], rsb,
                   reads=(tsc, rsb_res), writes=(tsc,), tag=f"l1f_gb{c0}_{i}")
            P.i_tt(ALU.add, xres[i][:, G + c0:G + c0 + CH],
                   xres[i][:, G + c0:G + c0 + CH], tsc[:, 0:CH],
                   reads=(tsc, xres[i]), writes=(xres[i],), tag=f"l1f_gc{c0}_{i}")


def load_l1_ffn_weights(C, names=("l1f_wrT", "l1f_wkT", "l1f_wvT")):
    nc, P = C.nc, C.P
    w_sb = {}
    for key, nm in zip(("r", "k", "v"), names):
        dr = nc.dram_tensor(nm, [1024, 1024], F32, kind="ExternalInput").ap()
        sb = C.sbuf(nm + "_sb", [128, 8 * 1024], BF16)
        src = dr.rearrange("(ct p) d -> p ct d", p=128)
        dst = sb[:].rearrange("p (ct d) -> p ct d", d=1024)
        P.dma_pool(dst, src, writes=(sb,), tag=f"load_{nm}")
        w_sb[key] = sb
    mu_dr = nc.dram_tensor("l1f_mu", [128, 16], F32, kind="ExternalInput").ap()
    mu_sb = C.sbuf("l1f_mu_sb", [128, 16], F32)
    P.dma_sp(mu_sb[:], mu_dr[:], writes=(mu_sb,), tag="load_l1f_mu")
    return w_sb, mu_sb






L1A_TC = 256
L1A_SUB = 16
L1A_CLAMP = 80.0


def l1a_consts():
    SUB, NEAR = 16, 4
    t1 = np.arange(128)[:, None]
    t2 = np.arange(128)[None, :]
    same_sub = (t1 // SUB) == (t2 // SUB)
    amask = (same_sub & ((t2 - t1) >= (NEAR + 1))).astype(np.float32)
    cm = np.zeros((128, 80), np.float32)
    for d in range(0, 5):
        for h in range(16):
            cm[:, d * 16 + h] = ((np.arange(128) % SUB) >= d).astype(np.float32)
    return amask, cm


def build_l1_attn(C, xres_dram_in, xres_dram_out):
    P, nc = C.P, C.nc
    TC, SUB = L1A_TC, L1A_SUB
    NCH = T // TC
    NBLK = TC // 128
    NSUB = TC // SUB
    nct = 8

    wTs = {}
    for nm in ("r", "k", "v", "g", "o"):
        wTs[nm] = nc.dram_tensor(f"l1a_{nm}WT", [1024, 1024], F32, kind="ExternalInput").ap()
    la_dr = nc.dram_tensor("l1a_laT", [1024, 320], F32, kind="ExternalInput").ap()
    lb_dr = nc.dram_tensor("l1a_lbT", [64, 5120], F32, kind="ExternalInput").ap()
    ll_dr = nc.dram_tensor("l1a_ll", [128, 40], F32, kind="ExternalInput").ap()
    xw_dr = nc.dram_tensor("l1a_xw", [128, 8], F32, kind="ExternalInput").ap()
    u_dr = nc.dram_tensor("l1a_u", [128, 8], F32, kind="ExternalInput").ap()
    st0_dr = nc.dram_tensor("l1a_st0", [8, 128, 64], F32, kind="ExternalInput").ap()
    lnwb_dr = nc.dram_tensor("l1a_lnwb", [2, 1024], F32, kind="ExternalInput").ap()
    amask_dr = nc.dram_tensor("l1a_amask", [128, 128], F32, kind="ExternalInput").ap()
    ident_dr = nc.dram_tensor("l1a_ident", [128, 128], F32, kind="ExternalInput").ap()
    cmask_dr = nc.dram_tensor("l1a_cmask", [128, 80], F32, kind="ExternalInput").ap()

    la_sb = C.sbuf("l1a_la", [128, 8 * 320], BF16)
    P.dma_pool(la_sb[:].rearrange("p (c n) -> p c n", n=320),
               la_dr.rearrange("(c p) n -> p c n", p=128), writes=(la_sb,), tag="la")
    lb_sb = C.sbuf("l1a_lb", [64, 5120], BF16)
    P.dma_pool(lb_sb[:], lb_dr[:], writes=(lb_sb,), tag="lb")
    ll_sb = C.sbuf("l1a_ll", [128, 40], F32)
    P.dma_sp(ll_sb[:], ll_dr[:], writes=(ll_sb,), tag="ll")
    xw_sb = C.sbuf("l1a_xw", [128, 8], F32)
    P.dma_sp(xw_sb[:], xw_dr[:], writes=(xw_sb,), tag="xw")
    u_sb = C.sbuf("l1a_u", [128, 8], F32)
    P.dma_sp(u_sb[:], u_dr[:], writes=(u_sb,), tag="u")
    amask_sb = C.sbuf("l1a_am", [128, 128], F32)
    P.dma_sp(amask_sb[:], amask_dr[:], writes=(amask_sb,), tag="am")
    cmask_sb = C.sbuf("l1a_cm", [128, 80], F32)
    P.dma_sp(cmask_sb[:], cmask_dr[:], writes=(cmask_sb,), tag="cm")
    lnw_rep = C.sbuf("l1a_lnw", [128, 1024], F32)
    P.dma_sp(lnw_rep[:], lnwb_dr[0].unsqueeze(0).partition_broadcast(128),
             writes=(lnw_rep,), tag="lnw")
    lnb_rep = C.sbuf("l1a_lnb", [128, 1024], F32)
    P.dma_sp(lnb_rep[:], lnwb_dr[1].unsqueeze(0).partition_broadcast(128),
             writes=(lnb_rep,), tag="lnb")
    st = [C.sbuf(f"l1a_st{i}", [128, 64], F32) for i in range(8)]
    for hp in range(8):
        P.dma_sp(st[hp][:], st0_dr[hp], writes=(st[hp],), tag=f"st0_{hp}")
    ones_bf = C.sbuf("l1a_ones", [128, TC], BF16)
    P.i_memset(ones_bf[:], 1.0, writes=(ones_bf,), tag="onesbf")
    ident_sb = C.sbuf("l1a_id", [128, 128], BF16)
    P.dma_pool(ident_sb[:], ident_dr[:], writes=(ident_sb,), tag="ident")

    rc = C.tiles("l1a_rc", nct, TC, F32)
    xl = C.tiles("l1a_xl", nct, TC, BF16, guard=1)
    dx = C.tiles("l1a_dx", nct, TC, BF16)
    lerp = C.tiles("l1a_lerp", nct, TC, BF16)
    zz = C.tiles("l1a_z", nct, TC, BF16)
    ww = C.tiles("l1a_ww", nct, TC, BF16)
    r_fm = C.tiles("l1a_r", nct, TC, BF16)
    k_fm = C.tiles("l1a_k", nct, TC, BF16, guard=4)
    rate = C.tiles("l1a_rate", nct, TC, F32)
    Rg = C.tiles("l1a_Rg", nct, TC + 1, F32)
    wf = C.tiles("l1a_wf", nct, TC, BF16, guard=4)
    rq = C.tiles("l1a_rq", nct, TC, F32)
    kA = C.tiles("l1a_kA", nct, TC, F32)
    k2 = C.tiles("l1a_k2", nct, TC, BF16)
    ruk = C.tiles("l1a_ruk", nct, TC, BF16)
    Vt = [C.sbuf(f"l1a_Vt{i}", [128, 1024], BF16) for i in range(NBLK)]
    Gt = [C.sbuf(f"l1a_Gt{i}", [128, 1024], BF16) for i in range(NBLK)]
    k2t = [C.sbuf(f"l1a_k2t{i}", [128, 1024], BF16) for i in range(NBLK)]
    Vsh = [C.sbuf(f"l1a_Vsh{i}", [128, 1024], BF16) for i in range(NBLK)]
    k2s = [C.sbuf(f"l1a_k2s{i}", [16, 1024], BF16) for i in range(2)]
    Vts = [C.sbuf(f"l1a_Vts{i}", [16, 1024], BF16) for i in range(2)]
    ystage = [C.sbuf(f"l1a_yst{i}", [16, 1024], F32) for i in range(2)]
    Vtail = C.sbuf("l1a_vtail", [4, 1024], BF16)
    y_tm = [C.sbuf(f"l1a_ytm{i}", [128, 1024], F32) for i in range(NBLK)]
    ysc = [C.sbuf(f"l1a_ysc{i}", [128, 1024], F32) for i in range(2)]
    o_t = [C.sbuf(f"l1a_ot{i}", [128, 1024], BF16) for i in range(NBLK)]
    o_fm = C.tiles("l1a_ofm", nct, TC, BF16)
    outc = C.tiles("l1a_outc", nct, TC, F32)
    th = [C.sbuf(f"l1a_th{i}", [64, TC], BF16) for i in range(2)]
    scr = [C.sbuf(f"l1a_scr{i}", [128, TC], F32) for i in range(2)]
    scrb = [C.sbuf(f"l1a_scrb{i}", [128, TC], BF16) for i in range(2)]
    Eends = [C.sbuf(f"l1a_Ee{i}", [128, NSUB], F32) for i in range(8)]
    c_sb = [C.sbuf(f"l1a_c{i}", [128, 80], F32) for i in range(2)]
    gsum = C.sbuf("l1a_gsum", [128, 16], F32)
    gsq = C.sbuf("l1a_gsq", [128, 16], F32)
    gstat = C.sbuf("l1a_gstat", [128, 16], F32)
    ysq = C.sbuf("l1a_ysq", [128, 1024], F32)
    wbuf = C.sbuf("l1a_wbuf", [128, 8192], BF16)
    rowbuf = C.sbuf("l1a_rows", [1, 3 * TC], F32)
    zcol = C.sbuf("l1a_zc", [128, 1], F32)
    P.i_memset(zcol[:], 0.0, writes=(zcol,), tag="zcol")
    ztile = C.sbuf("l1a_zt", [128, 1024], BF16)
    P.i_memset(ztile[:], 0.0, writes=(ztile,), tag="ztile")

    ps = [C.psum(f"l1a_ps{i}", [128, 512]) for i in range(5)]
    pst = C.psum("l1a_pst", [128, 1024], BF16)
    ps16 = C.psum("l1a_ps16", [16, 1024])
    lnpool = {"sx": (ps16, 0), "sq": (ps16, 512), "nmb": (ps[3], 0), "rsb": (ps[4], 0)}

    for i in range(nct):
        P.i_memset(xl[i][:, 0:1], 0.0, writes=(xl[i],), tag=f"xlg{i}")
        P.i_memset(k_fm[i][:, 0:4], 0.0, writes=(k_fm[i],), tag=f"kg{i}")
        P.i_memset(wf[i][:, 0:4], 0.0, writes=(wf[i],), tag=f"wfg{i}")
    P.i_memset(Vtail[:], 0.0, writes=(Vtail,), tag="vtail0")

    def load_w(nm, tag):
        src = wTs[nm].rearrange("(ct p) d -> p ct d", p=128)
        dst = wbuf[:].rearrange("p (ct d) -> p ct d", d=1024)
        return P.dma_pool(dst, src, writes=(wbuf,), tag=tag)

    def stats_pool_ln(F):
        return {"sx": ps16, "sq": ps16, "nmb": ps[3], "rsb": ps[4]}

    for ch in range(NCH):
        c0 = ch * TC
        if ch > 0:
            for i in range(nct):
                P.i_copy(xl[i][:, 0:1], xl[i][:, TC:TC + 1], eng="dve",
                         reads=(xl[i],), writes=(xl[i],), tag=f"xcar{ch}_{i}")
        for i in range(nct):
            P.dma_sp(rc[i][:], xres_dram_in[i * 128:(i + 1) * 128, c0:c0 + TC],
                     writes=(rc[i],), tag=f"rcl{ch}_{i}")
        nmb, rsb, nmb_res, rsb_res = ln_stats(C, rc, 0, TC, rowbuf, lnpool,
                                              tag=f"l1aln{ch}", sq_scratch=scr)
        for i in range(nct):
            P.i_tt(ALU.add, scr[i % 2][:, 0:TC], rc[i][:], nmb,
                   reads=(rc[i], nmb_res), writes=(scr[i % 2],), tag=f"lna{ch}_{i}")
            P.i_tt(ALU.mult, xl[i][:, 1:TC + 1], scr[i % 2][:, 0:TC], rsb,
                   reads=(scr[i % 2], rsb_res), writes=(xl[i],), tag=f"lnb{ch}_{i}")
        for i in range(nct):
            P.i_tt(ALU.subtract, dx[i][:], xl[i][:, 0:TC], xl[i][:, 1:TC + 1],
                   reads=(xl[i],), writes=(dx[i],), tag=f"dx{ch}_{i}")
            P.i_stt(lerp[i][:], dx[i][:], xw_sb[:, i:i + 1], xl[i][:, 1:TC + 1],
                    ALU.mult, ALU.add, reads=(dx[i], xw_sb, xl[i]),
                    writes=(lerp[i],), tag=f"lerp{ch}_{i}")

        def lora(pidx, src_tiles, dst_tiles, tagp, rate_out=False):
            pairs = [(la_sb[:, ct * 320 + pidx * 64: ct * 320 + pidx * 64 + 64],
                      src_tiles[ct][:]) for ct in range(nct)]
            mm_group(P, ps[4][0:64, 0:TC], ps[4], pairs,
                     reads=tuple(src_tiles) + (la_sb,), tag=f"{tagp}m1")
            tb = th[pidx % 2]
            P.i_act(AF.Tanh, tb[:, 0:TC], ps[4][0:64, 0:TC],
                    reads=(ps[4],), writes=(tb,), tag=f"{tagp}tanh")
            for ot in range(nct):
                p2 = ps[ot % 2]
                mm_group(P, p2[:, 0:TC], p2,
                         [(lb_sb[0:64, pidx * 1024 + ot * 128: pidx * 1024 + ot * 128 + 128],
                           tb[:, 0:TC])],
                         reads=(tb, lb_sb), tag=f"{tagp}m2_{ot}")
                if rate_out:
                    P.i_act(AF.Exp, dst_tiles[ot][:], p2[:, 0:TC],
                            bias=ll_sb[:, pidx * 8 + ot: pidx * 8 + ot + 1],
                            reads=(p2, ll_sb), writes=(dst_tiles[ot],),
                            tag=f"{tagp}rate{ot}")
                else:
                    P.i_stt(scr[ot % 2][:, 0:TC], p2[:, 0:TC],
                            ll_sb[:, pidx * 8 + ot: pidx * 8 + ot + 1], dx[ot][:],
                            ALU.add, ALU.mult,
                            reads=(p2, ll_sb, dx[ot]), writes=(scr[ot % 2],),
                            tag=f"{tagp}sc{ot}")
                    P.i_tt(ALU.add, dst_tiles[ot][:], xl[ot][:, 1:TC + 1],
                           scr[ot % 2][:, 0:TC],
                           reads=(xl[ot], scr[ot % 2]), writes=(dst_tiles[ot],),
                           tag=f"{tagp}z{ot}")

        lora(0, lerp, zz, f"lr{ch}_")
        ldw = load_w("r", f"wr{ch}")
        for ot in range(nct):
            p2 = ps[ot % 2]
            pairs = [(wbuf[:, ct * 1024 + ot * 128: ct * 1024 + ot * 128 + 128],
                      zz[ct][:]) for ct in range(nct)]
            mm_group(P, p2[:, 0:TC], p2, pairs, reads=tuple(zz) + (wbuf,),
                     tag=f"rmm{ch}_{ot}")
            P.i_copy(r_fm[ot][:], p2[:, 0:TC], eng="dve",
                     reads=(p2,), writes=(r_fm[ot],), tag=f"rev{ch}_{ot}")
        lora(1, lerp, zz, f"lk{ch}_")
        load_w("k", f"wk{ch}")
        for ot in range(nct):
            p2 = ps[ot % 2]
            pairs = [(wbuf[:, ct * 1024 + ot * 128: ct * 1024 + ot * 128 + 128],
                      zz[ct][:]) for ct in range(nct)]
            mm_group(P, p2[:, 0:TC], p2, pairs, reads=tuple(zz) + (wbuf,),
                     tag=f"kmm{ch}_{ot}")
            P.i_copy(k_fm[ot][:, 4:4 + TC], p2[:, 0:TC], eng="dve",
                     reads=(p2,), writes=(k_fm[ot],), tag=f"kev{ch}_{ot}")
        lora(2, lerp, zz, f"lv{ch}_")
        load_w("v", f"wv{ch}")
        for bt in range(NBLK):
            for jh in range(2):
                p2 = ps[(bt * 2 + jh) % 2]
                pairs = [(zz[ct][:, bt * 128:(bt + 1) * 128],
                          wbuf[:, ct * 1024 + jh * 512: ct * 1024 + jh * 512 + 512])
                         for ct in range(nct)]
                mm_group(P, p2[:, 0:512], p2, pairs, reads=tuple(zz) + (wbuf,),
                         tag=f"vmm{ch}_{bt}_{jh}")
                P.i_copy(Vt[bt][:, jh * 512:(jh + 1) * 512], p2[:, 0:512], eng="dve",
                         reads=(p2,), writes=(Vt[bt],), tag=f"vev{ch}_{bt}_{jh}")
        lora(3, lerp, zz, f"lg{ch}_")
        load_w("g", f"wg{ch}")
        for bt in range(NBLK):
            for jh in range(2):
                p2 = ps[(bt * 2 + jh) % 2]
                pairs = [(zz[ct][:, bt * 128:(bt + 1) * 128],
                          wbuf[:, ct * 1024 + jh * 512: ct * 1024 + jh * 512 + 512])
                         for ct in range(nct)]
                mm_group(P, p2[:, 0:512], p2, pairs, reads=tuple(zz) + (wbuf,),
                         tag=f"gmm{ch}_{bt}_{jh}")
                P.i_copy(Gt[bt][:, jh * 512:(jh + 1) * 512], p2[:, 0:512], eng="dve",
                         reads=(p2,), writes=(Gt[bt],), tag=f"gev{ch}_{bt}_{jh}")
        lora(4, lerp, ww, f"ld1{ch}_")
        lora(4, ww, rate, f"ld2{ch}_", rate_out=True)

        for i in range(nct):
            P.i_memset(Rg[i][:, 0:1], 0.0, writes=(Rg[i],), tag=f"rg0{ch}_{i}")
            P.i_scan(Rg[i][:, 1:TC + 1], ones_bf[:, 0:TC], rate[i][:], 0.0,
                     reads=(ones_bf, rate[i]), writes=(Rg[i],), tag=f"rgs{ch}_{i}")
            P.i_act(AF.Exp, wf[i][:, 4:4 + TC], rate[i][:], scale=-1.0,
                    reads=(rate[i],), writes=(wf[i],), tag=f"wf{ch}_{i}")
            Bv = Rg[i][:, 0:TC:SUB].unsqueeze(2).broadcast_to([128, NSUB, SUB])
            Ev = Rg[i][:, SUB:TC + 1:SUB].unsqueeze(2).broadcast_to([128, NSUB, SUB])
            s3 = scr[i % 2][:, 0:TC].rearrange("p (s c) -> p s c", c=SUB)
            P.i_tt(ALU.subtract, s3, Bv,
                   Rg[i][:, 0:TC].rearrange("p (s c) -> p s c", c=SUB),
                   reads=(Rg[i],), writes=(scr[i % 2],), tag=f"rqa{ch}_{i}")
            P.i_act(AF.Exp, scr[i % 2][:, 0:TC], scr[i % 2][:, 0:TC],
                    reads=(scr[i % 2],), writes=(scr[i % 2],), tag=f"rqe{ch}_{i}")
            P.i_tt(ALU.mult, rq[i][:], r_fm[i][:], scr[i % 2][:, 0:TC],
                   reads=(r_fm[i], scr[i % 2]), writes=(rq[i],), tag=f"rqm{ch}_{i}")
            P.i_tt(ALU.subtract, s3, Rg[i][:, 1:TC + 1].rearrange("p (s c) -> p s c", c=SUB),
                   Bv, reads=(Rg[i],), writes=(scr[i % 2],), tag=f"kaa{ch}_{i}")
            P.i_ts(scr[i % 2][:, 0:TC], scr[i % 2][:, 0:TC], L1A_CLAMP, None, ALU.min,
                   reads=(scr[i % 2],), writes=(scr[i % 2],), tag=f"kac{ch}_{i}")
            P.i_act(AF.Exp, scr[i % 2][:, 0:TC], scr[i % 2][:, 0:TC],
                    reads=(scr[i % 2],), writes=(scr[i % 2],), tag=f"kae{ch}_{i}")
            P.i_tt(ALU.mult, kA[i][:], k_fm[i][:, 4:4 + TC], scr[i % 2][:, 0:TC],
                   reads=(k_fm[i], scr[i % 2]), writes=(kA[i],), tag=f"kam{ch}_{i}")
            P.i_tt(ALU.subtract, s3, Rg[i][:, 1:TC + 1].rearrange("p (s c) -> p s c", c=SUB),
                   Ev, reads=(Rg[i],), writes=(scr[i % 2],), tag=f"k2a{ch}_{i}")
            P.i_act(AF.Exp, scr[i % 2][:, 0:TC], scr[i % 2][:, 0:TC],
                    reads=(scr[i % 2],), writes=(scr[i % 2],), tag=f"k2e{ch}_{i}")
            P.i_tt(ALU.mult, k2[i][:], k_fm[i][:, 4:4 + TC], scr[i % 2][:, 0:TC],
                   reads=(k_fm[i], scr[i % 2]), writes=(k2[i],), tag=f"k2m{ch}_{i}")
            P.i_tt(ALU.subtract, Eends[i][:, 0:NSUB], Rg[i][:, 0:TC:SUB],
                   Rg[i][:, SUB:TC + 1:SUB], reads=(Rg[i],), writes=(Eends[i],),
                   tag=f"ee{ch}_{i}")
            P.i_act(AF.Exp, Eends[i][:, 0:NSUB], Eends[i][:, 0:NSUB],
                    reads=(Eends[i],), writes=(Eends[i],), tag=f"eee{ch}_{i}")
            P.i_tt(ALU.mult, scrb[i % 2][:], r_fm[i][:], k_fm[i][:, 4:4 + TC],
                   reads=(r_fm[i], k_fm[i]), writes=(scrb[i % 2],), tag=f"ruk{ch}_{i}")
            P.i_ts(ruk[i][:], scrb[i % 2][:], u_sb[:, i:i + 1], None, ALU.mult,
                   reads=(scrb[i % 2], u_sb), writes=(ruk[i],), tag=f"ruku{ch}_{i}")

        for bt in range(NBLK):
            for it in range(nct):
                sl = pst[:, (it % 4) * 128:(it % 4) * 128 + 128]
                P.add("pe", lambda h, o=sl, i=k2[it][:, bt * 128:(bt + 1) * 128],
                      idn=ident_sb: h.transpose(o, i, idn[:]),
                      reads=(k2[it], ident_sb), writes=(pst,), tag=f"tr{ch}_{bt}_{it}")
                P.i_copy(k2t[bt][:, it * 128:(it + 1) * 128], sl, eng="dve",
                         reads=(pst,), writes=(k2t[bt],), tag=f"trev{ch}_{bt}_{it}")


        for d in range(0, 5):
            for hp in range(nct):
                if d == 0:
                    Pd = ruk[hp]
                    Pres = (ruk[hp],)
                else:
                    sb2 = scrb[0]
                    if d == 1:
                        P.i_tt(ALU.mult, sb2[:, 0:TC], r_fm[hp][:],
                               k_fm[hp][:, 4 - d:4 - d + TC],
                               reads=(r_fm[hp], k_fm[hp]), writes=(sb2,),
                               tag=f"P1_{ch}_{hp}")
                    else:
                        P.i_tt(ALU.mult, sb2[:, 0:TC], wf[hp][:, 3:3 + TC],
                               wf[hp][:, 2:2 + TC],
                               reads=(wf[hp],), writes=(sb2,), tag=f"Ga{d}_{ch}_{hp}")
                        if d == 4:
                            P.i_tt(ALU.mult, sb2[:, 0:TC], sb2[:, 0:TC],
                                   wf[hp][:, 1:1 + TC],
                                   reads=(sb2, wf[hp]), writes=(sb2,),
                                   tag=f"Gb{d}_{ch}_{hp}")
                        if d == 2:
                            P.i_copy(sb2[:, 0:TC], wf[hp][:, 3:3 + TC], eng="dve",
                                     reads=(wf[hp],), writes=(sb2,),
                                     tag=f"Gc{d}_{ch}_{hp}")
                        P.i_tt(ALU.mult, sb2[:, 0:TC], sb2[:, 0:TC], r_fm[hp][:],
                               reads=(sb2, r_fm[hp]), writes=(sb2,),
                               tag=f"Pr{d}_{ch}_{hp}")
                        P.i_tt(ALU.mult, sb2[:, 0:TC], sb2[:, 0:TC],
                               k_fm[hp][:, 4 - d:4 - d + TC],
                               reads=(sb2, k_fm[hp]), writes=(sb2,),
                               tag=f"Pk{d}_{ch}_{hp}")
                    Pd = sb2
                    Pres = (sb2,)
                for e in range(2):
                    hh = hp * 2 + e
                    for bt in range(NBLK):
                        src = Pd[64 * e:64 * e + 64, bt * 128:(bt + 1) * 128] if d == 0 \
                            else Pd[64 * e:64 * e + 64, bt * 128:(bt + 1) * 128]
                        P.add("pe", lambda h, o=ps[4][:, bt * 80 + d * 16 + hh: bt * 80 + d * 16 + hh + 1],
                              l=src, r_=ones_bf[64 * e:64 * e + 64, 0:1]:
                              h.matmul(o, l, r_, start=True, stop=True),
                              reads=Pres + (ones_bf,), writes=(ps[4],),
                              tag=f"cmm{ch}_{bt}_{d}_{hh}")
        for bt in range(NBLK):
            P.i_tt(ALU.mult, c_sb[bt][:], ps[4][:, bt * 80:bt * 80 + 80], cmask_sb[:],
                   reads=(ps[4], cmask_sb), writes=(c_sb[bt],), tag=f"cev{ch}_{bt}")

        for bt in range(NBLK):
            for s in range(8):
                sg = bt * 8 + s
                ks_b = k2s[sg % 2]
                vs_b = Vts[sg % 2]
                ys_b = ystage[sg % 2]
                P.add("sp", lambda h, o=ks_b[0:16, :], i=k2t[bt][s * 16:s * 16 + 16, :]:
                      h.dma_start(out=o, in_=i), dma=True,
                      reads=(k2t[bt],), writes=(ks_b,), tag=f"k2sl{ch}_{sg}")
                P.add("sp", lambda h, o=vs_b[0:16, :], i=Vt[bt][s * 16:s * 16 + 16, :]:
                      h.dma_start(out=o, in_=i), dma=True,
                      reads=(Vt[bt],), writes=(vs_b,), tag=f"vtsl{ch}_{sg}")
                for hh in range(16):
                    hp, e = hh // 2, hh % 2
                    P.add("pe", lambda h, o=ps16[0:16, hh * 64:hh * 64 + 64],
                          l=rq[hp][64 * e:64 * e + 64, sg * 16:sg * 16 + 16],
                          r_=st[hp][64 * e:64 * e + 64, :]:
                          h.matmul(o, l, r_, start=True, stop=True),
                          reads=(rq[hp], st[hp]), writes=(ps16,),
                          tag=f"ys{ch}_{sg}_{hh}")
                P.i_copy(ys_b[0:16, :], ps16[0:16, :], eng="act",
                         reads=(ps16,), writes=(ys_b,), tag=f"y16e{ch}_{sg}")
                P.add("sp", lambda h, o=y_tm[bt][s * 16:s * 16 + 16, :], i=ys_b[0:16, :]:
                      h.dma_start(out=o, in_=i), dma=True,
                      reads=(ys_b,), writes=((id(y_tm[bt]), "r", s),),
                      tag=f"ysw{ch}_{sg}")
                for hh in range(16):
                    hp, e = hh // 2, hh % 2
                    P.add("pe", lambda h, o=ps[2][64 * e:64 * e + 64, (hh // 2) * 64:(hh // 2) * 64 + 64],
                          l=ks_b[0:16, hh * 64:hh * 64 + 64],
                          r_=vs_b[0:16, hh * 64:hh * 64 + 64]:
                          h.matmul(o, l, r_, start=True, stop=True),
                          reads=(ks_b, vs_b), writes=(ps[2],),
                          tag=f"kv{ch}_{sg}_{hh}")
                for hp in range(nct):
                    P.i_stt(st[hp][:], st[hp][:], Eends[hp][:, sg:sg + 1],
                            ps[2][:, hp * 64:(hp + 1) * 64], ALU.mult, ALU.add,
                            reads=(Eends[hp], ps[2], st[hp]), writes=(st[hp],),
                            tag=f"stu{ch}_{sg}_{hp}")
        for bt in range(NBLK):
            csb = c_sb[bt]
            ykeys = tuple((id(y_tm[bt]), "r", s) for s in range(8))
            for hh in range(16):
                hp, e = hh // 2, hh % 2
                pA = ps[e]
                P.add("pe", lambda h, o=pA[:, 0:128],
                      l=kA[hp][64 * e:64 * e + 64, bt * 128:(bt + 1) * 128],
                      r_=rq[hp][64 * e:64 * e + 64, bt * 128:(bt + 1) * 128]:
                      h.matmul(o, l, r_, start=True, stop=True),
                      reads=(kA[hp], rq[hp]), writes=(pA,), tag=f"Amm{ch}_{bt}_{hh}")
                Asb = scrb[hh % 2]
                P.i_tt(ALU.mult, Asb[:, 0:128], pA[:, 0:128], amask_sb[:],
                       reads=(pA, amask_sb), writes=(Asb,), tag=f"Amk{ch}_{bt}_{hh}")
                pY = ps[3 + hh // 8] if False else (ps[3] if hh < 8 else ps[4])
                P.add("pe", lambda h, o=pY[:, (hh % 8) * 64:(hh % 8) * 64 + 64],
                      l=Asb[:, 0:128], r_=Vt[bt][:, hh * 64:hh * 64 + 64]:
                      h.matmul(o, l, r_, start=True, stop=True),
                      reads=(Asb, Vt[bt]), writes=(pY,), tag=f"Ymm{ch}_{bt}_{hh}")
            P.i_tt(ALU.add, y_tm[bt][:, 0:512], y_tm[bt][:, 0:512], ps[3][:, 0:512],
                   reads=(ps[3],) + ykeys, writes=(y_tm[bt],), tag=f"yf0{ch}_{bt}")
            P.i_tt(ALU.add, y_tm[bt][:, 512:1024], y_tm[bt][:, 512:1024], ps[4][:, 0:512],
                   reads=(ps[4],) + ykeys, writes=(y_tm[bt],), tag=f"yf1{ch}_{bt}")
            for d in range(0, 5):
                if d == 0:
                    vsrc = Vt[bt]
                    vres = (Vt[bt],)
                else:
                    if bt == 0:
                        P.add("sp", lambda h, o=Vsh[bt][0:d, :], i=Vtail[4 - d:4, :]:
                              h.dma_start(out=o, in_=i), dma=True,
                              reads=(Vtail,), writes=(Vsh[bt],),
                              tag=f"vshh{ch}_{bt}_{d}")
                    else:
                        P.add("sp", lambda h, o=Vsh[bt][0:d, :],
                              i=Vt[bt - 1][128 - d:128, :]:
                              h.dma_start(out=o, in_=i), dma=True,
                              reads=(Vt[bt - 1],), writes=(Vsh[bt],),
                              tag=f"vshh{ch}_{bt}_{d}")
                    P.add("sp", lambda h, o=Vsh[bt][d:128, :], i=Vt[bt][0:128 - d, :]:
                          h.dma_start(out=o, in_=i), dma=True,
                          reads=(Vt[bt],), writes=(Vsh[bt],),
                          tag=f"vshb{ch}_{bt}_{d}")
                    vsrc = Vsh[bt]
                    vres = (Vsh[bt],)
                cview = csb[:, d * 16:d * 16 + 16].unsqueeze(2).broadcast_to([128, 16, 64])
                P.i_tt(ALU.mult, ysc[d % 2][:].rearrange("p (s j) -> p s j", j=64),
                       vsrc[:].rearrange("p (s j) -> p s j", j=64), cview,
                       reads=vres + (csb,), writes=(ysc[d % 2],), tag=f"yn{ch}_{bt}_{d}")
                P.i_tt(ALU.add, y_tm[bt][:], y_tm[bt][:], ysc[d % 2][:],
                       reads=(ysc[d % 2],) + ykeys, writes=(y_tm[bt],),
                       tag=f"yna{ch}_{bt}_{d}")
        P.add("sp", lambda h, o=Vtail[0:4, :], i=Vt[NBLK - 1][124:128, :]:
              h.dma_start(out=o, in_=i), dma=True,
              reads=(Vt[NBLK - 1],), writes=(Vtail,), tag=f"vtailu{ch}")

        for bt in range(NBLK):
            y3 = y_tm[bt][:].rearrange("p (s j) -> p s j", j=64)
            P.i_reduce(ALU.add, gsum[:, 0:16], y3, reads=(y_tm[bt],), writes=(gsum,),
                       tag=f"gn1{ch}_{bt}")
            P.i_act(AF.Square, ysq[:], y_tm[bt][:], reads=(y_tm[bt],), writes=(ysq,),
                    tag=f"gn2{ch}_{bt}")
            P.i_reduce(ALU.add, gsq[:, 0:16], ysq[:].rearrange("p (s j) -> p s j", j=64),
                       reads=(ysq,), writes=(gsq,), tag=f"gn3{ch}_{bt}")
            P.i_ts(gsum[:], gsum[:], -1.0 / 64, None, ALU.mult,
                   reads=(gsum,), writes=(gsum,), tag=f"gn4{ch}_{bt}")
            P.i_ts(gsq[:], gsq[:], 1.0 / 64, None, ALU.mult,
                   reads=(gsq,), writes=(gsq,), tag=f"gn5{ch}_{bt}")
            P.i_stt(gstat[:], gsum[:], 1.0, gsum[:], ALU.mult, ALU.mult,
                    reads=(gsum,), writes=(gstat,), tag=f"gn6{ch}_{bt}")
            P.i_stt(gstat[:], gstat[:], -1.0, gsq[:], ALU.mult, ALU.add,
                    reads=(gstat, gsq), writes=(gstat,), tag=f"gn7{ch}_{bt}")
            P.i_ts(gstat[:], gstat[:], 1.0, 1e-5, ALU.mult, ALU.add,
                   reads=(gstat,), writes=(gstat,), tag=f"gn8{ch}_{bt}")
            P.i_act(AF.Ln, gstat[:], gstat[:], reads=(gstat,), writes=(gstat,),
                    tag=f"gn9{ch}_{bt}")
            P.i_act(AF.Exp, gstat[:], gstat[:], scale=-0.5, reads=(gstat,),
                    writes=(gstat,), tag=f"gn10{ch}_{bt}")
            nmv = gsum[:].unsqueeze(2).broadcast_to([128, 16, 64])
            rsv = gstat[:].unsqueeze(2).broadcast_to([128, 16, 64])
            P.i_tt(ALU.add, ysq[:].rearrange("p (s j) -> p s j", j=64), y3, nmv,
                   reads=(y_tm[bt], gsum), writes=(ysq,), tag=f"gna{ch}_{bt}")
            P.i_tt(ALU.mult, ysq[:].rearrange("p (s j) -> p s j", j=64),
                   ysq[:].rearrange("p (s j) -> p s j", j=64), rsv,
                   reads=(ysq, gstat), writes=(ysq,), tag=f"gnb{ch}_{bt}")
            P.i_tt(ALU.mult, ysq[:], ysq[:], lnw_rep[:],
                   reads=(ysq, lnw_rep), writes=(ysq,), tag=f"gnc{ch}_{bt}")
            P.i_tt(ALU.add, ysq[:], ysq[:], lnb_rep[:],
                   reads=(ysq, lnb_rep), writes=(ysq,), tag=f"gnd{ch}_{bt}")
            P.i_act(AF.Silu, o_t[bt][:], Gt[bt][:], reads=(Gt[bt],), writes=(o_t[bt],),
                    tag=f"sig{ch}_{bt}")
            P.i_tt(ALU.mult, o_t[bt][:], o_t[bt][:], ysq[:],
                   reads=(o_t[bt], ysq), writes=(o_t[bt],), tag=f"og{ch}_{bt}")
        for bt in range(NBLK):
            for it in range(nct):
                sl = pst[:, (it % 4) * 128:(it % 4) * 128 + 128]
                P.add("pe", lambda h, o=sl, i=o_t[bt][:, it * 128:(it + 1) * 128],
                      idn=ident_sb: h.transpose(o, i, idn[:]),
                      reads=(o_t[bt], ident_sb), writes=(pst,), tag=f"otr{ch}_{bt}_{it}")
                P.i_copy(o_fm[it][:, bt * 128:(bt + 1) * 128], sl, eng="dve",
                         reads=(pst,), writes=(o_fm[it],), tag=f"otrev{ch}_{bt}_{it}")
        load_w("o", f"wo{ch}")
        for ot in range(nct):
            p2 = ps[ot % 2]
            pairs = [(wbuf[:, ct * 1024 + ot * 128: ct * 1024 + ot * 128 + 128],
                      o_fm[ct][:]) for ct in range(nct)]
            mm_group(P, p2[:, 0:TC], p2, pairs, reads=tuple(o_fm) + (wbuf,),
                     tag=f"omm{ch}_{ot}")
            P.i_copy(outc[ot][:], p2[:, 0:TC], eng="dve",
                     reads=(p2,), writes=(outc[ot],), tag=f"oev{ch}_{ot}")
        nmb, rsb, nmb_res, rsb_res = ln_stats(C, outc, 0, TC, rowbuf, lnpool,
                                              tag=f"l1agl{ch}", sq_scratch=scr)
        for i in range(nct):
            P.i_tt(ALU.add, scr[i % 2][:, 0:TC], outc[i][:], nmb,
                   reads=(outc[i], nmb_res), writes=(scr[i % 2],), tag=f"ga{ch}_{i}")
            P.i_tt(ALU.mult, scr[i % 2][:, 0:TC], scr[i % 2][:, 0:TC], rsb,
                   reads=(scr[i % 2], rsb_res), writes=(scr[i % 2],), tag=f"gb{ch}_{i}")
            P.i_tt(ALU.add, rc[i][:], rc[i][:], scr[i % 2][:, 0:TC],
                   reads=(scr[i % 2], rc[i]), writes=(rc[i],), tag=f"gc{ch}_{i}")
            P.dma_sp(xres_dram_out[i * 128:(i + 1) * 128, c0:c0 + TC], rc[i][:],
                     reads=(rc[i],), writes=(f"xout{i}",), tag=f"xst{ch}_{i}")


def l1_attn_host_inputs(inputs):
    d = {}
    for nm, key in (("r", "L1_attn_rW"), ("k", "L1_attn_kW"), ("v", "L1_attn_vW"),
                    ("g", "L1_attn_gW"), ("o", "L1_attn_oW")):
        d[f"l1a_{nm}WT"] = hT(inputs[key])
    la = np.concatenate([hT(inputs[f"L1_attn_{p}_la"]) for p in "rkvgd"], axis=1)
    d["l1a_laT"] = np.ascontiguousarray(la)
    lb = np.concatenate([hT(inputs[f"L1_attn_{p}_lb"]) for p in "rkvgd"], axis=1)
    d["l1a_lbT"] = np.ascontiguousarray(lb)
    ll = np.concatenate([colpack(inputs[f"L1_attn_{p}_ll"]) for p in "rkvgd"], axis=1)
    d["l1a_ll"] = np.ascontiguousarray(ll)
    d["l1a_xw"] = colpack(inputs["L1_attn_xw"])
    d["l1a_u"] = colpack(inputs["L1_attn_u"])
    st0 = np.asarray(inputs["L1_attn_init_state"], np.float32)
    d["l1a_st0"] = np.ascontiguousarray(st0.reshape(8, 128, 64))
    lnw = np.tile(np.asarray(inputs["L1_attn_ln_w"], np.float32), 16)
    lnb = np.tile(np.asarray(inputs["L1_attn_ln_b"], np.float32), 16)
    d["l1a_lnwb"] = np.ascontiguousarray(np.stack([lnw, lnb]))
    am, cm = l1a_consts()
    d["l1a_amask"] = am
    d["l1a_cmask"] = cm
    d["l1a_ident"] = np.eye(128, dtype=np.float32)
    return d


def prog_l1_attn(strict=True):
    nc = bass.Bass("TRN2", target_bir_lowering=False, debug=False)
    with ExitStack() as es:
        C = Ctx(nc, es)
        xin = nc.dram_tensor("xres_in", [D, T], F32, kind="ExternalInput").ap()
        xout = nc.dram_tensor("xres_out", [D, T], F32, kind="ExternalOutput").ap()
        build_l1_attn(C, xin, xout)
        stores = [op for op in C.P.ops if op.tag.startswith("xst")]
        C.P.add("sp", lambda h: None, deps=tuple(stores), tag="final_wait")
        C.P.emit(strict=strict)
    return nc





def build_l0_attn_p1(C):
    P, nc = C.P, C.nc
    nct = 8
    TCH = 512
    NCH = T // TCH
    xin = nc.dram_tensor("xres_in", [D, T], F32, kind="ExternalInput").ap()
    ipw = nc.dram_tensor("l0a_inprojT", [1024, 4096], F32, kind="ExternalInput").ap()
    nwb = nc.dram_tensor("l0a_nwb", [128, 16], F32, kind="ExternalInput").ap()
    cwp = nc.dram_tensor("l0a_convw", [128, 16 * 10], F32, kind="ExternalInput").ap()
    xpjf = nc.dram_tensor("l0a_xprojTf", [2048, 96], F32, kind="ExternalInput").ap()
    xpjr = nc.dram_tensor("l0a_xprojTr", [2048, 96], F32, kind="ExternalInput").ap()
    dpjf = nc.dram_tensor("l0a_dtprojTf", [64, 2048], F32, kind="ExternalInput").ap()
    dpjr = nc.dram_tensor("l0a_dtprojTr", [64, 2048], F32, kind="ExternalInput").ap()
    dtbp = nc.dram_tensor("l0a_dtb", [128, 32], F32, kind="ExternalInput").ap()
    z_sp = nc.dram_tensor("l0a_z", [2048, T], BF16, kind="ExternalOutput").ap()
    xc_sp = {dd: nc.dram_tensor(f"l0a_xc_{dd}", [2048, T], BF16, kind="ExternalOutput").ap()
             for dd in "fr"}
    dt_sp = {dd: nc.dram_tensor(f"l0a_dt_{dd}", [2048, T], F32, kind="ExternalOutput").ap()
             for dd in "fr"}
    bc_sp = {dd: nc.dram_tensor(f"l0a_bc_{dd}", [32, T], F32, kind="ExternalOutput").ap()
             for dd in "fr"}

    w_in = C.sbuf("l0a_wi", [128, 8 * 4096], BF16)
    P.dma_pool(w_in[:].rearrange("p (c n) -> p c n", n=4096),
               ipw.rearrange("(c p) n -> p c n", p=128), writes=(w_in,), tag="wi")
    w_xp = {}
    w_dp = {}
    for dd, xp, dp in (("f", xpjf, dpjf), ("r", xpjr, dpjr)):
        w_xp[dd] = C.sbuf(f"l0a_wxp{dd}", [128, 16 * 96], BF16)
        P.dma_pool(w_xp[dd][:].rearrange("p (c n) -> p c n", n=96),
                   xp.rearrange("(c p) n -> p c n", p=128), writes=(w_xp[dd],),
                   tag=f"wxp{dd}")
        w_dp[dd] = C.sbuf(f"l0a_wdp{dd}", [64, 2048], BF16)
        P.dma_pool(w_dp[dd][:], dp[:], writes=(w_dp[dd],), tag=f"wdp{dd}")
    nwb_sb = C.sbuf("l0a_nwb", [128, 16], F32)
    P.dma_sp(nwb_sb[:], nwb[:], writes=(nwb_sb,), tag="nwb")
    cw_sb = C.sbuf("l0a_cw", [128, 160], F32)
    P.dma_sp(cw_sb[:], cwp[:], writes=(cw_sb,), tag="cw")
    dtb_sb = C.sbuf("l0a_dtb", [128, 32], F32)
    P.dma_sp(dtb_sb[:], dtbp[:], writes=(dtb_sb,), tag="dtb")

    rc = C.tiles("l0a_rc", nct, TCH, F32)
    hh = C.tiles("l0a_h", nct, TCH, BF16)
    xp_bufs = [[C.sbuf(f"l0a_xp{b}_{i}", [128, 3 + TCH + 3], BF16) for i in range(16)]
               for b in range(2)]
    xcb = [C.sbuf(f"l0a_xcb{i}", [128, TCH], BF16) for i in range(16)]
    dtr_sb = C.sbuf("l0a_dtr", [64, TCH], BF16)
    zb_sb = C.sbuf("l0a_zb", [128, TCH], BF16)
    bc_sb = C.sbuf("l0a_bcsb", [32, TCH], F32)
    scr = [C.sbuf(f"l0a_scr{i}", [128, TCH], F32) for i in range(3)]
    rowbuf = C.sbuf("l0a_rows", [1, 3 * TCH], F32)
    ps = [C.psum(f"l0a_ps{i}", [128, 512]) for i in range(4)]
    ps_s = C.psum("l0a_pss", [1, 1024])
    lnpool = {"sx": (ps_s, 0), "sq": (ps_s, 512), "nmb": (ps[2], 0), "rsb": (ps[3], 0)}

    for b in range(2):
        for i in range(16):
            P.i_memset(xp_bufs[b][i][:, 0:3], 0.0, writes=(xp_bufs[b][i],), tag="xpg")
            P.i_memset(xp_bufs[b][i][:, 3 + TCH:], 0.0, writes=(xp_bufs[b][i],), tag="xpg2")

    def conv_dir(ch, b, dd):
        c0 = ch * TCH
        for i in range(16):
            xpt = xp_bufs[b][i]
            wcol = lambda j: cw_sb[:, i * 10 + (0 if dd == "f" else 5) + j:
                                   i * 10 + (0 if dd == "f" else 5) + j + 1]
            s0 = scr[i % 2]
            if dd == "f":
                P.i_ts(s0[:, 0:TCH], xpt[:, 0:TCH], wcol(0), None, ALU.mult,
                       reads=(xpt, cw_sb), writes=(s0,), tag=f"cv0{dd}{ch}_{i}")
                for k in (1, 2, 3):
                    P.i_stt(s0[:, 0:TCH], xpt[:, k:k + TCH], wcol(k), s0[:, 0:TCH],
                            ALU.mult, ALU.add, reads=(xpt, cw_sb, s0), writes=(s0,),
                            tag=f"cv{k}{dd}{ch}_{i}")
            else:
                P.i_ts(s0[:, 0:TCH], xpt[:, 6:6 + TCH], wcol(0), None, ALU.mult,
                       reads=(xpt, cw_sb), writes=(s0,), tag=f"cv0{dd}{ch}_{i}")
                for k in (1, 2, 3):
                    P.i_stt(s0[:, 0:TCH], xpt[:, 6 - k:6 - k + TCH], wcol(k),
                            s0[:, 0:TCH], ALU.mult, ALU.add,
                            reads=(xpt, cw_sb, s0), writes=(s0,),
                            tag=f"cv{k}{dd}{ch}_{i}")
            bcol = cw_sb[:, i * 10 + (4 if dd == "f" else 9):
                         i * 10 + (4 if dd == "f" else 9) + 1]
            P.i_act(AF.Silu, xcb[i][:], s0[:, 0:TCH], bias=bcol,
                    reads=(s0, cw_sb), writes=(xcb[i],), tag=f"cvs{dd}{ch}_{i}")
            P.dma_sp(xc_sp[dd][i * 128:(i + 1) * 128, c0:c0 + TCH], xcb[i][:],
                     reads=(xcb[i],), writes=(f"xcsp{dd}{i}",), tag=f"xcw{dd}{ch}_{i}")
        pairs = [(w_xp[dd][:, ct * 96:(ct + 1) * 96], xcb[ct][:]) for ct in range(16)]
        mm_group(P, ps[0][0:96, 0:TCH], ps[0], pairs,
                 reads=tuple(xcb) + (w_xp[dd],), tag=f"xdbl{dd}{ch}")
        P.i_copy(dtr_sb[:, 0:TCH], ps[0][0:64, 0:TCH], eng="dve",
                 reads=(ps[0],), writes=(dtr_sb,), tag=f"dtr{dd}{ch}")
        P.i_copy(bc_sb[:, 0:TCH], ps[0][64:96, 0:TCH], eng="act",
                 reads=(ps[0],), writes=(bc_sb,), tag=f"bc{dd}{ch}")
        P.dma_sp(bc_sp[dd][:, c0:c0 + TCH], bc_sb[:, 0:TCH],
                 reads=(bc_sb,), writes=(f"bcsp{dd}",), tag=f"bcw{dd}{ch}")
        for ot in range(16):
            p1 = ps[1]
            mm_group(P, p1[:, 0:TCH], p1,
                     [(w_dp[dd][:, ot * 128:(ot + 1) * 128], dtr_sb[:, 0:TCH])],
                     reads=(dtr_sb, w_dp[dd]), tag=f"dtm{dd}{ch}_{ot}")
            dcol = dtb_sb[:, (0 if dd == "f" else 16) + ot:
                          (0 if dd == "f" else 16) + ot + 1]
            s1, s2 = scr[0], scr[1]
            P.i_ts(s1[:, 0:TCH], p1[:, 0:TCH], dcol, None, ALU.add,
                   reads=(p1, dtb_sb), writes=(s1,), tag=f"dta{dd}{ch}_{ot}")
            P.i_act(AF.Abs, s2[:, 0:TCH], s1[:, 0:TCH],
                    reads=(s1,), writes=(s2,), tag=f"dtb{dd}{ch}_{ot}")
            P.i_act(AF.Exp, s2[:, 0:TCH], s2[:, 0:TCH], scale=-1.0,
                    reads=(s2,), writes=(s2,), tag=f"dtc{dd}{ch}_{ot}")
            P.i_ts(s2[:, 0:TCH], s2[:, 0:TCH], 1.0, 1.0, ALU.mult, ALU.add,
                   reads=(s2,), writes=(s2,), tag=f"dtd{dd}{ch}_{ot}")
            P.i_act(AF.Ln, s2[:, 0:TCH], s2[:, 0:TCH],
                    reads=(s2,), writes=(s2,), tag=f"dte{dd}{ch}_{ot}")
            P.i_ts(s1[:, 0:TCH], s1[:, 0:TCH], 0.0, None, ALU.max,
                   reads=(s1,), writes=(s1,), tag=f"dtf{dd}{ch}_{ot}")
            P.i_tt(ALU.add, scr[2][:, 0:TCH], s1[:, 0:TCH], s2[:, 0:TCH],
                   reads=(s1, s2), writes=(scr[2],), tag=f"dtg{dd}{ch}_{ot}")
            P.dma_sp(dt_sp[dd][ot * 128:(ot + 1) * 128, c0:c0 + TCH],
                     scr[2][:, 0:TCH], reads=(scr[2],), writes=(f"dtsp{dd}{ot}",),
                     tag=f"dtw{dd}{ch}_{ot}")

    for ch in range(NCH):
        c0 = ch * TCH
        b = ch % 2
        for i in range(nct):
            P.dma_sp(rc[i][:], xin[i * 128:(i + 1) * 128, c0:c0 + TCH],
                     writes=(rc[i],), tag=f"rcl{ch}_{i}")
        nmb, rsb, nmb_res, rsb_res = ln_stats(C, rc, 0, TCH, rowbuf, lnpool,
                                              tag=f"l0aln{ch}", sq_scratch=scr)
        for i in range(nct):
            s0 = scr[i % 2]
            P.i_tt(ALU.add, s0[:, 0:TCH], rc[i][:], nmb,
                   reads=(rc[i], nmb_res), writes=(s0,), tag=f"lna{ch}_{i}")
            P.i_tt(ALU.mult, s0[:, 0:TCH], s0[:, 0:TCH], rsb,
                   reads=(s0, rsb_res), writes=(s0,), tag=f"lnb{ch}_{i}")
            P.i_ts(hh[i][:], s0[:, 0:TCH], nwb_sb[:, i:i + 1], nwb_sb[:, 8 + i:9 + i],
                   ALU.mult, ALU.add, reads=(s0, nwb_sb), writes=(hh[i],),
                   tag=f"lnc{ch}_{i}")
        for ot in range(32):
            p1 = ps[ot % 2]
            pairs = [(w_in[:, ct * 4096 + ot * 128: ct * 4096 + ot * 128 + 128],
                      hh[ct][:]) for ct in range(nct)]
            mm_group(P, p1[:, 0:TCH], p1, pairs, reads=tuple(hh) + (w_in,),
                     tag=f"ip{ch}_{ot}")
            if ot < 16:
                P.i_copy(xp_bufs[b][ot][:, 3:3 + TCH], p1[:, 0:TCH], eng="dve",
                         reads=(p1,), writes=(xp_bufs[b][ot],), tag=f"ipx{ch}_{ot}")
            else:
                P.i_copy(zb_sb[:, 0:TCH], p1[:, 0:TCH], eng="dve",
                         reads=(p1,), writes=(zb_sb,), tag=f"ipz{ch}_{ot}")
                P.dma_sp(z_sp[(ot - 16) * 128:(ot - 15) * 128, c0:c0 + TCH],
                         zb_sb[:, 0:TCH], reads=(zb_sb,), writes=(f"zsp{ot}",),
                         tag=f"zw{ch}_{ot}")
        if ch > 0:
            for i in range(16):
                P.add("sp", lambda h, o=xp_bufs[b][i][:, 0:3],
                      i_=xp_bufs[1 - b][i][:, TCH:TCH + 3]: h.dma_start(out=o, in_=i_),
                      dma=True, reads=(xp_bufs[1 - b][i],), writes=(xp_bufs[b][i],),
                      tag=f"xgc{ch}_{i}")
            for i in range(16):
                P.add("sp", lambda h, o=xp_bufs[1 - b][i][:, 3 + TCH:6 + TCH],
                      i_=xp_bufs[b][i][:, 3:6]: h.dma_start(out=o, in_=i_),
                      dma=True, reads=(xp_bufs[b][i],), writes=(xp_bufs[1 - b][i],),
                      tag=f"xgr{ch}_{i}")
        conv_dir(ch, b, "f")
        if ch > 0:
            bprev = 1 - b
            _conv_rev(C, ch - 1, bprev, TCH, xp_bufs, xcb, cw_sb, scr, w_xp, dtr_sb,
                      bc_sb, ps, dtb_sb, w_dp, xc_sp, dt_sp, bc_sp, P)
    bl = (NCH - 1) % 2
    for i in range(16):
        P.i_memset(xp_bufs[bl][i][:, 3 + TCH:6 + TCH], 0.0,
                   writes=(xp_bufs[bl][i],), tag=f"xpgz{i}")
    _conv_rev(C, NCH - 1, bl, TCH, xp_bufs, xcb, cw_sb, scr, w_xp,
              dtr_sb, bc_sb, ps, dtb_sb, w_dp, xc_sp, dt_sp, bc_sp, P)
    stores = [op for op in P.ops if op.dma and (op.tag.startswith(("xcw", "dtw", "bcw", "zw")))]
    P.add("sp", lambda h: None, deps=tuple(stores), tag="final_wait")


def _conv_rev(C, ch, b, TCH, xp_bufs, xcb, cw_sb, scr, w_xp, dtr_sb, bc_sb, ps,
              dtb_sb, w_dp, xc_sp, dt_sp, bc_sp, P):
    dd = "r"
    c0 = ch * TCH
    for i in range(16):
        xpt = xp_bufs[b][i]
        wcol = lambda j: cw_sb[:, i * 10 + 5 + j: i * 10 + 5 + j + 1]
        s0 = scr[i % 2]
        P.i_ts(s0[:, 0:TCH], xpt[:, 6:6 + TCH], wcol(0), None, ALU.mult,
               reads=(xpt, cw_sb), writes=(s0,), tag=f"rcv0{ch}_{i}")
        for k in (1, 2, 3):
            P.i_stt(s0[:, 0:TCH], xpt[:, 6 - k:6 - k + TCH], wcol(k), s0[:, 0:TCH],
                    ALU.mult, ALU.add, reads=(xpt, cw_sb, s0), writes=(s0,),
                    tag=f"rcv{k}{ch}_{i}")
        bcol = cw_sb[:, i * 10 + 9: i * 10 + 10]
        P.i_act(AF.Silu, xcb[i][:], s0[:, 0:TCH], bias=bcol,
                reads=(s0, cw_sb), writes=(xcb[i],), tag=f"rcvs{ch}_{i}")
        P.dma_sp(xc_sp[dd][i * 128:(i + 1) * 128, c0:c0 + TCH], xcb[i][:],
                 reads=(xcb[i],), writes=(f"xcsp{dd}{i}",), tag=f"xcw{dd}{ch}_{i}")
    pairs = [(w_xp[dd][:, ct * 96:(ct + 1) * 96], xcb[ct][:]) for ct in range(16)]
    mm_group(P, ps[0][0:96, 0:TCH], ps[0], pairs,
             reads=tuple(xcb) + (w_xp[dd],), tag=f"rxdbl{ch}")
    P.i_copy(dtr_sb[:, 0:TCH], ps[0][0:64, 0:TCH], eng="dve",
             reads=(ps[0],), writes=(dtr_sb,), tag=f"rdtr{ch}")
    P.i_copy(bc_sb[:, 0:TCH], ps[0][64:96, 0:TCH], eng="act",
             reads=(ps[0],), writes=(bc_sb,), tag=f"rbc{ch}")
    P.dma_sp(bc_sp[dd][:, c0:c0 + TCH], bc_sb[:, 0:TCH],
             reads=(bc_sb,), writes=(f"bcsp{dd}",), tag=f"bcw{dd}{ch}")
    for ot in range(16):
        p1 = ps[1]
        mm_group(P, p1[:, 0:TCH], p1,
                 [(w_dp[dd][:, ot * 128:(ot + 1) * 128], dtr_sb[:, 0:TCH])],
                 reads=(dtr_sb, w_dp[dd]), tag=f"rdtm{ch}_{ot}")
        dcol = dtb_sb[:, 16 + ot:16 + ot + 1]
        s1, s2 = scr[0], scr[1]
        P.i_ts(s1[:, 0:TCH], p1[:, 0:TCH], dcol, None, ALU.add,
               reads=(p1, dtb_sb), writes=(s1,), tag=f"rdta{ch}_{ot}")
        P.i_act(AF.Abs, s2[:, 0:TCH], s1[:, 0:TCH],
                reads=(s1,), writes=(s2,), tag=f"rdtb{ch}_{ot}")
        P.i_act(AF.Exp, s2[:, 0:TCH], s2[:, 0:TCH], scale=-1.0,
                reads=(s2,), writes=(s2,), tag=f"rdtc{ch}_{ot}")
        P.i_ts(s2[:, 0:TCH], s2[:, 0:TCH], 1.0, 1.0, ALU.mult, ALU.add,
               reads=(s2,), writes=(s2,), tag=f"rdtd{ch}_{ot}")
        P.i_act(AF.Ln, s2[:, 0:TCH], s2[:, 0:TCH],
                reads=(s2,), writes=(s2,), tag=f"rdte{ch}_{ot}")
        P.i_ts(s1[:, 0:TCH], s1[:, 0:TCH], 0.0, None, ALU.max,
               reads=(s1,), writes=(s1,), tag=f"rdtf{ch}_{ot}")
        P.i_tt(ALU.add, scr[2][:, 0:TCH], s1[:, 0:TCH], s2[:, 0:TCH],
               reads=(s1, s2), writes=(scr[2],), tag=f"rdtg{ch}_{ot}")
        P.dma_sp(dt_sp[dd][ot * 128:(ot + 1) * 128, c0:c0 + TCH],
                 scr[2][:, 0:TCH], reads=(scr[2],), writes=(f"dtsp{dd}{ot}",),
                 tag=f"dtw{dd}{ch}_{ot}")


def build_l0_attn_p2(C):
    P, nc = C.P, C.nc
    nct = 8
    SC = 512
    NSC = T // SC
    xin = nc.dram_tensor("xres_in", [D, T], F32, kind="ExternalInput").ap()
    xout = nc.dram_tensor("xres_out", [D, T], F32, kind="ExternalOutput").ap()
    z_sp = nc.dram_tensor("l0a_z", [2048, T], BF16, kind="ExternalInput").ap()
    xc_sp = {dd: nc.dram_tensor(f"l0a_xc_{dd}", [2048, T], BF16, kind="ExternalInput").ap()
             for dd in "fr"}
    dt_sp = {dd: nc.dram_tensor(f"l0a_dt_{dd}", [2048, T], F32, kind="ExternalInput").ap()
             for dd in "fr"}
    bc_sp = {dd: nc.dram_tensor(f"l0a_bc_{dd}", [32, T], F32, kind="ExternalInput").ap()
             for dd in "fr"}
    alog = nc.dram_tensor("l0a_alog", [128, 32 * 16], F32, kind="ExternalInput").ap()
    dpack = nc.dram_tensor("l0a_d", [128, 32], F32, kind="ExternalInput").ap()
    opw = nc.dram_tensor("l0a_outprojT", [2048, 1024], F32, kind="ExternalInput").ap()

    alog_sb = C.sbuf("l0a_alog", [128, 512], F32)
    P.dma_sp(alog_sb[:], alog[:], writes=(alog_sb,), tag="alog")
    A_sb = C.sbuf("l0a_A", [128, 512], F32)
    P.i_act(AF.Exp, A_sb[:], alog_sb[:], reads=(alog_sb,), writes=(A_sb,), tag="Aexp")
    P.i_ts(A_sb[:], A_sb[:], -1.0, None, ALU.mult, reads=(A_sb,), writes=(A_sb,),
           tag="Aneg")
    d_sb = C.sbuf("l0a_dsb", [128, 32], F32)
    P.dma_sp(d_sb[:], dpack[:], writes=(d_sb,), tag="dp")
    w_op = C.sbuf("l0a_wop", [128, 16 * 1024], BF16)
    P.dma_pool(w_op[:].rearrange("p (c n) -> p c n", n=1024),
               opw.rearrange("(c p) n -> p c n", p=128), writes=(w_op,), tag="wop")

    yg_sum = [C.sbuf(f"l0a_yg{i}", [128, T], BF16) for i in range(16)]
    Brep = [C.sbuf(f"l0a_Br{n}", [128, SC], BF16) for n in range(16)]
    Crep = [C.sbuf(f"l0a_Cr{n}", [128, SC], BF16) for n in range(16)]
    hstate = [C.sbuf(f"l0a_hs{i}", [128, 16], F32) for i in range(16)]
    dt_sl = [C.sbuf(f"l0a_dts{i}", [128, SC], F32) for i in range(2)]
    xc_sl = [C.sbuf(f"l0a_xcs{i}", [128, SC], BF16) for i in range(2)]
    z_sl = [C.sbuf(f"l0a_zs{i}", [128, SC], BF16) for i in range(2)]
    du_sl = [C.sbuf(f"l0a_dus{i}", [128, SC], BF16) for i in range(2)]
    a_sc = [C.sbuf(f"l0a_asc{i}", [128, SC], F32) for i in range(2)]
    b_sc = [C.sbuf(f"l0a_bsc{i}", [128, SC], F32) for i in range(2)]
    h_sc = [C.sbuf(f"l0a_hsc{i}", [128, SC], BF16) for i in range(2)]
    y_sl = [C.sbuf(f"l0a_ysl{i}", [128, SC], F32) for i in range(2)]
    scr = [C.sbuf(f"l0a_p2scr{i}", [128, SC], F32) for i in range(2)]
    rc = C.tiles("l0a_p2rc", nct, SC, F32)
    outc = C.tiles("l0a_p2out", nct, SC, F32)
    rowbuf = C.sbuf("l0a_p2rows", [1, 3 * SC], F32)
    ps = [C.psum(f"l0a_p2ps{i}", [128, 512]) for i in range(4)]
    ps_s = C.psum("l0a_p2pss", [1, 1024])
    lnpool = {"sx": (ps_s, 0), "sq": (ps_s, 512), "nmb": (ps[2], 0), "rsb": (ps[3], 0)}

    for i in range(16):
        P.i_memset(hstate[i][:], 0.0, writes=(hstate[i],), tag=f"hs0_{i}")

    for dd in ("f", "r"):
        doff = 0 if dd == "f" else 256
        chunks = list(range(NSC)) if dd == "f" else list(range(NSC - 1, -1, -1))
        if dd == "r":
            for i in range(16):
                P.i_memset(hstate[i][:], 0.0, writes=(hstate[i],), tag=f"hsr_{i}")
        for ci, ch in enumerate(chunks):
            c0 = ch * SC
            for n in range(16):
                P.dma_pool(Brep[n][:],
                           bc_sp[dd][n:n + 1, c0:c0 + SC].partition_broadcast(128),
                           writes=(Brep[n],), tag=f"br{dd}{ch}_{n}")
                P.dma_pool(Crep[n][:],
                           bc_sp[dd][16 + n:17 + n, c0:c0 + SC].partition_broadcast(128),
                           writes=(Crep[n],), tag=f"cr{dd}{ch}_{n}")
            for sl in range(16):
                B = sl % 2
                P.dma_sp(dt_sl[B][:], dt_sp[dd][sl * 128:(sl + 1) * 128, c0:c0 + SC],
                         writes=(dt_sl[B],), tag=f"dtl{dd}{ch}_{sl}")
                P.dma_sp(xc_sl[B][:], xc_sp[dd][sl * 128:(sl + 1) * 128, c0:c0 + SC],
                         writes=(xc_sl[B],), tag=f"xcl{dd}{ch}_{sl}")
                P.dma_sp(z_sl[B][:], z_sp[sl * 128:(sl + 1) * 128, c0:c0 + SC],
                         writes=(z_sl[B],), tag=f"zl{dd}{ch}_{sl}")
                P.i_tt(ALU.mult, du_sl[B][:], dt_sl[B][:], xc_sl[B][:],
                       reads=(dt_sl[B], xc_sl[B]), writes=(du_sl[B],),
                       tag=f"du{dd}{ch}_{sl}")
                P.i_memset(y_sl[B][:], 0.0, writes=(y_sl[B],), tag=f"y0{dd}{ch}_{sl}")
                for n in range(16):
                    acol = A_sb[:, doff + sl * 16 + n: doff + sl * 16 + n + 1]
                    P.i_act(AF.Exp, a_sc[B][:], dt_sl[B][:], scale=acol,
                            reads=(dt_sl[B], A_sb), writes=(a_sc[B],),
                            tag=f"ae{dd}{ch}_{sl}_{n}")
                    P.i_tt(ALU.mult, b_sc[B][:], du_sl[B][:], Brep[n][:],
                           reads=(du_sl[B], Brep[n]), writes=(b_sc[B],),
                           tag=f"bb{dd}{ch}_{sl}_{n}")
                    hs = hstate[sl][:, n:n + 1]
                    if dd == "f":
                        P.i_scan(h_sc[B][:], a_sc[B][:], b_sc[B][:], hs,
                                 reads=(a_sc[B], b_sc[B], hstate[sl]),
                                 writes=(h_sc[B],), tag=f"sc{dd}{ch}_{sl}_{n}")
                        P.i_copy(hstate[sl][:, n:n + 1], h_sc[B][:, SC - 1:SC],
                                 eng="dve", reads=(h_sc[B],), writes=(hstate[sl],),
                                 tag=f"hc{dd}{ch}_{sl}_{n}")
                    else:
                        P.i_scan(h_sc[B][:, ::-1], a_sc[B][:, ::-1], b_sc[B][:, ::-1],
                                 hs, reads=(a_sc[B], b_sc[B], hstate[sl]),
                                 writes=(h_sc[B],), tag=f"sc{dd}{ch}_{sl}_{n}")
                        P.i_copy(hstate[sl][:, n:n + 1], h_sc[B][:, 0:1],
                                 eng="dve", reads=(h_sc[B],), writes=(hstate[sl],),
                                 tag=f"hc{dd}{ch}_{sl}_{n}")
                    P.i_tt(ALU.mult, h_sc[B][:], h_sc[B][:], Crep[n][:],
                           reads=(h_sc[B], Crep[n]), writes=(h_sc[B],),
                           tag=f"hC{dd}{ch}_{sl}_{n}")
                    P.i_tt(ALU.add, y_sl[B][:], y_sl[B][:], h_sc[B][:],
                           reads=(h_sc[B],), writes=(y_sl[B],),
                           tag=f"ya{dd}{ch}_{sl}_{n}")
                dcol = d_sb[:, (0 if dd == "f" else 16) + sl:
                            (0 if dd == "f" else 16) + sl + 1]
                P.i_stt(y_sl[B][:], xc_sl[B][:], dcol, y_sl[B][:], ALU.mult, ALU.add,
                        reads=(xc_sl[B], d_sb, y_sl[B]), writes=(y_sl[B],),
                        tag=f"yd{dd}{ch}_{sl}")
                P.i_act(AF.Silu, scr[B][:], z_sl[B][:],
                        reads=(z_sl[B],), writes=(scr[B],), tag=f"sz{dd}{ch}_{sl}")
                if dd == "f":
                    P.i_tt(ALU.mult, yg_sum[sl][:, c0:c0 + SC], y_sl[B][:], scr[B][:],
                           reads=(y_sl[B], scr[B]), writes=(yg_sum[sl],),
                           tag=f"yg{dd}{ch}_{sl}")
                else:
                    P.i_tt(ALU.mult, scr[B][:], y_sl[B][:], scr[B][:],
                           reads=(y_sl[B], scr[B]), writes=(scr[B],),
                           tag=f"ygm{dd}{ch}_{sl}")
                    P.i_tt(ALU.add, yg_sum[sl][:, c0:c0 + SC],
                           yg_sum[sl][:, c0:c0 + SC], scr[B][:],
                           reads=(scr[B], yg_sum[sl]), writes=(yg_sum[sl],),
                           tag=f"yg{dd}{ch}_{sl}")

    for ch in range(NSC):
        c0 = ch * SC
        for i in range(nct):
            P.dma_sp(rc[i][:], xin[i * 128:(i + 1) * 128, c0:c0 + SC],
                     writes=(rc[i],), tag=f"p2rcl{ch}_{i}")
        for ot in range(nct):
            p1 = ps[ot % 2]
            pairs = [(w_op[:, ct * 1024 + ot * 128: ct * 1024 + ot * 128 + 128],
                      yg_sum[ct][:, c0:c0 + SC]) for ct in range(16)]
            mm_group(P, p1[:, 0:SC], p1, pairs, reads=tuple(yg_sum) + (w_op,),
                     tag=f"op{ch}_{ot}")
            P.i_copy(outc[ot][:], p1[:, 0:SC], eng="dve",
                     reads=(p1,), writes=(outc[ot],), tag=f"opev{ch}_{ot}")
        nmb, rsb, nmb_res, rsb_res = ln_stats(C, rc, 0, SC, rowbuf, lnpool,
                                              tag=f"l0ailn{ch}", sq_scratch=scr)
        for i in range(nct):
            s0 = scr[i % 2]
            P.i_tt(ALU.add, s0[:, 0:SC], rc[i][:], nmb,
                   reads=(rc[i], nmb_res), writes=(s0,), tag=f"p2ia{ch}_{i}")
            P.i_tt(ALU.mult, s0[:, 0:SC], s0[:, 0:SC], rsb,
                   reads=(s0, rsb_res), writes=(s0,), tag=f"p2ib{ch}_{i}")
            P.i_tt(ALU.add, outc[i][:], outc[i][:], s0[:, 0:SC],
                   reads=(s0, outc[i]), writes=(outc[i],), tag=f"p2ic{ch}_{i}")
        nmb, rsb, nmb_res, rsb_res = ln_stats(C, outc, 0, SC, rowbuf, lnpool,
                                              tag=f"l0agl{ch}", sq_scratch=scr)
        for i in range(nct):
            s0 = scr[i % 2]
            P.i_tt(ALU.add, s0[:, 0:SC], outc[i][:], nmb,
                   reads=(outc[i], nmb_res), writes=(s0,), tag=f"p2ga{ch}_{i}")
            P.i_tt(ALU.mult, s0[:, 0:SC], s0[:, 0:SC], rsb,
                   reads=(s0, rsb_res), writes=(s0,), tag=f"p2gb{ch}_{i}")
            P.i_tt(ALU.add, rc[i][:], rc[i][:], s0[:, 0:SC],
                   reads=(s0, rc[i]), writes=(rc[i],), tag=f"p2gc{ch}_{i}")
            P.dma_sp(xout[i * 128:(i + 1) * 128, c0:c0 + SC], rc[i][:],
                     reads=(rc[i],), writes=(f"p2xo{i}",), tag=f"p2xst{ch}_{i}")
    stores = [op for op in P.ops if op.tag.startswith("p2xst")]
    P.add("sp", lambda h: None, deps=tuple(stores), tag="final_wait")


def l0_attn_host_inputs(inputs):
    d = {}
    d["l0a_inprojT"] = hT(inputs["L0_attn_in_proj_w"])
    nw = colpack(inputs["L0_attn_norm_w"])
    nb = colpack(inputs["L0_attn_norm_b"])
    d["l0a_nwb"] = np.ascontiguousarray(np.concatenate([nw, nb], axis=1))
    cw = np.zeros((128, 160), np.float32)
    cwf = np.asarray(inputs["L0_attn_conv_w_f"], np.float32)[:, 0, :]
    cwr = np.asarray(inputs["L0_attn_conv_w_r"], np.float32)[:, 0, :]
    cbf = np.asarray(inputs["L0_attn_conv_b_f"], np.float32)
    cbr = np.asarray(inputs["L0_attn_conv_b_r"], np.float32)
    for i in range(16):
        sl = slice(i * 128, (i + 1) * 128)
        cw[:, i * 10 + 0:i * 10 + 4] = cwf[sl]
        cw[:, i * 10 + 4] = cbf[sl]
        cw[:, i * 10 + 5:i * 10 + 9] = cwr[sl]
        cw[:, i * 10 + 9] = cbr[sl]
    d["l0a_convw"] = cw
    d["l0a_xprojTf"] = hT(inputs["L0_attn_x_proj_w_f"])
    d["l0a_xprojTr"] = hT(inputs["L0_attn_x_proj_w_r"])
    d["l0a_dtprojTf"] = hT(inputs["L0_attn_dt_proj_w_f"])
    d["l0a_dtprojTr"] = hT(inputs["L0_attn_dt_proj_w_r"])
    dtb = np.concatenate([colpack(inputs["L0_attn_dt_proj_b_f"], 16),
                          colpack(inputs["L0_attn_dt_proj_b_r"], 16)], axis=1)
    d["l0a_dtb"] = np.ascontiguousarray(dtb)
    al = np.zeros((128, 512), np.float32)
    for dd, key, off in (("f", "L0_attn_A_log_f", 0), ("r", "L0_attn_A_log_r", 256)):
        a = np.asarray(inputs[key], np.float32)
        for sl in range(16):
            al[:, off + sl * 16:off + sl * 16 + 16] = a[sl * 128:(sl + 1) * 128, :]
    d["l0a_alog"] = al
    dpk = np.concatenate([colpack(inputs["L0_attn_D_f"], 16),
                          colpack(inputs["L0_attn_D_r"], 16)], axis=1)
    d["l0a_d"] = np.ascontiguousarray(dpk)
    d["l0a_outprojT"] = hT(inputs["L0_attn_out_proj_w"])
    return d


def prog_l0_attn_p1(strict=True):
    nc = bass.Bass("TRN2", target_bir_lowering=False, debug=False)
    with ExitStack() as es:
        C = Ctx(nc, es)
        build_l0_attn_p1(C)
        C.P.emit(strict=strict)
    return nc


def prog_l0_attn_p2(strict=True):
    nc = bass.Bass("TRN2", target_bir_lowering=False, debug=False)
    with ExitStack() as es:
        C = Ctx(nc, es)
        build_l0_attn_p2(C)
        C.P.emit(strict=strict)
    return nc





def l0f_consts():
    DIM_HEAD, WINDOW, n = 64, 256, T
    half = np.arange(0, DIM_HEAD, 2, dtype=np.float64)
    inv_freq = 1.0 / 10000.0 ** (half / DIM_HEAD)
    t = np.arange(n, dtype=np.float64)
    freqs = t[:, None] * inv_freq[None, :]
    theta = np.concatenate([freqs, freqs], 1)
    base_scale = (half + 0.4 * DIM_HEAD) / (1.4 * DIM_HEAD)
    power = (t - n // 2) / (WINDOW // 2)
    scale = base_scale[None, :] ** power[:, None]
    scale = np.concatenate([scale, scale], 1)
    cq = (np.cos(theta) * scale).T
    ck = (np.cos(theta) / scale).T
    sgn = np.concatenate([-np.ones(32), np.ones(32)])[:, None]
    sq = (np.sin(theta) * scale).T * sgn
    sk = (np.sin(theta) / scale).T * sgn
    def dup(a):
        return np.ascontiguousarray(np.concatenate([a, a], 0).astype(np.float32))
    tabs = {"rot_cq": dup(cq), "rot_sq": dup(sq), "rot_ck": dup(ck), "rot_sk": dup(sk)}
    sw = np.zeros((128, 128), np.float32)
    for e in range(128):
        base = (e // 64) * 64
        sw[base + (e - base + 32) % 64, e] = 1.0
    tabs["swapP"] = sw
    sel = np.zeros((16, 8 * 128), np.float32)
    for kt in range(8):
        for el in range(128):
            sel[kt * 2 + el // 64, kt * 128 + el] = 1.0
    tabs["selh"] = sel
    m = np.zeros((128, 3 * 384), np.float32)
    for v in range(3):
        for il in range(128):
            for jj in range(384):
                ok = (il <= jj <= il + 256) and (jj >= 256 - v * 128)
                m[il, v * 384 + jj] = 0.0 if ok else -1e30
    tabs["simmask"] = m
    tabs["identity"] = np.eye(128, dtype=np.float32)
    return tabs


L0F_PHASES = 99


def build_l0f_attn(C):
    P, nc = C.P, C.nc
    nct = 8
    TCH = 512
    NCH = T // TCH
    sin = nc.dram_tensor("sres_in", [D, T], F32, kind="ExternalInput").ap()
    sout = nc.dram_tensor("sres_out", [D, T], F32, kind="ExternalOutput").ap()
    wqkv = nc.dram_tensor("l0f_wqkv", [1024, 3072], F32, kind="ExternalInput").ap()
    wo = nc.dram_tensor("l0f_wo", [1024, 1024], F32, kind="ExternalInput").ap()
    qsks = nc.dram_tensor("l0f_qsks", [128, 16], F32, kind="ExternalInput").ap()
    tabs = {}
    for nm, sh in (("rot_cq", [128, T]), ("rot_sq", [128, T]), ("rot_ck", [128, T]),
                   ("rot_sk", [128, T]), ("swapP", [128, 128]),
                   ("selh", [16, 1024]), ("simmask", [128, 1152]),
                   ("identity", [128, 128])):
        tabs[nm] = nc.dram_tensor("l0f_" + nm, sh, F32, kind="ExternalInput").ap()

    wblk = [C.sbuf(f"l0f_wb{i}", [128, 8 * 1024], BF16) for i in range(2)]

    def load_wblock(buf, dram_cols_src, tag):
        P.dma_pool(buf[:].rearrange("p (c n) -> p c n", n=1024),
                   dram_cols_src.rearrange("(c p) n -> p c n", p=128),
                   writes=(buf,), tag=tag)
    qsks_sb = C.sbuf("l0f_qsks", [128, 16], F32)
    P.dma_sp(qsks_sb[:], qsks[:], writes=(qsks_sb,), tag="qsks")
    qs8 = C.sbuf("l0f_qs8", [128, 16], F32)
    P.i_ts(qs8[:, 0:8], qsks_sb[:, 0:8], 8.0, None, ALU.mult,
           reads=(qsks_sb,), writes=(qs8,), tag="qs8")
    P.i_copy(qs8[:, 8:16], qsks_sb[:, 8:16], eng="dve",
             reads=(qsks_sb,), writes=(qs8,), tag="ks1")
    tb = {}
    swap_sb = C.sbuf("swp", [128, 128], BF16)
    P.dma_pool(swap_sb[:], tabs["swapP"][:], writes=(swap_sb,), tag="swp")
    sel_sb = C.sbuf("selh", [16, 1024], BF16)
    P.dma_pool(sel_sb[:], tabs["selh"][:], writes=(sel_sb,), tag="selh")
    mask_sb = C.sbuf("mask", [128, 1152], BF16)
    P.dma_pool(mask_sb[:], tabs["simmask"][:], writes=(mask_sb,), tag="mask")
    id_sb = C.sbuf("idn", [128, 128], BF16)
    P.dma_pool(id_sb[:], tabs["identity"][:], writes=(id_sb,), tag="idn")
    idf_sb = C.sbuf("idnf", [128, 128], F32)
    P.dma_sp(idf_sb[:], tabs["identity"][:], writes=(idf_sb,), tag="idnf")
    hsel = C.sbuf("hsel", [128, 2], F32)
    P.i_memset(hsel[:, 0:2], 0.0, writes=(hsel,), tag="hs0")
    P.i_memset(hsel[0:64, 0:1], 1.0, writes=(hsel,), tag="hs1")
    P.i_memset(hsel[64:128, 1:2], 1.0, writes=(hsel,), tag="hs2")

    qb = [C.sbuf(f"l0f_q{i}", [128, T], BF16) for i in range(nct)]
    kb = [C.sbuf(f"l0f_k{i}", [128, 256 + T], BF16) for i in range(nct)]
    vzero = C.sbuf("l0f_vz", [128, 1024], BF16)
    vt = [vzero, vzero] + [C.sbuf(f"l0f_v{i}", [128, 1024], BF16) for i in range(16)]
    ofm = [C.sbuf(f"l0f_o{i}", [128, T], BF16) for i in range(nct)]
    scr3 = C.sbuf("l0f_scr3", [128, 1536], F32)
    sc = [scr3[:, i * 512:(i + 1) * 512] for i in range(3)]
    scrF = scr3[:]
    SCR = scr3
    attn_sb = C.sbuf("l0f_attn", [128, 1536], BF16)
    attnT = C.sbuf("l0f_attnT", [128, 2 * 384], BF16)
    z4 = C.sbuf("l0f_z4", [128, 12], F32)
    rnq = C.sbuf("l0f_rnq", [128, 16 * 16], F32)
    rnk = C.sbuf("l0f_rnk", [128, 16 * 16], F32)
    rnhT = C.sbuf("l0f_rnhT", [16, 2 * T], BF16)
    for i in range(nct):
        P.i_memset(kb[i][:, 0:256], 0.0, writes=(kb[i],), tag=f"kg{i}")
    P.i_memset(vzero[:], 0.0, writes=(vzero,), tag="vg")

    ps_rot = [C.psum(f"l0f_psr{i}", [128, 512]) for i in range(2)]
    ps4 = C.psum("l0f_ps4", [128, 2048])
    ps_sq = C.psum("l0f_pssq", [128, 512])
    pst = C.psum("l0f_pst", [128, 1024], BF16)

    shb = [C.sbuf(f"l0f_sh{i}", [128, TCH], BF16) for i in range(nct)]

    def load_sh(ch, tag):
        c0 = ch * TCH
        for i in range(nct):
            if i < 4:
                P.dma_pool(shb[i][:], sin[i * 128:(i + 1) * 128, c0:c0 + TCH],
                           writes=(shb[i],), tag=f"{tag}{ch}_{i}")
            elif c0 > 0:
                P.dma_pool(shb[i][:], sin[i * 128:(i + 1) * 128, c0 - 1:c0 - 1 + TCH],
                           writes=(shb[i],), tag=f"{tag}{ch}_{i}")
            else:
                P.i_memset(shb[i][:, 0:1], 0.0, writes=(shb[i],), tag=f"{tag}z{i}")
                P.dma_pool(shb[i][:, 1:TCH], sin[i * 128:(i + 1) * 128, 0:TCH - 1],
                           writes=(shb[i],), tag=f"{tag}{ch}_{i}")

    load_wblock(wblk[0], wqkv[:, 0:1024], "wq_q")
    load_wblock(wblk[1], wqkv[:, 1024:2048], "wq_k")
    for ch in range(NCH):
        c0 = ch * TCH
        load_sh(ch, "shq")
        for ot in range(16):
            wb = wblk[0] if ot < 8 else wblk[1]
            p1 = ps_rot[ot % 2]
            pairs = [(wb[:, ct * 1024 + (ot % 8) * 128: ct * 1024 + (ot % 8) * 128 + 128],
                      shb[ct][:]) for ct in range(nct)]
            mm_group(P, p1[:, 0:TCH], p1, pairs, reads=tuple(shb) + (wb,),
                     tag=f"qk{ch}_{ot}")
            if ot < 8:
                P.i_copy(qb[ot][:, c0:c0 + TCH], p1[:, 0:TCH], eng="dve",
                         reads=(p1,), writes=(qb[ot],), tag=f"qe{ch}_{ot}")
            else:
                P.i_copy(kb[ot - 8][:, 256 + c0:256 + c0 + TCH], p1[:, 0:TCH],
                         eng="dve", reads=(p1,), writes=(kb[ot - 8],),
                         tag=f"ke{ch}_{ot}")
    load_wblock(wblk[1], wqkv[:, 2048:3072], "wq_v")
    for ch in range(NCH):
        c0 = ch * TCH
        load_sh(ch, "shv")
        for sl in range(4):
            for jh in range(2):
                p1 = ps_rot[jh]
                pairs = [(shb[ct][:, sl * 128:(sl + 1) * 128],
                          wblk[1][:, ct * 1024 + jh * 512: ct * 1024 + jh * 512 + 512])
                         for ct in range(nct)]
                mm_group(P, p1[:, 0:512], p1, pairs, reads=tuple(shb) + (wblk[1],),
                         tag=f"vm{ch}_{sl}_{jh}")
                gt = 2 + ch * 4 + sl
                P.i_copy(vt[gt][:, jh * 512:(jh + 1) * 512], p1[:, 0:512], eng="dve",
                         reads=(p1,), writes=(vt[gt],), tag=f"ve{ch}_{sl}_{jh}")
    for ci, nm in enumerate(("rot_cq", "rot_sq", "rot_ck", "rot_sk")):
        tb[nm] = wblk[0][:, ci * 2048:(ci + 1) * 2048]
        P.dma_pool(tb[nm], tabs[nm][:], writes=(wblk[0],), tag=nm)

    def prep(side, buf, goff, ctab, stab, scol_off, rn_sb, rn_off, tabres):
        for sl in range(16):
            for i in range(nct):
                P.i_act(AF.Square, sc[i % 2][:, 0:128],
                        buf[i][:, goff + sl * 128: goff + (sl + 1) * 128],
                        reads=(buf[i],), writes=(SCR,), tag=f"{side}sq{sl}_{i}")
                mm_group(P, ps_sq[:, i * 2:i * 2 + 2], ps_sq,
                         [(sc[i % 2][:, 0:128], hsel[:, 0:2])],
                         reads=(SCR, hsel), tag=f"{side}sm{sl}_{i}")
            P.i_act(AF.Sqrt, rn_sb[:, sl * 16:(sl + 1) * 16], ps_sq[:, 0:16],
                    reads=(ps_sq,), writes=(rn_sb,), tag=f"{side}rs{sl}")
            P.i_ts(rn_sb[:, sl * 16:(sl + 1) * 16], rn_sb[:, sl * 16:(sl + 1) * 16],
                   1e-12, None, ALU.max, reads=(rn_sb,), writes=(rn_sb,),
                   tag=f"{side}rm{sl}")
            P.add("dve", lambda h, o=rn_sb[:, sl * 16:(sl + 1) * 16],
                  i=rn_sb[:, sl * 16:(sl + 1) * 16]: h.reciprocal(o, i),
                  reads=(rn_sb,), writes=(rn_sb,), tag=f"{side}rr{sl}")
            P.add("pe", lambda h, o=ps_sq[0:16, 0:128],
                  l=rn_sb[:, sl * 16:(sl + 1) * 16], r_=idf_sb:
                  h.matmul(o, l, r_[:], start=True, stop=True),
                  reads=(rn_sb, idf_sb), writes=(ps_sq,), tag=f"{side}rt{sl}")
            P.i_copy(rnhT[0:16, rn_off + sl * 128: rn_off + (sl + 1) * 128],
                     ps_sq[0:16, 0:128], eng="act",
                     reads=(ps_sq,), writes=(rnhT,), tag=f"{side}rte{sl}")
        for i in range(nct):
            P.i_ts(buf[i][:, goff:goff + T], buf[i][:, goff:goff + T],
                   qs8[:, scol_off + i:scol_off + i + 1], None, ALU.mult,
                   reads=(buf[i], qs8), writes=(buf[i],), tag=f"{side}qs{i}")
            for ch in range(NCH):
                c0 = ch * TCH
                P.add("pe", lambda h, o=ps_rot[0][:, 0:TCH], l=swap_sb[:],
                      r_=buf[i][:, goff + c0: goff + c0 + TCH]:
                      h.matmul(o, l, r_, start=True, stop=True),
                      reads=(swap_sb, buf[i]), writes=(ps_rot[0],),
                      tag=f"{side}sw{i}_{ch}")
                P.i_tt(ALU.mult, sc[2], buf[i][:, goff + c0: goff + c0 + TCH],
                       ctab[:, c0:c0 + TCH], reads=(buf[i], tabres), writes=(SCR,),
                       tag=f"{side}r1{i}_{ch}")
                P.i_tt(ALU.mult, sc[0], ps_rot[0][:, 0:TCH],
                       stab[:, c0:c0 + TCH], reads=(ps_rot[0], tabres), writes=(SCR,),
                       tag=f"{side}r2{i}_{ch}")
                P.i_tt(ALU.add, buf[i][:, goff + c0: goff + c0 + TCH],
                       sc[2], sc[0],
                       reads=(SCR,), writes=(buf[i],), tag=f"{side}r3{i}_{ch}")
                P.add("pe", lambda h, o=ps_rot[1][:, 0:TCH],
                      l=sel_sb[0:16, i * 128:(i + 1) * 128],
                      r_=rnhT[0:16, rn_off + c0: rn_off + c0 + TCH]:
                      h.matmul(o, l, r_, start=True, stop=True),
                      reads=(sel_sb, rnhT), writes=(ps_rot[1],),
                      tag=f"{side}rb{i}_{ch}")
                P.i_tt(ALU.mult, buf[i][:, goff + c0: goff + c0 + TCH],
                       buf[i][:, goff + c0: goff + c0 + TCH], ps_rot[1][:, 0:TCH],
                       reads=(buf[i], ps_rot[1]), writes=(buf[i],),
                       tag=f"{side}rn{i}_{ch}")

    if L0F_PHASES >= 2:
        prep("q", qb, 0, tb["rot_cq"], tb["rot_sq"], 0, rnq, 0, wblk[0])
        prep("k", kb, 256, tb["rot_ck"], tb["rot_sk"], 8, rnk, T, wblk[0])

    for ib in range(16 if L0F_PHASES >= 3 else 0):
        i0 = ib * 128
        mv = mask_sb[:, min(ib, 2) * 384: min(ib, 2) * 384 + 384]
        for hg in range(4):
            for hi in range(4):
                hh = hg * 4 + hi
                hp, e = hh // 2, hh % 2
                reg = ps4[:, hi * 512: hi * 512 + 384]
                P.add("pe", lambda h, o=reg,
                      l=qb[hp][64 * e:64 * e + 64, i0:i0 + 128],
                      r_=kb[hp][64 * e:64 * e + 64, i0:i0 + 384]:
                      h.matmul(o, l, r_, start=True, stop=False),
                      reads=(qb[hp], kb[hp]), writes=(ps4,), tag=f"qk{ib}_{hh}")
                P.add("pe", lambda h, o=reg, l=id_sb[:], r_=mv:
                      h.matmul(o, l, r_, start=False, stop=True),
                      reads=(id_sb, mask_sb), writes=(ps4,), tag=f"mk{ib}_{hh}")
            v4 = ps4[:].rearrange("p (g x) -> p g x", x=512)[:, :, 0:384]
            P.i_reduce(ALU.max, z4[:, 4:8].unsqueeze(2), v4,
                       reads=(ps4,), writes=(z4,), tag=f"mx{ib}_{hg}")
            m4b = z4[:, 4:8].unsqueeze(2).broadcast_to([128, 4, 384])
            P.i_tt(ALU.subtract, scrF.rearrange("p (g x) -> p g x", x=384), v4,
                   m4b, reads=(ps4, z4), writes=(SCR,), tag=f"sb{ib}_{hg}")
            for hi in range(4):
                P.i_act(AF.Exp, attn_sb[:, hi * 384:(hi + 1) * 384],
                        scrF[:, hi * 384:(hi + 1) * 384],
                        accum_out=z4[:, hi:hi + 1],
                        reads=(SCR,), writes=(attn_sb, z4), tag=f"ex{ib}_{hg}_{hi}")
            P.add("dve", lambda h, o=z4[:, 8:12], i=z4[:, 0:4]: h.reciprocal(o, i),
                  reads=(z4,), writes=(z4,), tag=f"rc{ib}_{hg}")
            rzb = z4[:, 8:12].unsqueeze(2).broadcast_to([128, 4, 384])
            P.i_tt(ALU.mult, attn_sb[:].rearrange("p (g x) -> p g x", x=384),
                   attn_sb[:].rearrange("p (g x) -> p g x", x=384), rzb,
                   reads=(attn_sb, z4), writes=(attn_sb,), tag=f"nm{ib}_{hg}")
            for hi in range(4):
                hh = hg * 4 + hi
                hp, e = hh // 2, hh % 2
                for kk in range(3):
                    P.add("pe", lambda h, o=pst[:, (kk % 4) * 128:(kk % 4) * 128 + 128],
                          i=attn_sb[:, hi * 384 + kk * 128: hi * 384 + (kk + 1) * 128],
                          idn=id_sb: h.transpose(o, i, idn[:]),
                          reads=(attn_sb, id_sb), writes=(pst,), tag=f"tr{ib}_{hh}_{kk}")
                    P.i_copy(attnT[:, (hi % 2) * 384 + kk * 128:(hi % 2) * 384 + (kk + 1) * 128],
                             pst[:, (kk % 4) * 128:(kk % 4) * 128 + 128], eng="dve",
                             reads=(pst,), writes=(attnT,), tag=f"te{ib}_{hh}_{kk}")
                    P.add("pe", lambda h,
                          o=ps_rot[hg % 2][64 * e:64 * e + 64, (hp % 2) * 128:(hp % 2) * 128 + 128],
                          l=vt[ib + kk][:, hh * 64: hh * 64 + 64],
                          r_=attnT[:, (hi % 2) * 384 + kk * 128:(hi % 2) * 384 + (kk + 1) * 128],
                          st=(kk == 0), sp=(kk == 2):
                          h.matmul(o, l, r_, start=st, stop=sp),
                          reads=(vt[ib + kk], attnT), writes=(ps_rot[hg % 2],),
                          tag=f"pv{ib}_{hh}_{kk}")
                if e == 1:
                    P.i_copy(ofm[hp][:, i0:i0 + 128],
                             ps_rot[hg % 2][:, (hp % 2) * 128:(hp % 2) * 128 + 128],
                             eng="dve", reads=(ps_rot[hg % 2],), writes=(ofm[hp],),
                             tag=f"oe{ib}_{hp}")

    if L0F_PHASES < 4:
        for i in range(nct):
            P.i_memset(ofm[i][:], 0.0, writes=(ofm[i],), tag=f"ofz{i}")
    load_wblock(wblk[1], wo[:, :], "wo")
    for ch in range(NCH):
        c0 = ch * TCH
        for ot in range(nct):
            p1 = ps_rot[ot % 2]
            pairs = [(wblk[1][:, ct * 1024 + ot * 128: ct * 1024 + ot * 128 + 128],
                      ofm[ct][:, c0:c0 + TCH]) for ct in range(nct)]
            mm_group(P, p1[:, 0:TCH], p1, pairs, reads=tuple(ofm) + (wblk[1],),
                     tag=f"wom{ch}_{ot}")
            P.dma_sp(sc[2], sin[ot * 128:(ot + 1) * 128, c0:c0 + TCH],
                     writes=(SCR,), tag=f"wrl{ch}_{ot}")
            P.i_tt(ALU.add, sc[2], sc[2], p1[:, 0:TCH],
                   reads=(p1, SCR), writes=(SCR,), tag=f"wra{ch}_{ot}")
            P.dma_sp(sout[ot * 128:(ot + 1) * 128, c0:c0 + TCH], sc[2],
                     reads=(SCR,), writes=(f"so{ot}",), tag=f"sst{ch}_{ot}")
    stores = [op for op in P.ops if op.tag.startswith("sst")]
    P.add("sp", lambda h: None, deps=tuple(stores), tag="final_wait")


def build_l0f_ffn(C):
    P, nc = C.P, C.nc
    nct = 8
    sin = nc.dram_tensor("sres_in", [D, T], F32, kind="ExternalInput").ap()
    sout = nc.dram_tensor("sres_out", [D, T], F32, kind="ExternalOutput").ap()
    w1 = nc.dram_tensor("l0f_w1", [1024, 5460], F32, kind="ExternalInput").ap()
    w2 = nc.dram_tensor("l0f_w2", [2730, 1024], F32, kind="ExternalInput").ap()
    lngb = nc.dram_tensor("l0f_lngb", [128, 16], F32, kind="ExternalInput").ap()
    b1p = nc.dram_tensor("l0f_b1", [128, 44], F32, kind="ExternalInput").ap()
    b2p = nc.dram_tensor("l0f_b2", [128, 8], F32, kind="ExternalInput").ap()
    KB = [128] * 21 + [42]
    K0 = [128 * i for i in range(22)]
    TH = 1024

    lngb_sb = C.sbuf("l0f_lngb", [128, 16], F32)
    P.dma_sp(lngb_sb[:], lngb[:], writes=(lngb_sb,), tag="lngb")
    b1_sb = C.sbuf("l0f_b1s", [128, 44], F32)
    P.dma_sp(b1_sb[:], b1p[:], writes=(b1_sb,), tag="b1")
    b2_sb = C.sbuf("l0f_b2s", [128, 8], F32)
    P.dma_sp(b2_sb[:], b2p[:], writes=(b2_sb,), tag="b2")
    w2_sb = C.sbuf("l0f_w2s", [128, 22 * 1024], BF16)
    P.dma_pool(w2_sb[0:128, :].rearrange("p (c n) -> p c n", n=1024)[:, 0:21, :],
               w2[0:2688, :].rearrange("(c p) n -> p c n", p=128),
               writes=(w2_sb,), tag="w2a")
    P.dma_pool(w2_sb[0:42, 21 * 1024:22 * 1024], w2[2688:2730, :],
               writes=(w2_sb,), tag="w2b")

    xn = [C.sbuf(f"l0f_xn{i}", [128, TH], BF16) for i in range(nct)]
    ff = C.sbuf("l0f_ff", [128, 22 * TH], BF16)
    w1a = [C.sbuf(f"l0f_w1a{i}", [128, 8 * 128], BF16) for i in range(2)]
    w1g = [C.sbuf(f"l0f_w1g{i}", [128, 8 * 128], BF16) for i in range(2)]
    rc = C.tiles("l0f_frc", nct, 512, F32)
    sc = [C.sbuf(f"l0f_fsc{i}", [128, 512], F32) for i in range(3)]
    scb = C.sbuf("l0f_fscb", [128, 512], BF16)
    rowbuf = C.sbuf("l0f_frows", [1, 3 * 512], F32)
    ps = [C.psum(f"l0f_fps{i}", [128, 512]) for i in range(4)]
    ps_s = C.psum("l0f_fpss", [1, 1024])
    lnpool = {"sx": (ps_s, 0), "sq": (ps_s, 512), "nmb": (ps[2], 0), "rsb": (ps[3], 0)}

    for th in range(2):
        t0 = th * TH
        for cc in range(2):
            c0 = t0 + cc * 512
            for i in range(nct):
                if i < 4:
                    P.dma_sp(rc[i][:], sin[i * 128:(i + 1) * 128, c0:c0 + 512],
                             writes=(rc[i],), tag=f"fsl{th}{cc}_{i}")
                elif c0 > 0:
                    P.dma_sp(rc[i][:], sin[i * 128:(i + 1) * 128, c0 - 1:c0 + 511],
                             writes=(rc[i],), tag=f"fsl{th}{cc}_{i}")
                else:
                    P.i_memset(rc[i][:, 0:1], 0.0, writes=(rc[i],), tag=f"fsz{i}")
                    P.dma_sp(rc[i][:, 1:512], sin[i * 128:(i + 1) * 128, 0:511],
                             writes=(rc[i],), tag=f"fsl{th}{cc}_{i}")
            nmb, rsb, nmb_res, rsb_res = ln_stats(C, rc, 0, 512, rowbuf, lnpool,
                                                  tag=f"fln{th}{cc}", sq_scratch=sc)
            for i in range(nct):
                s0 = sc[i % 2]
                P.i_tt(ALU.add, s0[:, 0:512], rc[i][:], nmb,
                       reads=(rc[i], nmb_res), writes=(s0,), tag=f"fla{th}{cc}_{i}")
                P.i_tt(ALU.mult, s0[:, 0:512], s0[:, 0:512], rsb,
                       reads=(s0, rsb_res), writes=(s0,), tag=f"flb{th}{cc}_{i}")
                P.i_ts(xn[i][:, cc * 512:(cc + 1) * 512], s0[:, 0:512],
                       lngb_sb[:, i:i + 1], lngb_sb[:, 8 + i:9 + i],
                       ALU.mult, ALU.add, reads=(s0, lngb_sb), writes=(xn[i],),
                       tag=f"flc{th}{cc}_{i}")
        for k in range(22):
            kb, k0 = KB[k], K0[k]
            wa, wg = w1a[k % 2], w1g[k % 2]
            P.dma_pool(wa[:].rearrange("p (c n) -> p c n", n=128)[:, :, 0:kb],
                       w1[:, k0:k0 + kb].rearrange("(c p) n -> p c n", p=128),
                       writes=(wa,), tag=f"w1a{th}_{k}")
            P.dma_pool(wg[:].rearrange("p (c n) -> p c n", n=128)[:, :, 0:kb],
                       w1[:, 2730 + k0:2730 + k0 + kb].rearrange("(c p) n -> p c n", p=128),
                       writes=(wg,), tag=f"w1g{th}_{k}")
            for cc in range(2):
                pa, pg = ps[0], ps[1]
                pairs_a = [(wa[:, ct * 128: ct * 128 + kb],
                            xn[ct][:, cc * 512:(cc + 1) * 512]) for ct in range(nct)]
                mm_group(P, pa[0:kb, 0:512], pa, pairs_a, reads=tuple(xn) + (wa,),
                         tag=f"fam{th}{cc}_{k}")
                pairs_g = [(wg[:, ct * 128: ct * 128 + kb],
                            xn[ct][:, cc * 512:(cc + 1) * 512]) for ct in range(nct)]
                mm_group(P, pg[0:kb, 0:512], pg, pairs_g, reads=tuple(xn) + (wg,),
                         tag=f"fgm{th}{cc}_{k}")
                P.i_act(AF.Gelu, scb[0:kb, 0:512], pg[0:kb, 0:512],
                        bias=b1_sb[0:kb, 22 + k:23 + k],
                        reads=(pg, b1_sb), writes=(scb,), tag=f"fge{th}{cc}_{k}")
                P.i_ts(sc[2][0:kb, 0:512], pa[0:kb, 0:512], b1_sb[0:kb, k:k + 1],
                       None, ALU.add, reads=(pa, b1_sb), writes=(sc[2],),
                       tag=f"fab{th}{cc}_{k}")
                P.i_tt(ALU.mult, ff[0:kb, k * TH + cc * 512: k * TH + cc * 512 + 512],
                       sc[2][0:kb, 0:512], scb[0:kb, 0:512],
                       reads=(sc[2], scb), writes=((id(ff), k),), tag=f"ffm{th}{cc}_{k}")
        for cc in range(2):
            c0 = t0 + cc * 512
            for ot in range(nct):
                p1 = ps[ot % 2]
                pairs = [(w2_sb[0:KB[k], k * 1024 + ot * 128: k * 1024 + ot * 128 + 128],
                          ff[0:KB[k], k * TH + cc * 512: k * TH + cc * 512 + 512])
                         for k in range(22)]
                mm_group(P, p1[:, 0:512], p1, pairs,
                         reads=tuple((id(ff), k) for k in range(22)) + (w2_sb,),
                         tag=f"w2m{th}{cc}_{ot}")
                P.dma_sp(sc[2][:, 0:512], sin[ot * 128:(ot + 1) * 128, c0:c0 + 512],
                         writes=(sc[2],), tag=f"frl{th}{cc}_{ot}")
                P.i_ts(sc[0][:, 0:512], p1[:, 0:512], b2_sb[:, ot:ot + 1], None,
                       ALU.add, reads=(p1, b2_sb), writes=(sc[0],),
                       tag=f"fb2{th}{cc}_{ot}")
                P.i_tt(ALU.add, sc[2][:, 0:512], sc[2][:, 0:512], sc[0][:, 0:512],
                       reads=(sc[0], sc[2]), writes=(sc[2],), tag=f"fra{th}{cc}_{ot}")
                P.dma_sp(sout[ot * 128:(ot + 1) * 128, c0:c0 + 512], sc[2][:, 0:512],
                         reads=(sc[2],), writes=(f"fso{ot}",), tag=f"fst{th}{cc}_{ot}")
    stores = [op for op in P.ops if op.tag.startswith("fst")]
    P.add("sp", lambda h: None, deps=tuple(stores), tag="final_wait")


def build_ln_block(C, residual=False):
    P, nc = C.P, C.nc
    nct = 8
    xi = nc.dram_tensor("ln_in", [D, T], F32, kind="ExternalInput").ap()
    xo = nc.dram_tensor("ln_out", [D, T], F32, kind="ExternalOutput").ap()
    if residual:
        xr = nc.dram_tensor("ln_res", [D, T], F32, kind="ExternalInput").ap()
    rc = C.tiles("lnb_rc", nct, 512, F32)
    rr = C.tiles("lnb_rr", nct, 512, F32)
    sc = [C.sbuf(f"lnb_sc{i}", [128, 512], F32) for i in range(2)]
    rowbuf = C.sbuf("lnb_rows", [1, 3 * 512], F32)
    ps = [C.psum(f"lnb_ps{i}", [128, 512]) for i in range(2)]
    ps_s = C.psum("lnb_pss", [1, 1024])
    lnpool = {"sx": (ps_s, 0), "sq": (ps_s, 512), "nmb": (ps[0], 0), "rsb": (ps[1], 0)}
    for ch in range(4):
        c0 = ch * 512
        for i in range(nct):
            P.dma_sp(rc[i][:], xi[i * 128:(i + 1) * 128, c0:c0 + 512],
                     writes=(rc[i],), tag=f"ll{ch}_{i}")
            if residual:
                P.dma_sp(rr[i][:], xr[i * 128:(i + 1) * 128, c0:c0 + 512],
                         writes=(rr[i],), tag=f"lr{ch}_{i}")
        nmb, rsb, nmb_res, rsb_res = ln_stats(C, rc, 0, 512, rowbuf, lnpool,
                                              tag=f"lnb{ch}", sq_scratch=sc)
        for i in range(nct):
            s0 = sc[i % 2]
            P.i_tt(ALU.add, s0[:, 0:512], rc[i][:], nmb,
                   reads=(rc[i], nmb_res), writes=(s0,), tag=f"la{ch}_{i}")
            P.i_tt(ALU.mult, s0[:, 0:512], s0[:, 0:512], rsb,
                   reads=(s0, rsb_res), writes=(s0,), tag=f"lb{ch}_{i}")
            if residual:
                P.i_tt(ALU.add, s0[:, 0:512], s0[:, 0:512], rr[i][:],
                       reads=(s0, rr[i]), writes=(s0,), tag=f"lc{ch}_{i}")
            P.dma_sp(xo[i * 128:(i + 1) * 128, c0:c0 + 512], s0[:, 0:512],
                     reads=(s0,), writes=(f"lo{i}",), tag=f"lst{ch}_{i}")
    stores = [op for op in P.ops if op.tag.startswith("lst")]
    P.add("sp", lambda h: None, deps=tuple(stores), tag="final_wait")


def l0f_host_inputs(inputs, layer):
    d = {}
    d["l0f_wqkv"] = np.ascontiguousarray(np.asarray(inputs["L0_ffn_Wqkv"][layer], np.float32))
    d["l0f_wo"] = np.ascontiguousarray(np.asarray(inputs["L0_ffn_Wo"][layer], np.float32))
    qs = np.tile(np.asarray(inputs["L0_ffn_q_scale"][layer], np.float32), 16)
    ks = np.tile(np.asarray(inputs["L0_ffn_k_scale"][layer], np.float32), 16)
    d["l0f_qsks"] = np.ascontiguousarray(
        np.concatenate([colpack(qs), colpack(ks)], axis=1))
    d["l0f_w1"] = np.ascontiguousarray(np.asarray(inputs["L0_ffn_W1"][layer], np.float32))
    d["l0f_w2"] = np.ascontiguousarray(np.asarray(inputs["L0_ffn_W2"][layer], np.float32))
    d["l0f_lngb"] = np.ascontiguousarray(
        np.concatenate([colpack(np.asarray(inputs["L0_ffn_ln_g"][layer], np.float32)),
                        colpack(np.asarray(inputs["L0_ffn_ln_b"][layer], np.float32))],
                       axis=1))
    b1 = np.asarray(inputs["L0_ffn_b1"][layer], np.float32)
    b1a = np.zeros((128, 22), np.float32)
    b1g = np.zeros((128, 22), np.float32)
    for k in range(22):
        kb = 128 if k < 21 else 42
        b1a[0:kb, k] = b1[128 * k:128 * k + kb]
        b1g[0:kb, k] = b1[2730 + 128 * k:2730 + 128 * k + kb]
    d["l0f_b1"] = np.ascontiguousarray(np.concatenate([b1a, b1g], axis=1))
    d["l0f_b2"] = colpack(np.asarray(inputs["L0_ffn_b2"][layer], np.float32))
    return d


def prog_l0f_attn(strict=True):
    nc = bass.Bass("TRN2", target_bir_lowering=False, debug=False)
    with ExitStack() as es:
        C = Ctx(nc, es)
        build_l0f_attn(C)
        C.P.emit(strict=strict)
    return nc


def prog_l0f_ffn(strict=True):
    nc = bass.Bass("TRN2", target_bir_lowering=False, debug=False)
    with ExitStack() as es:
        C = Ctx(nc, es)
        build_l0f_ffn(C)
        C.P.emit(strict=strict)
    return nc


def prog_ln_block(residual=False, strict=True):
    nc = bass.Bass("TRN2", target_bir_lowering=False, debug=False)
    with ExitStack() as es:
        C = Ctx(nc, es)
        build_ln_block(C, residual=residual)
        C.P.emit(strict=strict)
    return nc


def load_residual(C, dram_name="xres_in"):
    nc, P = C.nc, C.P
    xdr = nc.dram_tensor(dram_name, [D, T], F32, kind="ExternalInput").ap()
    xres = C.tiles("xres", 8, T, F32, guard=G)
    for i, t in enumerate(xres):
        P.dve(lambda h, t=t: h.memset(t[:, 0:G], 0.0), writes=(t,), tag=f"xg{i}")
        P.dma_sp(t[:, G:G + T], xdr[i * 128:(i + 1) * 128, :], writes=(t,),
                 tag=f"xload{i}")
    return xres


def store_residual(C, xres, dram_name="xres_out"):
    nc, P = C.nc, C.P
    ydr = nc.dram_tensor(dram_name, [D, T], F32, kind="ExternalOutput").ap()
    stores = []
    for i, t in enumerate(xres):
        stores.append(P.dma_sp(ydr[i * 128:(i + 1) * 128, :], t[:, G:G + T],
                               reads=(t,), tag=f"xstore{i}"))
    C.P.add("sp", lambda h: None, deps=tuple(stores), tag="final_wait")




def colpack(v, nt=8):
    v = np.ascontiguousarray(v, dtype=np.float32)
    return np.ascontiguousarray(v.reshape(nt, 128).T)


def hT(w):
    return np.ascontiguousarray(np.asarray(w, dtype=np.float32).T)




def prog_l1_ffn():
    nc = bass.Bass("TRN2", target_bir_lowering=False, debug=False)
    with ExitStack() as es:
        C = Ctx(nc, es)
        xres = load_residual(C)
        w_sb, mu_sb = load_l1_ffn_weights(C)
        build_l1_ffn(C, xres, w_sb, mu_sb)
        store_residual(C, xres)
        C.P.emit()
    return nc


def l1_ffn_inmaps(inputs, xres_by_core):
    wr = hT(inputs["L1_ffn_W_r"])
    wk = hT(inputs["L1_ffn_W_k"])
    wv = hT(inputs["L1_ffn_W_v"])
    mu = np.concatenate([colpack(inputs["L1_ffn_mu_r"]), colpack(inputs["L1_ffn_mu_k"])], axis=1)
    return [{
        "xres_in": xres_by_core[b],
        "l1f_wrT": wr, "l1f_wkT": wk, "l1f_wvT": wv, "l1f_mu": mu,
    } for b in range(NCORES)]


_PROG_CACHE = {}


def _get_prog(name, builder):
    if name not in _PROG_CACHE:
        _PROG_CACHE[name] = builder()
    return _PROG_CACHE[name]


def kernel(**inputs):
    x0 = np.asarray(inputs["L0_attn_x"], np.float32)
    xres = [np.ascontiguousarray(x0[b].T) for b in range(NCORES)]
    wd_a = l0_attn_host_inputs(inputs)
    consts = {"l0f_" + k: v for k, v in l0f_consts().items()}
    wd_f = [l0f_host_inputs(inputs, i) for i in range(6)]
    wd_l1a = l1_attn_host_inputs(inputs)
    wr = hT(inputs["L1_ffn_W_r"])
    wk = hT(inputs["L1_ffn_W_k"])
    wv = hT(inputs["L1_ffn_W_v"])
    mu = np.concatenate([colpack(inputs["L1_ffn_mu_r"]),
                         colpack(inputs["L1_ffn_mu_k"])], axis=1)
    cores = list(range(NCORES))

    def launch(prog, per_core):
        res = run_bass_kernel_spmd(prog, per_core, core_ids=cores)
        return res.results

    p1_keys = ("l0a_inprojT", "l0a_nwb", "l0a_convw", "l0a_xprojTf", "l0a_xprojTr",
               "l0a_dtprojTf", "l0a_dtprojTr", "l0a_dtb")
    p2_keys = ("l0a_alog", "l0a_d", "l0a_outprojT")
    p = _get_prog("l0a1", prog_l0_attn_p1)
    wa1 = {k: wd_a[k] for k in p1_keys}
    r1 = launch(p, [dict(wa1, xres_in=xres[b]) for b in cores])
    p = _get_prog("l0a2", prog_l0_attn_p2)
    spill_keys = ("l0a_z", "l0a_xc_f", "l0a_xc_r", "l0a_dt_f", "l0a_dt_r",
                  "l0a_bc_f", "l0a_bc_r")
    wa2 = {k: wd_a[k] for k in p2_keys}
    r2 = launch(p, [dict(wa2, xres_in=xres[b],
                         **{k: r1[b][k] for k in spill_keys}) for b in cores])
    x1 = [np.asarray(r2[b]["xres_out"]) for b in cores]

    p = _get_prog("ln", lambda: prog_ln_block(residual=False))
    r = launch(p, [{"ln_in": x1[b]} for b in cores])
    s = [np.asarray(r[b]["ln_out"]) for b in cores]
    pa = _get_prog("l0fa", prog_l0f_attn)
    pf = _get_prog("l0ff", prog_l0f_ffn)
    attn_keys = ("l0f_wqkv", "l0f_wo", "l0f_qsks")
    ffn_keys = ("l0f_w1", "l0f_w2", "l0f_lngb", "l0f_b1", "l0f_b2")
    for i in range(6):
        wa = {k: wd_f[i][k] for k in attn_keys}
        r = launch(pa, [dict(wa, **consts, sres_in=s[b]) for b in cores])
        s = [np.asarray(r[b]["sres_out"]) for b in cores]
        wf = {k: wd_f[i][k] for k in ffn_keys}
        r = launch(pf, [dict(wf, sres_in=s[b]) for b in cores])
        s = [np.asarray(r[b]["sres_out"]) for b in cores]
    p = _get_prog("glue", lambda: prog_ln_block(residual=True))
    r = launch(p, [{"ln_in": s[b], "ln_res": x1[b]} for b in cores])
    x2 = [np.asarray(r[b]["ln_out"]) for b in cores]

    p = _get_prog("l1a", prog_l1_attn)
    r = launch(p, [dict(wd_l1a, xres_in=x2[b]) for b in cores])
    x3 = [np.asarray(r[b]["xres_out"]) for b in cores]

    p = _get_prog("l1f", prog_l1_ffn)
    r = launch(p, [{"xres_in": x3[b], "l1f_wrT": wr, "l1f_wkT": wk,
                    "l1f_wvT": wv, "l1f_mu": mu} for b in cores])
    x4 = [np.asarray(r[b]["xres_out"]) for b in cores]
    return np.ascontiguousarray(np.stack([x4[b].T for b in cores]))
